# version 08
# baseline (speedup 1.0000x reference)
import math
from contextlib import ExitStack

import numpy as np
import concourse.bass as bass
import concourse.mybir as mybir
from concourse.bass_utils import run_bass_kernel_spmd

F32 = mybir.dt.float32
F32R = mybir.dt.float32r
AF = mybir.ActivationFunctionType
ALU = mybir.AluOpType

LOG_EPS = float(np.log(1e-6))

FULL_CFG = dict(S=4096, DM=1024, H=16, DH=64, INTER=512, E=8,
                LN_EPS=1e-5, ACT_FRAC=0.375)

CONST_SHAPES = {
    "wbase": ([128, 192], F32R),
    "id128": ([128, 128], F32),
    "id1": ([1, 1], F32),
    "ones_c128": ([128, 1], F32),
    "halfones": ([128, 1], F32),
    "otherhalf": ([128, 1], F32),
    "logeps": ([1, 1], F32),
    "lneps": ([1, 1], F32),
}


def host_consts(ln_eps):
    wb = np.zeros((128, 192), dtype=np.float32)
    wb[0:64, 64] = 1.0
    wb[64:128, 128] = 1.0
    ho = np.zeros((128, 1), dtype=np.float32); ho[0:64] = 1.0
    oh = np.zeros((128, 1), dtype=np.float32); oh[64:128] = 1.0
    return {
        "wbase": wb,
        "id128": np.eye(128, dtype=np.float32),
        "id1": np.ones((1, 1), dtype=np.float32),
        "ones_c128": np.ones((128, 1), dtype=np.float32),
        "halfones": ho,
        "otherhalf": oh,
        "logeps": np.full((1, 1), LOG_EPS, dtype=np.float32),
        "lneps": np.full((1, 1), ln_eps, dtype=np.float32),
    }


def pick_act_blocks(nblk, frac):
    n = int(round(frac * nblk))
    n = min(n, (nblk + 1) // 2)
    out = set()
    if n > 0:
        step = nblk / n
        prev = -2
        for i in range(n):
            b = min(nblk - 1, int(i * step))
            if b <= prev + 1:
                b = prev + 2
            if b >= nblk:
                break
            out.add(b)
            prev = b
    return out


class Builder:
    ENGINES = ("sync", "gpsimd", "tensor", "vector", "scalar")

    def __init__(self, cfg):
        c = dict(cfg)
        c["NQ"] = c["S"] // 2
        c["NBLK"] = c["S"] // 128
        c["NDC"] = c["DM"] // 128
        c["TCW"] = min(512, c["S"])
        c["NTC"] = c["S"] // c["TCW"]
        c["QCW"] = min(512, c["NQ"])
        c["NQC"] = max(1, c["NQ"] // c["QCW"])
        c["ICW"] = min(512, c["NQ"])
        c["NIC"] = max(1, c["NQ"] // c["ICW"])
        c["NICH"] = c["INTER"] // 128
        c["HP"] = c["H"] // 2
        c["ACT_B"] = pick_act_blocks(c["NBLK"], c["ACT_FRAC"])
        self.c = c
        self.ops = {e: [] for e in self.ENGINES}
        self.ct = {}
        self.sem = {}
        self.pe_carry = []
        self.slotuse = {}

    def emit(self, eng, fn):
        self.ops[eng].append(fn)

    def wait(self, eng, s, v):
        if v > 0:
            self.emit(eng, lambda q, s=s, v=v: q.wait_ge(self.sem[s], v))

    def bump(self, s, n=1):
        self.ct[s] = self.ct.get(s, 0) + n
        return self.ct[s]

    def cnt(self, s):
        return self.ct.get(s, 0)

    def op(self, eng, fn, inc=None, carry_ok=False):
        if inc is None and carry_ok and self.pe_carry:
            inc = self.pe_carry.pop(0)
        if inc is not None:
            s, n = inc
            self.emit(eng, lambda q, fn=fn, s=s, n=n: fn(q).then_inc(self.sem[s], n))
        else:
            self.emit(eng, lambda q, fn=fn: fn(q))

    def inc(self, s, n=1):
        self.bump(s, n)
        return (s, n)

    def carry(self, s, n=1):
        self.bump(s, n)
        self.pe_carry.append((s, n))

    def release(self, eng, s, n=1):
        self.bump(s, n)
        self.emit(eng, lambda q, s=s, n=n: q.sem_inc(self.sem[s], n))

    def dma(self, eng, dst, src, s):
        v = self.bump(s, 16)
        self.emit(eng, lambda q, dst=dst, src=src, s=s: q.dma_start(out=dst, in_=src)
                  .then_inc(self.sem[s], 16))
        return v

    def mm(self, out, lhsT, rhs, start, stop, inc=None):
        self.op("tensor", lambda t, o=out, l=lhsT, r=rhs, st=start, sp=stop:
                t.matmul(out=o, lhsT=l, rhs=r, start=st, stop=sp), inc=inc,
                carry_ok=(inc is None))

    def transp(self, out, in_, ident, inc=None):
        self.op("tensor", lambda t, o=out, i=in_, d=ident:
                t.transpose(out=o, in_=i, identity=d), inc=inc, carry_ok=(inc is None))

    def act(self, out, in_, func, bias=0.0, scale=1.0, inc=None):
        self.op("scalar", lambda a, o=out, i=in_, f=func, b=bias, s=scale:
                a.activation(out=o, in_=i, func=f, bias=b, scale=s), inc=inc)

    def ts(self, out, in0, s1, op0, s2=None, op1=None, inc=None):
        if s2 is None:
            self.op("vector", lambda v, o=out, i=in0, a=s1, p=op0:
                    v.tensor_single_scalar(out=o, in_=i, scalar=a, op=p), inc=inc)
        else:
            self.op("vector", lambda v, o=out, i=in0, a=s1, b=s2, p=op0, p1=op1:
                    v.tensor_scalar(out=o, in0=i, scalar1=a, scalar2=b, op0=p, op1=p1),
                    inc=inc)

    def tt(self, out, in0, in1, alu, inc=None):
        self.op("vector", lambda v, o=out, a=in0, b=in1, p=alu:
                v.tensor_tensor(out=o, in0=a, in1=b, op=p), inc=inc)

    def stt(self, out, in0, scalar, in1, op0, op1, inc=None):
        self.op("vector", lambda v, o=out, a=in0, s=scalar, b=in1, p0=op0, p1=op1:
                v.scalar_tensor_tensor(out=o, in0=a, scalar=s, in1=b, op0=p0, op1=p1),
                inc=inc)

    def cp(self, out, in_, inc=None):
        self.op("vector", lambda v, o=out, i=in_: v.tensor_copy(out=o, in_=i), inc=inc)

    def mset(self, ap_, val, inc=None):
        self.op("vector", lambda v, a=ap_, x=val: v.memset(a, x), inc=inc)

    def red(self, out, in_, alu, inc=None):
        self.op("vector", lambda v, o=out, i=in_, p=alu:
                v.tensor_reduce(out=o, in_=i, axis=mybir.AxisListType.X, op=p), inc=inc)

    def recip(self, out, in_, inc=None):
        self.op("vector", lambda v, o=out, i=in_: v.reciprocal(out=o, in_=i), inc=inc)

    def vo(self, kind, *a, **kw):
        kw["inc"] = self.inc("vch")
        getattr(self, kind)(*a, **kw)
        self.wait("vector", "vch", self.cnt("vch"))

    def ao(self, *a, **kw):
        kw["inc"] = self.inc("ach")
        self.act(*a, **kw)
        self.wait("scalar", "ach", self.cnt("ach"))

    def build(self):
        c = self.c
        S, DM, H, DH, INTER, E = c["S"], c["DM"], c["H"], c["DH"], c["INTER"], c["E"]
        NQ, NBLK, NDC, TCW, NTC = c["NQ"], c["NBLK"], c["NDC"], c["TCW"], c["NTC"]
        QCW, NQC, ICW, NIC, NICH, HP = (c["QCW"], c["NQC"], c["ICW"], c["NIC"],
                                        c["NICH"], c["HP"])

        nc = bass.Bass("TRN2", target_bir_lowering=False, debug=False)
        self.nc = nc
        dram = {}

        def din(name, shape, dt_=F32):
            dram[name] = nc.dram_tensor(name, shape, dt_, kind="ExternalInput").ap()
            return dram[name]

        xbT = din("xbT", [DM, S])
        wqkvT = din("wqkvT", [DM, 3 * DM])
        bqkv = din("bqkv", [3 * DM, 1])
        wprojT = din("wprojT", [DM, DM])
        bprojD = din("bproj", [DM, 1])
        dwcwD = din("dwcw", [DM, 3])
        dwcbD = din("dwcb", [DM, 1])
        gatewT = din("gatewT", [DM, E])
        wgTD = din("wgT", [E, DM, INTER])
        wuTD = din("wuT", [E, DM, INTER])
        wdTD = din("wdT", [E, INTER, DM])
        sgTD = din("sgT", [DM, INTER])
        suTD = din("suT", [DM, INTER])
        sdTD = din("sdT", [INTER, DM])
        masksD = din("masks", [1, 2])
        for name, (shape, dt_) in CONST_SHAPES.items():
            din(name, shape, dt_)

        lnx1 = nc.dram_tensor("lnx1", [DM, S], F32, kind="Internal").ap()
        ln3d = nc.dram_tensor("ln3d", [DM, NQ], F32, kind="Internal").ap()
        x2d = nc.dram_tensor("x2d", [DM, NQ], F32, kind="Internal").ap()
        rowscr = nc.dram_tensor("rowscr", [8, max(S, 2048)], F32, kind="Internal").ap()
        cwd = nc.dram_tensor("cwd", [E, NQ], F32, kind="Internal").ap()
        outD = nc.dram_tensor("out", [DM, NQ], F32, kind="ExternalOutput").ap()

        es = ExitStack()
        with es:
            sbuf = lambda name, shape, dt_=F32: es.enter_context(
                nc.sbuf_tensor(name, shape, dt_))
            csb = {name: sbuf("c_" + name, shape, dt_)
                   for name, (shape, dt_) in CONST_SHAPES.items()}

            QT2 = sbuf("QT2", [128, NQ])
            nKT2 = sbuf("nKT2", [128, max(S // 2, 2 * NICH * 128)])
            VT = sbuf("VT", [128, max(S, NICH * NDC * 128, (c["DM"] // 128) * min(512, c["NQ"]))])
            vp = sbuf("vp", [128, max(NBLK * 65, NQ)])
            smallb = sbuf("smallb", [128, max(NBLK, 6 * c["E"] + 4) + 12])
            SMB = max(NBLK, 6 * c["E"] + 4)
            nskcol = smallb[:, 0:NBLK]
            mlrb = smallb[:, SMB + 8:SMB + 10]
            RN = 6
            ring = [sbuf(f"ring{i}", [128, NQ], F32R) for i in range(RN)]
            ringf = [r[:].bitcast(F32) for r in ring]
            kern = sbuf("kern", [128, max(NQ, DM)])
            AR = sbuf("AR", [128, NDC * NQ])
            wbuf = sbuf("wbuf", [128, 2 * DM])
            RR = 3
            rr = [sbuf(f"rr{i}", [128, TCW]) for i in range(RR)]
            bq2 = smallb[:, SMB:SMB + 1]
            bk2 = smallb[:, SMB + 1:SMB + 2]
            bv2 = smallb[:, SMB + 2:SMB + 3]
            bpj = smallb[:, SMB + 3:SMB + 4]
            dwc_s = smallb[:, SMB + 4:SMB + 7]
            dwcb_s = smallb[:, SMB + 7:SMB + 8]
            rowT_t = sbuf("rowT", [128, max(S, 4096)])
            nskrow = rowT_t[0:1, 0:S]
            rows = rowT_t[32:33, 0:4096]
            epsp = rowT_t[64:65, 0:NQ]
            recrow = rowT_t[64:65, 2048:2048 + NQ]
            ov1 = sbuf("ov1", [128, max(NQ, 2 * TCW)])
            RBt = ov1[:, 0:TCW]
            MBt = ov1[:, TCW:2 * TCW]
            OV2W = max(NQ, 4 * TCW)
            ov2 = sbuf("ov2", [128, OV2W + 2])
            sqb = ov2[:, 0:2 * TCW]
            gs_sb = ov2[:, 2 * TCW:4 * TCW]
            NG = NQ // 128
            cwtm = kern[:][:, NDC * E:NDC * E + NG * E]
            scx = smallb[:, :]

            PS_A = es.enter_context(nc.psum_tensor("PS_A", [128, 2048], F32))
            PS_B = es.enter_context(nc.psum_tensor("PS_B", [128, 2048], F32))

            semnames = ["cst", "x1rhs", "x2rhs", "sqd", "sqf",
                        "x1mm0", "x1mm1", "x1mm2", "x1mm3",
                        "qmm0", "qmm1", "qmm2", "qmm3",
                        "fmm0", "fmm1", "fmm2", "fmm3",
                        "lnxh0", "lnxh1",
                        "st0", "bc0", "lnxw", "lnxc",
                        "qw", "qrhs", "qgd", "qev",
                        "prep_pe", "prep_dve", "prep_act",
                        "tra", "tfa", "trv", "tfv",
                        "kdn", "evd", "pvd", "pad", "pbf", "p3d", "p3b",
                        "vch", "ach", "bgo", "x2w",
                        "bdw", "bmm", "bev", "bst", "bbc", "cwt", "cwr",
                        "fw", "frhs", "fgd", "fhp", "fpd", "fyw", "l4",
                        "outw"]
            for s in semnames:
                self.sem[s] = es.enter_context(nc.semaphore(s))

            ctx = dict(csb=csb, mlrb=mlrb, QT2=QT2, nKT2=nKT2, VT=VT, vp=vp,
                       nskcol=nskcol, nskrow=nskrow, epsp=epsp, recrow=recrow, ring=ring, NG=NG,
                       ringf=ringf, kern=kern, AR=AR, wbuf=wbuf, rr=rr, RR=RR,
                       bq2=bq2, bk2=bk2, bv2=bv2, bpj=bpj, dwc_s=dwc_s,
                       dwcb_s=dwcb_s, ov1=ov1, ov2=ov2, OV2W=OV2W,
                       rows=rows, RBt=RBt, MBt=MBt, sqb=sqb,
                       cwtm=cwtm, gs_sb=gs_sb, scx=scx, gw_sb=kern[:][:, 0:NDC * E],
                       PS_A=PS_A, PS_B=PS_B,
                       xbT=xbT, wqkvT=wqkvT, bqkv=bqkv, wprojT=wprojT,
                       bprojD=bprojD, dwcwD=dwcwD, dwcbD=dwcbD, gatewT=gatewT,
                       wgTD=wgTD, wuTD=wuTD, wdTD=wdTD, sgTD=sgTD, suTD=suTD,
                       sdTD=sdTD, masksD=masksD, lnx1=lnx1, ln3d=ln3d, x2d=x2d,
                       rowscr=rowscr, outD=outD, cwd=cwd)
            self.x = type("X", (), ctx)
            self.dram = dram

            phases = self.c.get("PHASES", "0ABC")
            self._consts()
            if "0" in phases:
                self._phase0()
            if "A" in phases:
                self._phase_a()
            if "B" in phases:
                self._phase_b()
            if "C" in phases:
                self._phase_c()
            if "C" not in phases:
                for s_ in ("bgo", "vch", "ach"):
                    self.wait("gpsimd", s_, self.cnt(s_))
                self.dma("gpsimd", outD[0:128, 0:min(512, NQ)],
                         self.x.gs_sb[:, 0:min(512, NQ)], "outw")
                self.wait("gpsimd", "outw", self.cnt("outw"))

            block = es.enter_context(nc.Block())

            @block.sync
            def _(q):
                for fn in self.ops["sync"]:
                    fn(q)

            @block.gpsimd
            def _(q):
                for fn in self.ops["gpsimd"]:
                    fn(q)

            @block.tensor
            def _(q):
                for fn in self.ops["tensor"]:
                    fn(q)

            @block.vector
            def _(q):
                for fn in self.ops["vector"]:
                    fn(q)

            @block.scalar
            def _(q):
                for fn in self.ops["scalar"]:
                    fn(q)

        return nc

    def _consts(self):
        x = self.x
        for name in CONST_SHAPES:
            self.dma("gpsimd", x.csb[name][:], self.dram[name], "cst")
        self.dma("gpsimd", x.rowscr[7:8, 0:2], x.masksD, "cst")
        self.wait("gpsimd", "cst", self.cnt("cst"))
        self.dma("gpsimd", x.mlrb, x.rowscr[7:8, 0:2].broadcast_to((128, 2)), "cst")
        CST = self.cnt("cst")
        for e in ("tensor", "vector", "scalar", "sync"):
            self.wait(e, "cst", CST)

    def _phase0(self):
        x, c = self.x, self.c
        DM, NDC, TCW, NTC, RR = c["DM"], c["NDC"], c["TCW"], c["NTC"], 3
        rel = []
        for tc in range(NTC):
            for ci in range(NDC):
                k = tc * 2 * NDC + ci
                if k >= RR:
                    self.wait("sync", *rel[k - RR])
                slot = "x1mm%d" % (k % RR)
                self.dma("sync", x.rr[k % RR][:],
                         x.xbT[128 * ci:128 * ci + 128, TCW * tc:TCW * tc + TCW], slot)
                self.wait("scalar", slot, self.cnt(slot))
                kk = tc * NDC + ci
                if kk >= 2:
                    self.wait("scalar", "x1rhs", kk - 1)
                rel.append(None)
                self.act(x.sqb[:, (ci % 2) * TCW:(ci % 2) * TCW + TCW], x.rr[k % RR][:],
                         AF.Square, inc=self.inc("sqd"))
                self.wait("tensor", slot, self.cnt(slot))
                if ci == 0:
                    self.wait("tensor", "st0", self.cnt("st0"))
                self.mm(x.PS_A[0:1, 0:TCW], x.csb["ones_c128"][:], x.rr[k % RR][:],
                        start=(ci == 0), stop=(ci == NDC - 1), inc=None)
                self.wait("tensor", "sqd", self.cnt("sqd"))
                self.mm(x.PS_A[0:1, 512:512 + TCW], x.csb["ones_c128"][:],
                        x.sqb[:, (ci % 2) * TCW:(ci % 2) * TCW + TCW],
                        start=(ci == 0), stop=(ci == NDC - 1), inc=self.inc("x1rhs"))
                rel[k] = ("x1rhs", self.cnt("x1rhs"))
            self._ln_rows(x.PS_A, x.rows, TCW, srhs=("x1rhs", self.cnt("x1rhs")))
            self.wait("gpsimd", "st0", self.cnt("st0"))
            self.wait("gpsimd", "bc0", self.cnt("bc0"))
            self.dma("gpsimd", x.rowscr[0:1, 0:TCW], x.rows[0:1, 1024:1024 + TCW], "bc0")
            self.dma("gpsimd", x.rowscr[1:2, 0:TCW], x.rows[0:1, 1536:1536 + TCW], "bc0")
            self.wait("gpsimd", "bc0", self.cnt("bc0"))
            self.dma("gpsimd", x.RBt, x.rowscr[0:1, 0:TCW].broadcast_to((128, TCW)), "bc0")
            self.dma("gpsimd", x.MBt, x.rowscr[1:2, 0:TCW].broadcast_to((128, TCW)), "bc0")
            BC = self.cnt("bc0")
            for ci in range(NDC):
                k2 = tc * 2 * NDC + NDC + ci
                if k2 >= RR:
                    self.wait("sync", *rel[k2 - RR])
                slot2 = "x1mm%d" % (k2 % RR)
                self.dma("sync", x.rr[k2 % RR][:],
                         x.xbT[128 * ci:128 * ci + 128, TCW * tc:TCW * tc + TCW], slot2)
                self.wait("vector", slot2, self.cnt(slot2))
                self.wait("vector", "bc0", BC)
                kk2 = tc * NDC + ci
                half = ci % 2
                if kk2 >= 2:
                    self.wait("vector", "lnxh%d" % half, self.cnt("lnxh%d" % half))
                self.vo("tt", x.gs_sb[:, half * TCW:half * TCW + TCW],
                        x.rr[k2 % RR][:], x.RBt, ALU.mult)
                self.release("vector", "x2rhs")
                rel.append(("x2rhs", self.cnt("x2rhs")))
                self.vo("tt", x.gs_sb[:, half * TCW:half * TCW + TCW],
                        x.gs_sb[:, half * TCW:half * TCW + TCW], x.MBt, ALU.add)
                self.release("vector", "lnxw")
                self.wait("gpsimd", "lnxw", self.cnt("lnxw"))
                self.dma("gpsimd", x.lnx1[128 * ci:128 * ci + 128, TCW * tc:TCW * tc + TCW],
                         x.gs_sb[:, half * TCW:half * TCW + TCW], "lnxh%d" % half)
        self.wait("sync", "lnxh0", self.cnt("lnxh0"))
        self.wait("sync", "lnxh1", self.cnt("lnxh1"))

    def _ln_rows(self, PS, rows, W, srhs=None, ssq=None):
        DM = self.c["DM"]
        if srhs:
            self.wait("vector", srhs[0], srhs[1])
        if ssq:
            self.wait("vector", ssq[0], ssq[1])
        self.vo("ts", rows[0:1, 0:W], PS[0:1, 0:W], 1.0 / DM, ALU.mult)
        self.vo("ts", rows[0:1, 512:512 + W], PS[0:1, 512:512 + W], 1.0 / DM, ALU.mult)
        self.wait("scalar", "vch", self.cnt("vch"))
        self.ao(rows[0:1, 2048:2048 + W], rows[0:1, 0:W], AF.Square)
        self.wait("vector", "ach", self.cnt("ach"))
        self.vo("tt", rows[0:1, 2560:2560 + W], rows[0:1, 512:512 + W],
                rows[0:1, 2048:2048 + W], ALU.subtract)
        self.wait("scalar", "vch", self.cnt("vch"))
        self.ao(rows[0:1, 2048:2048 + W], rows[0:1, 2560:2560 + W], AF.Ln,
                bias=self.x.csb["lneps"][:])
        self.ao(rows[0:1, 1024:1024 + W], rows[0:1, 2048:2048 + W], AF.Exp, scale=-0.5)
        self.wait("vector", "ach", self.cnt("ach"))
        self.vo("tt", rows[0:1, 2560:2560 + W], rows[0:1, 0:W], rows[0:1, 1024:1024 + W],
                ALU.mult)
        self.ts(rows[0:1, 1536:1536 + W], rows[0:1, 2560:2560 + W], -1.0, ALU.mult,
                inc=self.inc("st0"))

    def _phase_a(self):
        x, c = self.x, self.c
        S, DM, H, DH = c["S"], c["DM"], c["H"], c["DH"]
        NQ, NBLK, NDC, TCW, NTC = c["NQ"], c["NBLK"], c["NDC"], c["TCW"], c["NTC"]
        ICW, NIC = c["ICW"], c["NIC"]
        ACT_B = c["ACT_B"]
        HALF = S // 2
        RR = 3
        TA = [x.ring[i] for i in range(3)]
        TV = [x.ring[3 + i] for i in range(3)]
        NQCH = max(1, NQ // TCW)
        QW_LAST = NQ - TCW * (NQCH - 1) if NQ >= TCW else NQ

        for h in range(self.c.get("HLIM", H)):
            hp, hq = h // 2, h % 2
            even = (hq == 0)

            self.wait("gpsimd", "qev", self.cnt("qev"))
            for ci in range(NDC):
                src = x.wqkvT[128 * ci:128 * ci + 128, DH * h:DH * h + DH]
                self.dma("gpsimd", x.wbuf[:, 128 * ci:128 * ci + 64], src, "qw")
                self.dma("gpsimd", x.wbuf[:, 128 * ci + 64:128 * ci + 128], src, "qw")
            for ci in range(NDC):
                src = x.wqkvT[128 * ci:128 * ci + 128, DM + DH * h:DM + DH * h + DH]
                self.dma("gpsimd", x.wbuf[:, DM + 128 * ci:DM + 128 * ci + 64], src, "qw")
                self.dma("gpsimd", x.wbuf[:, DM + 128 * ci + 64:DM + 128 * ci + 128], src, "qw")
            if even:
                self.wait("gpsimd", "pad", h)
                for ci in range(NDC):
                    src = x.wqkvT[128 * ci:128 * ci + 128,
                                  2 * DM + 128 * hp:2 * DM + 128 * hp + 128]
                    self.dma("gpsimd", x.kern[:, 128 * ci:128 * ci + 128],
                             src, "qw")
            self.dma("gpsimd", x.bq2[0:64, :], x.bqkv[DH * h:DH * h + DH, :], "qw")
            self.dma("gpsimd", x.bq2[64:128, :], x.bqkv[DH * h:DH * h + DH, :], "qw")
            self.dma("gpsimd", x.bk2[0:64, :], x.bqkv[DM + DH * h:DM + DH * h + DH, :], "qw")
            self.dma("gpsimd", x.bk2[64:128, :], x.bqkv[DM + DH * h:DM + DH * h + DH, :], "qw")
            if even:
                self.dma("gpsimd", x.bv2,
                         x.bqkv[2 * DM + 128 * hp:2 * DM + 128 * hp + 128, :], "qw")
            self.wait("gpsimd", "bgo", self.cnt("bgo"))
            self.dma("gpsimd", x.dwc_s[64 * hq:64 * hq + 64, :],
                     x.dwcwD[DH * h:DH * h + DH, :], "qw")
            self.dma("gpsimd", x.dwcb_s[64 * hq:64 * hq + 64, :],
                     x.dwcbD[DH * h:DH * h + DH, :], "qw")
            QW = self.cnt("qw")

            self.wait("tensor", "qw", QW)
            for tc in range(NTC):
                own = tc < NQCH
                qw_ = TCW if (tc < NQCH - 1 or NQ >= TCW * NQCH) else QW_LAST
                if NQ < TCW:
                    qw_ = NQ
                if tc > 0:
                    self.wait("tensor", "qev", self.cnt("qev"))
                for ci in range(NDC):
                    k = (h * NTC + tc) * NDC + ci
                    if k >= RR:
                        self.wait("sync", "qrhs", k - RR + 1)
                    slot = "qmm%d" % (k % RR)
                    self.dma("sync", x.rr[k % RR][:],
                             x.lnx1[128 * ci:128 * ci + 128, TCW * tc:TCW * tc + TCW], slot)
                    self.wait("tensor", slot, self.cnt(slot))
                    first, last = (ci == 0), (ci == NDC - 1)
                    roles = [("K", x.PS_A[:, 512:512 + TCW], x.wbuf[:, DM:2 * DM], TCW)]
                    if even:
                        roles.append(("V", x.PS_A[:, 1024:1024 + TCW],
                                      x.kern[:, 0:DM], TCW))
                    if own:
                        roles.append(("Q", x.PS_A[:, 0:qw_], x.wbuf[:, 0:DM], qw_))
                    for ri, (rn, pslice, wslab, w_) in enumerate(roles):
                        lastrole = ri == len(roles) - 1
                        inc = self.inc("qrhs") if lastrole else None
                        self.mm(pslice, wslab[:, 128 * ci:128 * ci + 128],
                                x.rr[k % RR][:, 0:w_], start=first, stop=last, inc=inc)
                self.wait("vector", "qrhs", self.cnt("qrhs"))
                nblk_tc = TCW // 128
                g0 = tc * nblk_tc
                ktop = x.nKT2[0:64, 64 * g0:64 * (g0 + nblk_tc)]
                ktop = ktop.rearrange("p (g s) -> p g s", g=nblk_tc)
                src_t = x.PS_A[0:64, 512:512 + TCW].rearrange(
                    "p (g u s) -> p g u s", g=nblk_tc, u=2)[:, :, 0, :]
                self.vo("ts", ktop, src_t, x.bk2[0:64, :], ALU.add, -1.0, ALU.mult)
                kbot = x.nKT2[64:128, 64 * g0:64 * (g0 + nblk_tc)]
                kbot = kbot.rearrange("p (g s) -> p g s", g=nblk_tc)
                src_b = x.PS_A[64:128, 512:512 + TCW].rearrange(
                    "p (g u s) -> p g u s", g=nblk_tc, u=2)[:, :, 1, :]
                self.vo("ts", kbot, src_b, x.bk2[64:128, :], ALU.add, -1.0, ALU.mult)
                if even:
                    self.vo("ts", x.VT[:, TCW * tc:TCW * tc + TCW],
                            x.PS_A[:, 1024:1024 + TCW], x.bv2, ALU.add)
                if own:
                    self.vo("ts", x.QT2[:, TCW * tc:TCW * tc + qw_], x.PS_A[:, 0:qw_],
                            x.bq2, ALU.add)
                self.emit("vector", lambda q: q.sem_inc(self.sem["qev"], 1))
                self.bump("qev")

            if self.c.get("ASTOP") == "qkv":
                return
            self.wait("tensor", "qev", self.cnt("qev"))
            for q0 in range(0, NQ, 512):
                qw2 = min(512, NQ - q0)
                self.mm(x.PS_A[0:1, q0:q0 + qw2], x.csb["halfones"][:],
                        x.QT2[:, q0:q0 + qw2], True, True,
                        inc=self.inc("prep_pe") if q0 + qw2 >= NQ else None)
            self.wait("scalar", "prep_pe", self.cnt("prep_pe"))
            self.act(x.epsp, x.PS_A[0:1, 0:NQ], AF.Exp, bias=x.csb["logeps"][:],
                     scale=-1.0 / 16.0, inc=self.inc("prep_act"))
            self.wait("tensor", "prep_act", self.cnt("prep_act"))
            for q0 in range(0, HALF, 512):
                qw2 = min(512, HALF - q0)
                self.mm(x.PS_A[0:1, q0:q0 + qw2], x.csb["halfones"][:],
                        x.nKT2[:, q0:q0 + qw2], True, True,
                        inc=self.inc("prep_pe") if q0 + qw2 >= HALF else None)
            self.wait("vector", "prep_pe", self.cnt("prep_pe"))
            ev_row = x.nskrow[0:1, :].rearrange("p (g u s) -> p g u s", g=NBLK, u=2)
            self.ts(ev_row[:, :, 0, :], x.PS_A[0:1, 0:HALF], 1.0 / 16.0, ALU.mult,
                    inc=self.inc("prep_dve"))
            self.wait("tensor", "prep_dve", self.cnt("prep_dve"))
            for q0 in range(0, HALF, 512):
                qw2 = min(512, HALF - q0)
                self.mm(x.PS_A[0:1, q0:q0 + qw2], x.csb["otherhalf"][:],
                        x.nKT2[:, q0:q0 + qw2], True, True,
                        inc=self.inc("prep_pe") if q0 + qw2 >= HALF else None)
            self.wait("vector", "prep_pe", self.cnt("prep_pe"))
            self.ts(ev_row[:, :, 1, :], x.PS_A[0:1, 0:HALF], 1.0 / 16.0, ALU.mult,
                    inc=self.inc("prep_dve"))
            self.wait("tensor", "prep_dve", self.cnt("prep_dve"))
            for b in range(NBLK):
                self.transp(x.PS_A[:, b:b + 1], x.nskrow[0:1, 128 * b:128 * b + 128],
                            x.csb["id1"][:],
                            inc=self.inc("prep_pe") if b == NBLK - 1 else None)
            self.wait("vector", "prep_pe", self.cnt("prep_pe"))
            self.cp(x.nskcol, x.PS_A[:, 0:NBLK], inc=self.inc("prep_dve"))
            self.wait("vector", "prep_dve", self.cnt("prep_dve"))
            vsrc = x.VT[64 * hq:64 * hq + 64, :]
            ngrp = (NBLK + 7) // 8
            for g in range(ngrp):
                nb = min(8, NBLK - 8 * g)
                self.wait("tensor", "prep_dve", self.cnt("prep_dve"))
                for j in range(nb):
                    b = 8 * g + j
                    self.transp(x.PS_A[:, 64 * j:64 * j + 64],
                                vsrc[:, 128 * b:128 * b + 128],
                                x.csb["id128"][64 * hq:64 * hq + 64, 64 * hq:64 * hq + 64],
                                inc=self.inc("prep_pe") if j == nb - 1 else None)
                self.wait("vector", "prep_pe", self.cnt("prep_pe"))
                dst = x.vp[:, 65 * 8 * g:65 * (8 * g + nb)]
                dst = dst.rearrange("p (b w) -> p b w", b=nb)[:, :, 0:64]
                src = x.PS_A[:, 0:64 * nb].rearrange("p (b w) -> p b w", b=nb)
                self.cp(dst, src, inc=self.inc("prep_dve"))
                self.wait("vector", "prep_dve", self.cnt("prep_dve"))
            ones_ap = x.vp[:, :].rearrange("p (b w) -> p b w", b=NBLK)[:, :, 64:65]
            self.mset(ones_ap, 1.0, inc=self.inc("prep_dve"))
            PREP = self.cnt("prep_dve")
            if self.c.get("ASTOP") == "prep":
                return

            headblk = h * NBLK
            na2 = self.cnt("tra")
            nv2 = self.cnt("trv")
            fa0 = self.cnt("tfa")
            fv0 = self.cnt("tfv")
            actB = sorted(b for b in range(NBLK) if b in ACT_B)
            dveB = [b for b in range(NBLK) if b not in ACT_B]

            def emit_produce_block(eng, b, B_i, prevb, Tring, s_rdy, s_free, f0):
                R = len(Tring)
                for s in range(64):
                    kp = 64 * B_i + s
                    if s == 0:
                        if B_i == 0:
                            self.wait(eng, "prep_dve", PREP)
                        else:
                            self.wait(eng, s_free, f0 + 31 * B_i)
                    elif s == R - 2 and B_i > 0:
                        self.wait(eng, "evd", headblk + prevb + 1)
                    elif s >= R and (s - R) % 2 == 0:
                        self.wait(eng, s_free, f0 + 31 * B_i + (s - R) // 2 + 1)
                    tile = Tring[kp % R]
                    inc = self.inc(s_rdy) if (kp % 2 == 1) else None
                    if eng == "scalar":
                        self.act(tile[:, 0:NQ], x.QT2[:], AF.Relu,
                                 bias=x.nKT2[:, 64 * b + s:64 * b + s + 1], inc=inc)
                    else:
                        self.ts(tile[:, 0:NQ], x.QT2[:],
                                x.nKT2[:, 64 * b + s:64 * b + s + 1],
                                ALU.add, 0.0, ALU.max, inc=inc)

            def emit_evac(b):
                gbi = headblk + b
                if b == 0:
                    self.wait("scalar", "prep_dve", PREP)
                    self.wait("scalar", "pad", h)
                else:
                    self.wait("scalar", "pvd", gbi - (h + 1) + 1)
                self.wait("scalar", "kdn", gbi + 1)
                self.act(x.kern[:, 0:NQ], x.PS_A[:, 0:NQ], AF.Exp, bias=x.nskcol[:, b:b + 1],
                         scale=-1.0 / 8.0, inc=self.inc("evd"))

            ai = 0
            prev_act = None
            for g in range(NBLK):
                if g >= 1:
                    emit_evac(g - 1)
                if g in ACT_B:
                    emit_produce_block("scalar", g, ai, prev_act, TA, "tra", "tfa", fa0)
                    prev_act = g
                    ai += 1
            emit_evac(NBLK - 1)

            vi = 0
            prev_dve = None
            for g in dveB:
                emit_produce_block("vector", g, vi, prev_dve, TV, "trv", "tfv", fv0)
                prev_dve = g
                vi += 1

            pa = 0
            pv = 0
            for b in range(NBLK):
                gbi = headblk + b
                isact = b in ACT_B
                ringp = TA if isact else TV
                s_rdy = "tra" if isact else "trv"
                base = na2 if isact else nv2
                for s in range(64):
                    kp = pa if isact else pv
                    if s % 2 == 0:
                        self.wait("tensor", s_rdy, base + kp // 2 + 1)
                    if s == 0:
                        if gbi > 0:
                            self.wait("tensor", "evd", gbi)
                        if b == 0:
                            self.wait("tensor", "prep_dve", PREP)
                    tile = ringp[kp % len(ringp)]
                    for ic in range(NIC):
                        inc = None
                        if ic == NIC - 1:
                            if s == 63:
                                inc = self.inc("kdn")
                            elif s % 2 == 1:
                                inc = self.inc("tfa" if isact else "tfv")
                        self.mm(x.PS_A[:, 512 * ic:512 * ic + ICW],
                                x.csb["wbase"][:, 64 - s:192 - s],
                                tile[:, ICW * ic:ICW * ic + ICW],
                                start=(s == 0), stop=(s == 63), inc=inc)
                    if isact:
                        pa += 1
                    else:
                        pv += 1
                if b > 0:
                    self._emit_av(b - 1, h)
            self._emit_av(NBLK - 1, h)

            if self.c.get("ASTOP") == "dist":
                return
            b0 = 64 * hq
            rb_v = x.ov1[b0:b0 + 64, 0:NQ]
            lo_v = x.ov2[b0:b0 + 64, 0:NQ]
            e_v = x.ov2[b0:b0 + 64, x.OV2W:x.OV2W + 2]
            dwc_v = x.dwc_s[b0:b0 + 64, :]
            dwcb_v = x.dwcb_s[b0:b0 + 64, :]
            mlr_v = x.mlrb[b0:b0 + 64, :]
            dst = x.AR[b0:b0 + 64, NQ * hp:NQ * hp + NQ]
            self.wait("vector", "pad", h + 1)
            self.vo("tt", x.epsp, x.PS_B[64:65, 0:NQ], x.epsp, ALU.add)
            self.recip(x.recrow, x.epsp, inc=self.inc("p3d"))
            self.wait("gpsimd", "p3d", self.cnt("p3d"))
            self.wait("gpsimd", "p3b", self.cnt("p3b"))
            if h == 0:
                self.wait("gpsimd", "bc0", self.cnt("bc0"))
            self.dma("gpsimd", x.rowscr[2:3, 0:NQ], x.recrow, "p3b")
            self.wait("gpsimd", "p3b", self.cnt("p3b"))
            self.dma("gpsimd", rb_v, x.rowscr[2:3, 0:NQ].broadcast_to((64, NQ)), "p3b")
            if self.c.get("ASTOP") == "p3rec":
                return
            self.wait("vector", "p3b", self.cnt("p3b"))
            self.tt(dst, x.PS_B[0:64, 0:NQ], rb_v, ALU.mult, inc=self.inc("pbf"))
            self.wait("vector", "pbf", self.cnt("pbf"))
            if self.c.get("ASTOP") == "p3div":
                return
            v64 = x.VT[:][b0:b0 + 64, :]
            self.vo("ts", lo_v, v64[:, 0:NQ], dwc_v[:, 1:2], ALU.mult)
            self.vo("stt", lo_v[:, 1:NQ], v64[:, 0:NQ - 1], dwc_v[:, 0:1],
                    lo_v[:, 1:NQ], ALU.mult, ALU.add)
            self.vo("stt", lo_v[:, 0:NQ - 1], v64[:, 1:NQ], dwc_v[:, 2:3],
                    lo_v[:, 0:NQ - 1], ALU.mult, ALU.add)
            if self.c.get("ASTOP") == "p3conv":
                return
            self.vo("ts", e_v[:, 0:1], v64[:, S - 1:S], mlr_v[:, 0:1], ALU.mult)
            self.vo("stt", lo_v[:, 0:1], e_v[:, 0:1], dwc_v[:, 0:1],
                    lo_v[:, 0:1], ALU.mult, ALU.add)
            self.vo("ts", e_v[:, 1:2], v64[:, NQ:NQ + 1], mlr_v[:, 1:2], ALU.mult)
            self.vo("stt", lo_v[:, NQ - 1:NQ], e_v[:, 1:2], dwc_v[:, 2:3],
                    lo_v[:, NQ - 1:NQ], ALU.mult, ALU.add)
            if self.c.get("ASTOP") == "p3edge":
                return
            self.vo("ts", lo_v, lo_v, dwcb_v, ALU.add)
            if self.c.get("ASTOP") == "p3bias":
                return
            v_ = self.c.get("P3VAR", 0)
            if v_ == 1:
                self.tt(dst, dst, lo_v, ALU.add, inc=None)
                self.bump("bgo")
                self.release("vector", "bgo")
            elif v_ == 2:
                self.tt(x.gs_sb[:, 0:NQ] if NQ <= 2 * 512 else lo_v, dst, lo_v,
                        ALU.add, inc=self.inc("bgo"))
                self.wait("vector", "bgo", self.cnt("bgo"))
            else:
                self.tt(dst, dst, lo_v, ALU.add, inc=self.inc("bgo"))
                self.wait("vector", "bgo", self.cnt("bgo"))

    def _emit_av(self, b, h):
        x, c = self.x, self.c
        NQ, ICW, NIC, NBLK = c["NQ"], c["ICW"], c["NIC"], c["NBLK"]
        gbi = h * NBLK + b
        self.wait("tensor", "evd", gbi + 1)
        if b == 0:
            self.wait("tensor", "pbf", h)
        for ic in range(NIC):
            inc = None
            if ic == NIC - 1:
                inc = self.inc("pad") if b == NBLK - 1 else self.inc("pvd")
            self.mm(x.PS_B[0:65, 512 * ic:512 * ic + ICW], x.vp[:, 65 * b:65 * b + 65],
                    x.kern[:, ICW * ic:ICW * ic + ICW],
                    start=(b == 0), stop=(b == NBLK - 1), inc=inc)

    def _phase_b(self):
        x, c = self.x, self.c
        DM, NQ, NDC, QCW, NQC, E = c["DM"], c["NQ"], c["NDC"], c["QCW"], c["NQC"], c["E"]
        TCW = c["TCW"]
        RR = 3
        BGO = self.cnt("bgo")
        for e in ("tensor", "vector", "scalar", "gpsimd", "sync"):
            self.wait(e, "bgo", BGO)
        for ci in range(NDC):
            self.dma("gpsimd", x.gw_sb[:, E * ci:E * ci + E],
                     x.gatewT[128 * ci:128 * ci + 128, :], "bdw")
        GWD = self.cnt("bdw")
        slotV = lambda oc: x.VT[:][:, QCW * oc:QCW * oc + QCW]

        for tc in range(NQC):
            self.wait("tensor", "cwt", self.cnt("cwt"))
            self.wait("vector", "x2w", self.cnt("x2w"))
            for oc in range(NDC):
                self.wait("gpsimd", "bev", self.cnt("bev"))
                self.wait("gpsimd", "bdw", self.cnt("bdw"))
                for ci in range(NDC):
                    self.dma("gpsimd", x.wbuf[:, 128 * ci:128 * ci + 128],
                             x.wprojT[128 * ci:128 * ci + 128, 128 * oc:128 * oc + 128],
                             "bdw")
                self.dma("gpsimd", x.bpj, x.bprojD[128 * oc:128 * oc + 128, :], "bdw")
                self.wait("tensor", "bdw", self.cnt("bdw"))
                slot = oc % 2
                key = ("b1", slot)
                if key in self.slotuse:
                    self.wait("tensor", "bev", self.slotuse[key])
                for ci in range(NDC):
                    self.mm(x.PS_A[:, 512 * slot:512 * slot + QCW],
                            x.wbuf[:, 128 * ci:128 * ci + 128],
                            x.AR[:, ci * NQ + QCW * tc:ci * NQ + QCW * tc + QCW],
                            start=(ci == 0), stop=(ci == NDC - 1),
                            inc=self.inc("bmm") if ci == NDC - 1 else None)
                self.wait("vector", "bmm", self.cnt("bmm"))
                self.ts(slotV(oc), x.PS_A[:, 512 * slot:512 * slot + QCW], x.bpj[:],
                        ALU.add, inc=self.inc("bev"))
                self.slotuse[("b1", slot)] = self.cnt("bev")
                self.wait("vector", "bev", self.cnt("bev"))
            if self.c.get("BSTOP") == "proj":
                return
            PREV = self.cnt("bev")
            self.wait("tensor", "bev", PREV)
            self.wait("scalar", "bev", PREV)
            for oc in range(NDC):
                kk = tc * NDC + oc
                if oc == 0:
                    self.wait("tensor", "st0", self.cnt("st0"))
                self.mm(x.PS_B[0:1, 0:QCW], x.csb["ones_c128"][:], slotV(oc),
                        start=(oc == 0), stop=(oc == NDC - 1),
                        inc=self.inc("bst") if oc == NDC - 1 else None)
                if kk >= 2:
                    self.wait("scalar", "sqf", self.cnt("sqf") - 1)
                self.act(x.sqb[:, (oc % 2) * TCW:(oc % 2) * TCW + QCW], slotV(oc),
                         AF.Square, inc=self.inc("sqd"))
                self.wait("tensor", "sqd", self.cnt("sqd"))
                self.mm(x.PS_B[0:1, 512:512 + QCW], x.csb["ones_c128"][:],
                        x.sqb[:, (oc % 2) * TCW:(oc % 2) * TCW + QCW],
                        start=(oc == 0), stop=(oc == NDC - 1), inc=self.inc("sqf"))
            if self.c.get("BSTOP") == "ln2stats":
                return
            self._ln_rows(x.PS_B, x.rows, QCW, srhs=("bst", self.cnt("bst")),
                          ssq=("sqf", self.cnt("sqf")))
            if self.c.get("BSTOP") == "ln2rows":
                return
            self.wait("gpsimd", "st0", self.cnt("st0"))
            self.wait("gpsimd", "bbc", self.cnt("bbc"))
            self.dma("gpsimd", x.rowscr[3:4, 0:QCW], x.rows[0:1, 1024:1024 + QCW], "bbc")
            self.dma("gpsimd", x.rowscr[4:5, 0:QCW], x.rows[0:1, 1536:1536 + QCW], "bbc")
            self.wait("gpsimd", "bbc", self.cnt("bbc"))
            self.dma("gpsimd", x.RBt[:, 0:QCW],
                     x.rowscr[3:4, 0:QCW].broadcast_to((128, QCW)), "bbc")
            self.dma("gpsimd", x.MBt[:, 0:QCW],
                     x.rowscr[4:5, 0:QCW].broadcast_to((128, QCW)), "bbc")
            BBC = self.cnt("bbc")
            self.wait("vector", "lnxh0", self.cnt("lnxh0"))
            for oc in range(NDC):
                kx = tc * NDC + oc
                self.wait("sync", "bev", self.cnt("bev"))
                slotx = "qmm%d" % (kx % RR)
                self.dma("sync", x.rr[kx % RR][:, 0:QCW],
                         x.xbT[128 * oc:128 * oc + 128, QCW * tc:QCW * tc + QCW], slotx)
                self.wait("vector", slotx, self.cnt(slotx))
                self.wait("vector", "bbc", BBC)
                self.vo("tt", x.gs_sb[:, 0:QCW], slotV(oc), x.RBt[:, 0:QCW], ALU.mult)
                self.vo("tt", x.gs_sb[:, 0:QCW], x.gs_sb[:, 0:QCW], x.MBt[:, 0:QCW],
                        ALU.add)
                self.tt(slotV(oc), x.gs_sb[:, 0:QCW], x.rr[kx % RR][:, 0:QCW], ALU.add,
                        inc=self.inc("bev"))
                self.wait("vector", "bev", self.cnt("bev"))
                self.wait("gpsimd", "bev", self.cnt("bev"))
                self.dma("gpsimd", x.x2d[128 * oc:128 * oc + 128, QCW * tc:QCW * tc + QCW],
                         slotV(oc), "x2w")
                self.wait("tensor", "bev", self.cnt("bev"))
                if oc == 0:
                    self.wait("tensor", "st0", self.cnt("st0"))
                self.mm(x.PS_B[0:1, 1024:1024 + QCW], x.csb["ones_c128"][:], slotV(oc),
                        start=(oc == 0), stop=(oc == NDC - 1),
                        inc=self.inc("bst") if oc == NDC - 1 else None)
                self.wait("scalar", "bev", self.cnt("bev"))
                kk = tc * NDC + oc
                if kk >= 2:
                    self.wait("scalar", "sqf", self.cnt("sqf") - 1)
                self.act(x.sqb[:, (oc % 2) * TCW:(oc % 2) * TCW + QCW], slotV(oc),
                         AF.Square, inc=self.inc("sqd"))
                self.wait("tensor", "sqd", self.cnt("sqd"))
                self.mm(x.PS_B[0:1, 1536:1536 + QCW], x.csb["ones_c128"][:],
                        x.sqb[:, (oc % 2) * TCW:(oc % 2) * TCW + QCW],
                        start=(oc == 0), stop=(oc == NDC - 1), inc=self.inc("sqf"))
            if self.c.get("BSTOP") == "apply":
                return
            self._ln_rows(x.PS_B[:, 1024:2048], x.rows, QCW,
                          srhs=("bst", self.cnt("bst")), ssq=("sqf", self.cnt("sqf")))
            self.wait("gpsimd", "st0", self.cnt("st0"))
            self.wait("gpsimd", "bbc", self.cnt("bbc"))
            self.dma("gpsimd", x.rowscr[5:6, 0:QCW], x.rows[0:1, 1024:1024 + QCW], "bbc")
            self.dma("gpsimd", x.rowscr[6:7, 0:QCW], x.rows[0:1, 1536:1536 + QCW], "bbc")
            self.wait("gpsimd", "bbc", self.cnt("bbc"))
            self.dma("gpsimd", x.RBt[:, 0:QCW],
                     x.rowscr[5:6, 0:QCW].broadcast_to((128, QCW)), "bbc")
            self.dma("gpsimd", x.MBt[:, 0:QCW],
                     x.rowscr[6:7, 0:QCW].broadcast_to((128, QCW)), "bbc")
            BBC3 = self.cnt("bbc")
            if tc == 0:
                self.wait("tensor", "bdw", GWD)
            for oc in range(NDC):
                slot = oc % 2
                self.wait("vector", "bbc", BBC3)
                if tc * NDC + oc >= 2:
                    self.wait("vector", "lnxh%d" % slot, self.cnt("lnxh%d" % slot))
                self.vo("tt", x.gs_sb[:, slot * TCW:slot * TCW + QCW], slotV(oc),
                        x.RBt[:, 0:QCW], ALU.mult)
                self.tt(x.gs_sb[:, slot * TCW:slot * TCW + QCW],
                        x.gs_sb[:, slot * TCW:slot * TCW + QCW], x.MBt[:, 0:QCW],
                        ALU.add, inc=self.inc("bev"))
                self.wait("vector", "bev", self.cnt("bev"))
                self.wait("gpsimd", "bev", self.cnt("bev"))
                self.dma("gpsimd", x.ln3d[128 * oc:128 * oc + 128, QCW * tc:QCW * tc + QCW],
                         x.gs_sb[:, slot * TCW:slot * TCW + QCW], "lnxh%d" % slot)
                self.wait("tensor", "bev", self.cnt("bev"))
                for gt in range(QCW // 128):
                    self.mm(x.PS_A[:, 512 * gt:512 * gt + E],
                            x.gs_sb[:, slot * TCW + 128 * gt:slot * TCW + 128 * gt + 128],
                            x.gw_sb[:, E * oc:E * oc + E],
                            start=(oc == 0), stop=(oc == NDC - 1),
                            inc=self.inc("bmm")
                            if (oc == NDC - 1 and gt == QCW // 128 - 1) else None)
            if self.c.get("BSTOP") == "ln3apply":
                return
            self.wait("vector", "bmm", self.cnt("bmm"))
            for gt in range(QCW // 128):
                g_all = tc * (QCW // 128) + gt
                self._softmax_top2(x.PS_A[:, 512 * gt:512 * gt + E],
                                   x.cwtm[:, E * g_all:E * g_all + E])
            if self.c.get("BSTOP") == "softmax":
                return
        NG = NQ // 128
        self.wait("tensor", "cwt", self.cnt("cwt"))
        for e in range(E):
            if e >= 1:
                self.wait("tensor", "cwt", self.cnt("cwt"))
            for g in range(NG):
                self.mm(x.PS_B[0:1, 128 * g:128 * g + 128],
                        x.cwtm[:, E * g + e:E * g + e + 1],
                        x.csb["id128"][:], True, True,
                        inc=self.inc("bmm") if g == NG - 1 else None)
            if self.c.get("BSTOP") == "cw0t" and e == 0:
                return
            self.wait("vector", "bmm", self.cnt("bmm"))
            if e >= 1:
                self.wait("vector", "cwr", self.cnt("cwr"))
            self.cp(x.ov2[0:1, 0:NQ], x.PS_B[0:1, 0:NQ], inc=self.inc("cwt"))
            self.wait("vector", "cwt", self.cnt("cwt"))
            if self.c.get("BSTOP") == "cw0c" and e == 0:
                return
            self.wait("gpsimd", "cwt", self.cnt("cwt"))
            self.dma("gpsimd", x.cwd[e:e + 1, :], x.ov2[0:1, 0:NQ], "cwr")
            if self.c.get("BSTOP") == "cw0d" and e == 0:
                return
    
    def _softmax_top2(self, ps, cw_out):
        E = self.c["E"]
        x = self.x
        s = x.scx
        self.vo("red", s[:, 0:1], ps, ALU.max)
        self.vo("ts", s[:, 1:1 + E], ps, s[:, 0:1], ALU.subtract)
        self.wait("scalar", "vch", self.cnt("vch"))
        self.ao(s[:, 1 + E:1 + 2 * E], s[:, 1:1 + E], AF.Exp)
        self.wait("vector", "ach", self.cnt("ach"))
        self.vo("red", s[:, 1 + 2 * E:2 + 2 * E], s[:, 1 + E:1 + 2 * E], ALU.add)
        self.vo("recip", s[:, 2 + 2 * E:3 + 2 * E], s[:, 1 + 2 * E:2 + 2 * E])
        sm = s[:, 3 + 2 * E:3 + 3 * E]
        self.vo("ts", sm, s[:, 1 + E:1 + 2 * E], s[:, 2 + 2 * E:3 + 2 * E], ALU.mult)
        self.vo("red", s[:, 0:1], sm, ALU.max)
        eq = s[:, 3 + 3 * E:3 + 4 * E]
        self.vo("ts", eq, sm, s[:, 0:1], ALU.is_ge)
        msk = s[:, 3 + 4 * E:3 + 5 * E]
        self.vo("ts", msk, eq, -1e30, ALU.mult)
        self.vo("tt", msk, msk, sm, ALU.add)
        self.vo("red", s[:, 1:2], msk, ALU.max)
        ge = s[:, 3 + 5 * E:3 + 6 * E]
        self.vo("ts", ge, sm, s[:, 1:2], ALU.is_ge)
        self.tt(cw_out, sm, ge, ALU.mult, inc=self.inc("cwt"))
        self.wait("vector", "cwt", self.cnt("cwt"))

    def _phase_c(self):
        x, c = self.x, self.c
        DM, NQ, NDC, QCW, NQC = c["DM"], c["NQ"], c["NDC"], c["QCW"], c["NQC"]
        INTER, E, NICH, TCW = c["INTER"], c["E"], c["NICH"], c["TCW"]
        RR = 3
        CWB = x.QT2
        for eng in ("tensor", "vector", "scalar", "gpsimd", "sync"):
            self.wait(eng, "cwr", 16 * E)
        for eng in ("vector", "tensor", "sync", "gpsimd"):
            self.wait(eng, "x2w", self.cnt("x2w"))
            self.wait(eng, "lnxh0", self.cnt("lnxh0"))
            self.wait(eng, "lnxh1", self.cnt("lnxh1"))

        def wg_tile(ci, ic):
            f = (ci * NICH + ic) * 128
            return x.VT[:][:, f:f + 128]

        VP_CAP = (max(c["NBLK"] * 65, NQ) // 128) * 128
        assert NICH * NDC * 128 <= VP_CAP + 2 * DM, "wu arena too small"

        def wu_tile(ci, ic):
            f = (ci * NICH + ic) * 128
            if f < VP_CAP:
                return x.vp[:][:, f:f + 128]
            return x.wbuf[:, f - VP_CAP:f - VP_CAP + 128]

        def wd_tile(half, ic):
            col = (half * NICH + ic) * 128
            return x.nKT2[:][:, col:col + 128]

        def hp_tile(ic, w):
            return x.kern[:, 512 * ic:512 * ic + w]

        for ei in range(-1, E):
            shared = ei < 0
            wgS = x.sgTD if shared else x.wgTD[ei]
            wuS = x.suTD if shared else x.wuTD[ei]
            wdS = x.sdTD if shared else x.wdTD[ei]
            self.wait("gpsimd", "frhs", self.cnt("frhs"))
            self.wait("gpsimd", "fw", self.cnt("fw"))
            for ci in range(NDC):
                for ic in range(NICH):
                    self.dma("gpsimd", wg_tile(ci, ic),
                             wgS[128 * ci:128 * ci + 128, 128 * ic:128 * ic + 128], "fw")
                    self.dma("gpsimd", wu_tile(ci, ic),
                             wuS[128 * ci:128 * ci + 128, 128 * ic:128 * ic + 128], "fw")
            FW = self.cnt("fw")
            if not shared:
                self.wait("gpsimd", "fhp", self.cnt("fhp"))
                self.wait("gpsimd", "l4", self.cnt("l4"))
                self.dma("gpsimd", CWB[:][:, 0:NQ],
                         x.cwd[ei:ei + 1, :].broadcast_to((128, NQ)), "l4")
            L4C = self.cnt("l4")

            self.wait("tensor", "fw", FW)
            FPD0 = self.cnt("fpd")
            for tc in range(NQC):
                for ci in range(NDC):
                    k = ((ei + 1) * NQC + tc) * NDC + ci
                    if k >= RR:
                        self.wait("sync", "frhs", k - RR + 1)
                    slot = "fmm%d" % (k % RR)
                    self.dma("sync", x.rr[k % RR][:, 0:QCW],
                             x.ln3d[128 * ci:128 * ci + 128, QCW * tc:QCW * tc + QCW],
                             slot)
                    self.wait("tensor", slot, self.cnt(slot))
                    if ci == 0:
                        self.wait("tensor", "fyw", self.cnt("fyw"))
                        self.wait("tensor", "fhp", self.cnt("fhp"))
                    for ic in range(NICH):
                        self.mm(x.PS_A[:, 512 * ic:512 * ic + QCW],
                                wg_tile(ci, ic), x.rr[k % RR][:, 0:QCW],
                                start=(ci == 0), stop=(ci == NDC - 1), inc=None)
                        inc = self.inc("frhs") if ic == NICH - 1 else None
                        self.mm(x.PS_B[:, 512 * ic:512 * ic + QCW],
                                wu_tile(ci, ic), x.rr[k % RR][:, 0:QCW],
                                start=(ci == 0), stop=(ci == NDC - 1), inc=inc)
                self.wait("scalar", "frhs", self.cnt("frhs"))
                self.wait("scalar", "l4", L4C)
                self.wait("vector", "l4", L4C)
                self.wait("vector", "fpd", self.cnt("fpd"))
                for ic in range(NICH):
                    self.wait("scalar", "vch", self.cnt("vch"))
                    self.ao(x.gs_sb[:, 0:QCW], x.PS_A[:, 512 * ic:512 * ic + QCW],
                            AF.Silu)
                    self.wait("vector", "ach", self.cnt("ach"))
                    self.vo("tt", x.gs_sb[:, TCW:TCW + QCW], x.gs_sb[:, 0:QCW],
                            x.PS_B[:, 512 * ic:512 * ic + QCW], ALU.mult)
                    if shared:
                        self.cp(hp_tile(ic, QCW), x.gs_sb[:, TCW:TCW + QCW],
                                inc=self.inc("fhp"))
                    else:
                        self.tt(hp_tile(ic, QCW), x.gs_sb[:, TCW:TCW + QCW],
                                CWB[:][:, QCW * tc:QCW * tc + QCW], ALU.mult,
                                inc=self.inc("fhp"))
                    self.wait("vector", "fhp", self.cnt("fhp"))
                FHPTC = self.cnt("fhp")
                self.wait("tensor", "fhp", FHPTC)
                for dmc in range(NDC):
                    half = dmc % 2
                    if ("wd", half) in self.slotuse:
                        self.wait("gpsimd", "fpd", self.slotuse[("wd", half)])
                    self.wait("gpsimd", "fw", self.cnt("fw"))
                    for ic in range(NICH):
                        self.dma("gpsimd", wd_tile(half, ic),
                                 wdS[128 * ic:128 * ic + 128, 128 * dmc:128 * dmc + 128],
                                 "fw")
                    FWD = self.cnt("fw")
                    self.wait("tensor", "fw", FWD)
                    yslot = dmc % 2
                    key = ("dn", yslot)
                    if key in self.slotuse:
                        self.wait("tensor", "fyw", self.slotuse[key])
                    for ic in range(NICH):
                        inc = self.inc("fpd") if ic == NICH - 1 else None
                        self.mm(x.PS_A[:, 512 * yslot:512 * yslot + QCW],
                                wd_tile(half, ic), hp_tile(ic, QCW),
                                start=(ic == 0), stop=(ic == NICH - 1), inc=inc)
                    self.slotuse[("wd", half)] = self.cnt("fpd")
                    self.wait("vector", "fpd", self.cnt("fpd"))
                    dstc = x.AR[:, dmc * NQ + QCW * tc:dmc * NQ + QCW * tc + QCW]
                    if shared:
                        self.cp(dstc, x.PS_A[:, 512 * yslot:512 * yslot + QCW],
                                inc=self.inc("fyw"))
                    else:
                        self.tt(dstc, dstc, x.PS_A[:, 512 * yslot:512 * yslot + QCW],
                                ALU.add, inc=self.inc("fyw"))
                    self.slotuse[("dn", yslot)] = self.cnt("fyw")
                    self.wait("vector", "fyw", self.cnt("fyw"))

        FYW = self.cnt("fyw")
        self.wait("tensor", "fyw", FYW)
        self.wait("scalar", "fyw", FYW)
        out_dma = []
        for tc in range(NQC):
            for oc in range(NDC):
                kk = tc * NDC + oc
                if oc == 0:
                    self.wait("tensor", "st0", self.cnt("st0"))
                self.mm(x.PS_B[0:1, 0:QCW], x.csb["ones_c128"][:],
                        x.AR[:, oc * NQ + QCW * tc:oc * NQ + QCW * tc + QCW],
                        start=(oc == 0), stop=(oc == NDC - 1),
                        inc=self.inc("bst") if oc == NDC - 1 else None)
                if kk >= 2:
                    self.wait("scalar", "sqf", self.cnt("sqf") - 1)
                self.act(x.sqb[:, (oc % 2) * TCW:(oc % 2) * TCW + QCW],
                         x.AR[:, oc * NQ + QCW * tc:oc * NQ + QCW * tc + QCW], AF.Square,
                         inc=self.inc("sqd"))
                self.wait("tensor", "sqd", self.cnt("sqd"))
                self.mm(x.PS_B[0:1, 512:512 + QCW], x.csb["ones_c128"][:],
                        x.sqb[:, (oc % 2) * TCW:(oc % 2) * TCW + QCW],
                        start=(oc == 0), stop=(oc == NDC - 1), inc=self.inc("sqf"))
            self._ln_rows(x.PS_B, x.rows, QCW, srhs=("bst", self.cnt("bst")),
                          ssq=("sqf", self.cnt("sqf")))
            self.wait("gpsimd", "st0", self.cnt("st0"))
            self.wait("gpsimd", "bbc", self.cnt("bbc"))
            self.dma("gpsimd", x.rowscr[0:1, 0:QCW], x.rows[0:1, 1024:1024 + QCW], "bbc")
            self.dma("gpsimd", x.rowscr[1:2, 0:QCW], x.rows[0:1, 1536:1536 + QCW], "bbc")
            self.wait("gpsimd", "bbc", self.cnt("bbc"))
            self.dma("gpsimd", x.RBt[:, 0:QCW],
                     x.rowscr[0:1, 0:QCW].broadcast_to((128, QCW)), "bbc")
            self.dma("gpsimd", x.MBt[:, 0:QCW],
                     x.rowscr[1:2, 0:QCW].broadcast_to((128, QCW)), "bbc")
            BBC = self.cnt("bbc")
            for oc in range(NDC):
                kx = tc * NDC + oc
                self.wait("sync", "frhs", self.cnt("frhs"))
                slotx = "fmm%d" % (kx % RR)
                self.dma("sync", x.rr[kx % RR][:, 0:QCW],
                         x.x2d[128 * oc:128 * oc + 128, QCW * tc:QCW * tc + QCW], slotx)
                self.wait("vector", slotx, self.cnt(slotx))
                self.wait("vector", "bbc", BBC)
                if out_dma:
                    self.wait("vector", "outw", out_dma[-1])
                arc = x.AR[:, oc * NQ + QCW * tc:oc * NQ + QCW * tc + QCW]
                self.vo("tt", x.gs_sb[:, 0:QCW], arc, x.RBt[:, 0:QCW], ALU.mult)
                self.vo("tt", x.gs_sb[:, 0:QCW], x.gs_sb[:, 0:QCW], x.MBt[:, 0:QCW],
                        ALU.add)
                self.tt(x.gs_sb[:, TCW:TCW + QCW],
                        x.gs_sb[:, 0:QCW], x.rr[kx % RR][:, 0:QCW], ALU.add,
                        inc=self.inc("frhs"))
                self.wait("vector", "frhs", self.cnt("frhs"))
                self.wait("gpsimd", "frhs", self.cnt("frhs"))
                self.dma("gpsimd", x.outD[128 * oc:128 * oc + 128, QCW * tc:QCW * tc + QCW],
                         x.gs_sb[:, TCW:TCW + QCW], "outw")
                out_dma.append(self.cnt("outw"))
        self.wait("gpsimd", "outw", self.cnt("outw"))



def stage_core_inputs(cfg, core, full):
    S, DM = cfg["S"], cfg["DM"]
    b, half = core // 2, core % 2
    x = np.asarray(full["L0_attn_x"][b], dtype=np.float32)
    NQ = S // 2
    xr = x if half == 0 else np.concatenate([x[NQ:], x[:NQ]], axis=0)
    d = {
        "xbT": np.ascontiguousarray(xr.T),
        "wqkvT": np.ascontiguousarray(np.asarray(full["L0_attn_qkv_w"], np.float32).T),
        "bqkv": np.ascontiguousarray(np.asarray(full["L0_attn_qkv_b"], np.float32).reshape(-1, 1)),
        "wprojT": np.ascontiguousarray(np.asarray(full["L0_attn_proj_w"], np.float32).T),
        "bproj": np.ascontiguousarray(np.asarray(full["L0_attn_proj_b"], np.float32).reshape(-1, 1)),
        "dwcw": np.ascontiguousarray(np.asarray(full["L0_attn_dwc_w"], np.float32).reshape(DM, 3)),
        "dwcb": np.ascontiguousarray(np.asarray(full["L0_attn_dwc_b"], np.float32).reshape(-1, 1)),
        "gatewT": np.ascontiguousarray(np.asarray(full["L0_ffn_gate_w"], np.float32).T),
        "wgT": np.ascontiguousarray(np.asarray(full["L0_ffn_wg"], np.float32).transpose(0, 2, 1)),
        "wuT": np.ascontiguousarray(np.asarray(full["L0_ffn_wu"], np.float32).transpose(0, 2, 1)),
        "wdT": np.ascontiguousarray(np.asarray(full["L0_ffn_wd"], np.float32).transpose(0, 2, 1)),
        "sgT": np.ascontiguousarray(np.asarray(full["L0_ffn_sg"], np.float32).T),
        "suT": np.ascontiguousarray(np.asarray(full["L0_ffn_su"], np.float32).T),
        "sdT": np.ascontiguousarray(np.asarray(full["L0_ffn_sd"], np.float32).T),
        "masks": np.array([[1.0 if half == 1 else 0.0,
                            1.0 if half == 0 else 0.0]], dtype=np.float32),
    }
    d.update(host_consts(cfg["LN_EPS"]))
    return d


_PROG_CACHE = {}


def kernel(**inputs):
    cfg = FULL_CFG
    if "full" not in _PROG_CACHE:
        _PROG_CACHE["full"] = Builder(cfg).build()
    nc = _PROG_CACHE["full"]
    in_maps = [stage_core_inputs(cfg, c, inputs) for c in range(8)]
    res = run_bass_kernel_spmd(nc, in_maps, core_ids=list(range(8)))
    S, DM = cfg["S"], cfg["DM"]
    NQ = S // 2
    out = np.empty((4, S, DM), dtype=np.float32)
    for c in range(8):
        b, half = c // 2, c % 2
        o = np.asarray(res.results[c]["out"])
        if half == 0:
            out[b, 0:NQ, :] = o.T
        else:
            out[b, NQ:S, :] = o.T
    return out


# version 09
# speedup vs baseline: 1.9487x; 1.9487x over previous
import math
from contextlib import ExitStack

import numpy as np
import concourse.bass as bass
import concourse.mybir as mybir
from concourse.bass_utils import run_bass_kernel_spmd

F32 = mybir.dt.float32
F32R = mybir.dt.float32r
AF = mybir.ActivationFunctionType
ALU = mybir.AluOpType

LOG_EPS = float(np.log(1e-6))

FULL_CFG = dict(S=4096, DM=1024, H=16, DH=64, INTER=512, E=8,
                LN_EPS=1e-5, ACT_FRAC=0.375)

CONST_SHAPES = {
    "wbase": ([128, 192], F32R),
    "id128": ([128, 128], F32),
    "id1": ([1, 1], F32),
    "ones_c128": ([128, 1], F32),
    "halfones": ([128, 1], F32),
    "otherhalf": ([128, 1], F32),
    "logeps": ([1, 1], F32),
    "lneps": ([1, 1], F32),
}


def host_consts(ln_eps):
    wb = np.zeros((128, 192), dtype=np.float32)
    wb[0:64, 64] = 1.0
    wb[64:128, 128] = 1.0
    ho = np.zeros((128, 1), dtype=np.float32); ho[0:64] = 1.0
    oh = np.zeros((128, 1), dtype=np.float32); oh[64:128] = 1.0
    return {
        "wbase": wb,
        "id128": np.eye(128, dtype=np.float32),
        "id1": np.ones((1, 1), dtype=np.float32),
        "ones_c128": np.ones((128, 1), dtype=np.float32),
        "halfones": ho,
        "otherhalf": oh,
        "logeps": np.full((1, 1), LOG_EPS, dtype=np.float32),
        "lneps": np.full((1, 1), ln_eps, dtype=np.float32),
    }


def pick_act_blocks(nblk, frac):
    n = int(round(frac * nblk))
    n = min(n, (nblk + 1) // 2)
    out = set()
    if n > 0:
        step = nblk / n
        prev = -2
        for i in range(n):
            b = min(nblk - 1, int(i * step))
            if b <= prev + 1:
                b = prev + 2
            if b >= nblk:
                break
            out.add(b)
            prev = b
    return out


class Builder:
    ENGINES = ("sync", "gpsimd", "tensor", "vector", "scalar")

    def __init__(self, cfg):
        c = dict(cfg)
        c["NQ"] = c["S"] // 2
        c["NBLK"] = c["S"] // 128
        c["NDC"] = c["DM"] // 128
        c["TCW"] = min(512, c["S"])
        c["NTC"] = c["S"] // c["TCW"]
        c["QCW"] = min(512, c["NQ"])
        c["NQC"] = max(1, c["NQ"] // c["QCW"])
        c["ICW"] = min(512, c["NQ"])
        c["NIC"] = max(1, c["NQ"] // c["ICW"])
        c["NICH"] = c["INTER"] // 128
        c["HP"] = c["H"] // 2
        c["ACT_B"] = pick_act_blocks(c["NBLK"], c["ACT_FRAC"])
        c["ACT_S"] = frozenset(s for s in range(64) if s % 8 in (1, 4, 6))
        self.c = c
        self.ops = {e: [] for e in self.ENGINES}
        self.ct = {}
        self.sem = {}
        self.pe_carry = []
        self.slotuse = {}

    def emit(self, eng, fn):
        self.ops[eng].append(fn)

    def wait(self, eng, s, v):
        if v > 0:
            self.emit(eng, lambda q, s=s, v=v: q.wait_ge(self.sem[s], v))

    def bump(self, s, n=1):
        self.ct[s] = self.ct.get(s, 0) + n
        return self.ct[s]

    def cnt(self, s):
        return self.ct.get(s, 0)

    def op(self, eng, fn, inc=None, carry_ok=False):
        if inc is None and carry_ok and self.pe_carry:
            inc = self.pe_carry.pop(0)
        if inc is not None:
            s, n = inc
            self.emit(eng, lambda q, fn=fn, s=s, n=n: fn(q).then_inc(self.sem[s], n))
        else:
            self.emit(eng, lambda q, fn=fn: fn(q))

    def inc(self, s, n=1):
        self.bump(s, n)
        return (s, n)

    def carry(self, s, n=1):
        self.bump(s, n)
        self.pe_carry.append((s, n))

    def release(self, eng, s, n=1):
        self.bump(s, n)
        self.emit(eng, lambda q, s=s, n=n: q.sem_inc(self.sem[s], n))

    def dma(self, eng, dst, src, s):
        v = self.bump(s, 16)
        self.emit(eng, lambda q, dst=dst, src=src, s=s: q.dma_start(out=dst, in_=src)
                  .then_inc(self.sem[s], 16))
        return v

    def mm(self, out, lhsT, rhs, start, stop, inc=None):
        self.op("tensor", lambda t, o=out, l=lhsT, r=rhs, st=start, sp=stop:
                t.matmul(out=o, lhsT=l, rhs=r, start=st, stop=sp), inc=inc,
                carry_ok=(inc is None))

    def transp(self, out, in_, ident, inc=None):
        self.op("tensor", lambda t, o=out, i=in_, d=ident:
                t.transpose(out=o, in_=i, identity=d), inc=inc, carry_ok=(inc is None))

    def act(self, out, in_, func, bias=0.0, scale=1.0, inc=None):
        self.op("scalar", lambda a, o=out, i=in_, f=func, b=bias, s=scale:
                a.activation(out=o, in_=i, func=f, bias=b, scale=s), inc=inc)

    def ts(self, out, in0, s1, op0, s2=None, op1=None, inc=None):
        if s2 is None:
            self.op("vector", lambda v, o=out, i=in0, a=s1, p=op0:
                    v.tensor_single_scalar(out=o, in_=i, scalar=a, op=p), inc=inc)
        else:
            self.op("vector", lambda v, o=out, i=in0, a=s1, b=s2, p=op0, p1=op1:
                    v.tensor_scalar(out=o, in0=i, scalar1=a, scalar2=b, op0=p, op1=p1),
                    inc=inc)

    def tt(self, out, in0, in1, alu, inc=None):
        self.op("vector", lambda v, o=out, a=in0, b=in1, p=alu:
                v.tensor_tensor(out=o, in0=a, in1=b, op=p), inc=inc)

    def stt(self, out, in0, scalar, in1, op0, op1, inc=None):
        self.op("vector", lambda v, o=out, a=in0, s=scalar, b=in1, p0=op0, p1=op1:
                v.scalar_tensor_tensor(out=o, in0=a, scalar=s, in1=b, op0=p0, op1=p1),
                inc=inc)

    def cp(self, out, in_, inc=None):
        self.op("vector", lambda v, o=out, i=in_: v.tensor_copy(out=o, in_=i), inc=inc)

    def mset(self, ap_, val, inc=None):
        self.op("vector", lambda v, a=ap_, x=val: v.memset(a, x), inc=inc)

    def red(self, out, in_, alu, inc=None):
        self.op("vector", lambda v, o=out, i=in_, p=alu:
                v.tensor_reduce(out=o, in_=i, axis=mybir.AxisListType.X, op=p), inc=inc)

    def recip(self, out, in_, inc=None):
        self.op("vector", lambda v, o=out, i=in_: v.reciprocal(out=o, in_=i), inc=inc)

    def vo(self, kind, *a, **kw):
        kw["inc"] = self.inc("vch")
        getattr(self, kind)(*a, **kw)
        self.wait("vector", "vch", self.cnt("vch"))

    def ao(self, *a, **kw):
        kw["inc"] = self.inc("ach")
        self.act(*a, **kw)
        self.wait("scalar", "ach", self.cnt("ach"))

    def build(self):
        c = self.c
        S, DM, H, DH, INTER, E = c["S"], c["DM"], c["H"], c["DH"], c["INTER"], c["E"]
        NQ, NBLK, NDC, TCW, NTC = c["NQ"], c["NBLK"], c["NDC"], c["TCW"], c["NTC"]
        QCW, NQC, ICW, NIC, NICH, HP = (c["QCW"], c["NQC"], c["ICW"], c["NIC"],
                                        c["NICH"], c["HP"])

        nc = bass.Bass("TRN2", target_bir_lowering=False, debug=False)
        self.nc = nc
        dram = {}

        def din(name, shape, dt_=F32):
            dram[name] = nc.dram_tensor(name, shape, dt_, kind="ExternalInput").ap()
            return dram[name]

        xbT = din("xbT", [DM, S])
        wqkvT = din("wqkvT", [DM, 3 * DM])
        bqkv = din("bqkv", [3 * DM, 1])
        wprojT = din("wprojT", [DM, DM])
        bprojD = din("bproj", [DM, 1])
        dwcwD = din("dwcw", [DM, 3])
        dwcbD = din("dwcb", [DM, 1])
        gatewT = din("gatewT", [DM, E])
        wgTD = din("wgT", [E, DM, INTER])
        wuTD = din("wuT", [E, DM, INTER])
        wdTD = din("wdT", [E, INTER, DM])
        sgTD = din("sgT", [DM, INTER])
        suTD = din("suT", [DM, INTER])
        sdTD = din("sdT", [INTER, DM])
        masksD = din("masks", [1, 2])
        for name, (shape, dt_) in CONST_SHAPES.items():
            din(name, shape, dt_)

        lnx1 = nc.dram_tensor("lnx1", [DM, S], F32, kind="Internal").ap()
        ln3d = nc.dram_tensor("ln3d", [DM, NQ], F32, kind="Internal").ap()
        x2d = nc.dram_tensor("x2d", [DM, NQ], F32, kind="Internal").ap()
        rowscr = nc.dram_tensor("rowscr", [8, max(S, 2048)], F32, kind="Internal").ap()
        cwd = nc.dram_tensor("cwd", [E, NQ], F32, kind="Internal").ap()
        outD = nc.dram_tensor("out", [DM, NQ], F32, kind="ExternalOutput").ap()

        es = ExitStack()
        with es:
            sbuf = lambda name, shape, dt_=F32: es.enter_context(
                nc.sbuf_tensor(name, shape, dt_))
            csb = {name: sbuf("c_" + name, shape, dt_)
                   for name, (shape, dt_) in CONST_SHAPES.items()}

            QT2 = sbuf("QT2", [128, NQ])
            nKT2 = sbuf("nKT2", [128, max(S // 2, 2 * NICH * 128)])
            VT = sbuf("VT", [128, max(S, NICH * NDC * 128, (c["DM"] // 128) * min(512, c["NQ"]))])
            vp = sbuf("vp", [128, max(NBLK * 65, NQ)])
            smallb = sbuf("smallb", [128, max(NBLK, 6 * c["E"] + 4) + 12])
            SMB = max(NBLK, 6 * c["E"] + 4)
            nskcol = smallb[:, 0:NBLK]
            mlrb = smallb[:, SMB + 8:SMB + 10]
            RN = 6
            ring = [sbuf(f"ring{i}", [128, NQ], F32R) for i in range(RN)]
            ringf = [r[:].bitcast(F32) for r in ring]
            kern = sbuf("kern", [128, max(NQ, DM)])
            AR = sbuf("AR", [128, NDC * NQ])
            wbuf = sbuf("wbuf", [128, 2 * DM])
            RR = 3
            rr = [sbuf(f"rr{i}", [128, TCW]) for i in range(RR)]
            bq2 = smallb[:, SMB:SMB + 1]
            bk2 = smallb[:, SMB + 1:SMB + 2]
            bv2 = smallb[:, SMB + 2:SMB + 3]
            bpj = smallb[:, SMB + 3:SMB + 4]
            dwc_s = smallb[:, SMB + 4:SMB + 7]
            dwcb_s = smallb[:, SMB + 7:SMB + 8]
            rowT_t = sbuf("rowT", [128, max(S, 4096)])
            nskrow = rowT_t[0:1, 0:S]
            rows = rowT_t[32:33, 0:4096]
            epsp = rowT_t[64:65, 0:NQ]
            recrow = rowT_t[64:65, 2048:2048 + NQ]
            ov1 = sbuf("ov1", [128, max(NQ, 2 * TCW)])
            RBt = ov1[:, 0:TCW]
            MBt = ov1[:, TCW:2 * TCW]
            OV2W = max(NQ, 4 * TCW)
            ov2 = sbuf("ov2", [128, OV2W + 2])
            sqb = ov2[:, 0:2 * TCW]
            gs_sb = ov2[:, 2 * TCW:4 * TCW]
            NG = NQ // 128
            cwtm = kern[:][:, NDC * E:NDC * E + NG * E]
            scx = smallb[:, :]

            PS_A = es.enter_context(nc.psum_tensor("PS_A", [128, 2048], F32))
            PS_B = es.enter_context(nc.psum_tensor("PS_B", [128, 2048], F32))

            semnames = ["cst", "x1rhs", "x2rhs", "sqd", "sqf",
                        "x1mm0", "x1mm1", "x1mm2", "x1mm3",
                        "qmm0", "qmm1", "qmm2", "qmm3",
                        "fmm0", "fmm1", "fmm2", "fmm3",
                        "lnxh0", "lnxh1",
                        "st0", "bc0", "lnxw", "lnxc",
                        "qw", "qrhs", "qgd", "qev",
                        "prep_pe", "prep_dve", "prep_act",
                        "tra", "tfa", "trv", "tfv",
                        "kdn", "evd", "pvd", "pad", "pbf", "p3d", "p3b",
                        "vch", "ach", "bgo", "x2w",
                        "bdw", "bmm", "bev", "bst", "bbc", "cwt", "cwr",
                        "fw", "frhs", "fgd", "fhp", "fpd", "fyw", "l4",
                        "outw"]
            for s in semnames:
                self.sem[s] = es.enter_context(nc.semaphore(s))

            ctx = dict(csb=csb, mlrb=mlrb, QT2=QT2, nKT2=nKT2, VT=VT, vp=vp,
                       nskcol=nskcol, nskrow=nskrow, epsp=epsp, recrow=recrow, ring=ring, NG=NG,
                       ringf=ringf, kern=kern, AR=AR, wbuf=wbuf, rr=rr, RR=RR,
                       bq2=bq2, bk2=bk2, bv2=bv2, bpj=bpj, dwc_s=dwc_s,
                       dwcb_s=dwcb_s, ov1=ov1, ov2=ov2, OV2W=OV2W,
                       rows=rows, RBt=RBt, MBt=MBt, sqb=sqb,
                       cwtm=cwtm, gs_sb=gs_sb, scx=scx, gw_sb=kern[:][:, 0:NDC * E],
                       PS_A=PS_A, PS_B=PS_B,
                       xbT=xbT, wqkvT=wqkvT, bqkv=bqkv, wprojT=wprojT,
                       bprojD=bprojD, dwcwD=dwcwD, dwcbD=dwcbD, gatewT=gatewT,
                       wgTD=wgTD, wuTD=wuTD, wdTD=wdTD, sgTD=sgTD, suTD=suTD,
                       sdTD=sdTD, masksD=masksD, lnx1=lnx1, ln3d=ln3d, x2d=x2d,
                       rowscr=rowscr, outD=outD, cwd=cwd)
            self.x = type("X", (), ctx)
            self.dram = dram

            phases = self.c.get("PHASES", "0ABC")
            self._consts()
            if "0" in phases:
                self._phase0()
            if "A" in phases:
                self._phase_a()
            if "B" in phases:
                self._phase_b()
            if "C" in phases:
                self._phase_c()
            if "C" not in phases:
                for s_ in ("bgo", "vch", "ach"):
                    self.wait("gpsimd", s_, self.cnt(s_))
                self.dma("gpsimd", outD[0:128, 0:min(512, NQ)],
                         self.x.gs_sb[:, 0:min(512, NQ)], "outw")
                self.wait("gpsimd", "outw", self.cnt("outw"))

            block = es.enter_context(nc.Block())

            @block.sync
            def _(q):
                for fn in self.ops["sync"]:
                    fn(q)

            @block.gpsimd
            def _(q):
                for fn in self.ops["gpsimd"]:
                    fn(q)

            @block.tensor
            def _(q):
                for fn in self.ops["tensor"]:
                    fn(q)

            @block.vector
            def _(q):
                for fn in self.ops["vector"]:
                    fn(q)

            @block.scalar
            def _(q):
                for fn in self.ops["scalar"]:
                    fn(q)

        return nc

    def _consts(self):
        x = self.x
        for name in CONST_SHAPES:
            self.dma("gpsimd", x.csb[name][:], self.dram[name], "cst")
        self.dma("gpsimd", x.rowscr[7:8, 0:2], x.masksD, "cst")
        self.wait("gpsimd", "cst", self.cnt("cst"))
        self.dma("gpsimd", x.mlrb, x.rowscr[7:8, 0:2].broadcast_to((128, 2)), "cst")
        CST = self.cnt("cst")
        for e in ("tensor", "vector", "scalar", "sync"):
            self.wait(e, "cst", CST)

    def _phase0(self):
        x, c = self.x, self.c
        DM, NDC, TCW, NTC, RR = c["DM"], c["NDC"], c["TCW"], c["NTC"], 3
        rel = []
        for tc in range(NTC):
            for ci in range(NDC):
                k = tc * 2 * NDC + ci
                if k >= RR:
                    self.wait("sync", *rel[k - RR])
                slot = "x1mm%d" % (k % RR)
                self.dma("sync", x.rr[k % RR][:],
                         x.xbT[128 * ci:128 * ci + 128, TCW * tc:TCW * tc + TCW], slot)
                self.wait("scalar", slot, self.cnt(slot))
                kk = tc * NDC + ci
                if kk >= 2:
                    self.wait("scalar", "x1rhs", kk - 1)
                rel.append(None)
                self.act(x.sqb[:, (ci % 2) * TCW:(ci % 2) * TCW + TCW], x.rr[k % RR][:],
                         AF.Square, inc=self.inc("sqd"))
                self.wait("tensor", slot, self.cnt(slot))
                if ci == 0:
                    self.wait("tensor", "st0", self.cnt("st0"))
                self.mm(x.PS_A[0:1, 0:TCW], x.csb["ones_c128"][:], x.rr[k % RR][:],
                        start=(ci == 0), stop=(ci == NDC - 1), inc=None)
                self.wait("tensor", "sqd", self.cnt("sqd"))
                self.mm(x.PS_A[0:1, 512:512 + TCW], x.csb["ones_c128"][:],
                        x.sqb[:, (ci % 2) * TCW:(ci % 2) * TCW + TCW],
                        start=(ci == 0), stop=(ci == NDC - 1), inc=self.inc("x1rhs"))
                rel[k] = ("x1rhs", self.cnt("x1rhs"))
            self._ln_rows(x.PS_A, x.rows, TCW, srhs=("x1rhs", self.cnt("x1rhs")))
            self.wait("gpsimd", "st0", self.cnt("st0"))
            self.wait("gpsimd", "bc0", self.cnt("bc0"))
            self.dma("gpsimd", x.rowscr[0:1, 0:TCW], x.rows[0:1, 1024:1024 + TCW], "bc0")
            self.dma("gpsimd", x.rowscr[1:2, 0:TCW], x.rows[0:1, 1536:1536 + TCW], "bc0")
            self.wait("gpsimd", "bc0", self.cnt("bc0"))
            self.dma("gpsimd", x.RBt, x.rowscr[0:1, 0:TCW].broadcast_to((128, TCW)), "bc0")
            self.dma("gpsimd", x.MBt, x.rowscr[1:2, 0:TCW].broadcast_to((128, TCW)), "bc0")
            BC = self.cnt("bc0")
            for ci in range(NDC):
                k2 = tc * 2 * NDC + NDC + ci
                if k2 >= RR:
                    self.wait("sync", *rel[k2 - RR])
                slot2 = "x1mm%d" % (k2 % RR)
                self.dma("sync", x.rr[k2 % RR][:],
                         x.xbT[128 * ci:128 * ci + 128, TCW * tc:TCW * tc + TCW], slot2)
                self.wait("vector", slot2, self.cnt(slot2))
                self.wait("vector", "bc0", BC)
                kk2 = tc * NDC + ci
                half = ci % 2
                if kk2 >= 2:
                    self.wait("vector", "lnxh%d" % half, self.cnt("lnxh%d" % half))
                self.vo("tt", x.gs_sb[:, half * TCW:half * TCW + TCW],
                        x.rr[k2 % RR][:], x.RBt, ALU.mult)
                self.release("vector", "x2rhs")
                rel.append(("x2rhs", self.cnt("x2rhs")))
                self.vo("tt", x.gs_sb[:, half * TCW:half * TCW + TCW],
                        x.gs_sb[:, half * TCW:half * TCW + TCW], x.MBt, ALU.add)
                self.release("vector", "lnxw")
                self.wait("gpsimd", "lnxw", self.cnt("lnxw"))
                self.dma("gpsimd", x.lnx1[128 * ci:128 * ci + 128, TCW * tc:TCW * tc + TCW],
                         x.gs_sb[:, half * TCW:half * TCW + TCW], "lnxh%d" % half)
        self.wait("sync", "lnxh0", self.cnt("lnxh0"))
        self.wait("sync", "lnxh1", self.cnt("lnxh1"))

    def _ln_rows(self, PS, rows, W, srhs=None, ssq=None):
        DM = self.c["DM"]
        if srhs:
            self.wait("vector", srhs[0], srhs[1])
        if ssq:
            self.wait("vector", ssq[0], ssq[1])
        self.vo("ts", rows[0:1, 0:W], PS[0:1, 0:W], 1.0 / DM, ALU.mult)
        self.vo("ts", rows[0:1, 512:512 + W], PS[0:1, 512:512 + W], 1.0 / DM, ALU.mult)
        self.wait("scalar", "vch", self.cnt("vch"))
        self.ao(rows[0:1, 2048:2048 + W], rows[0:1, 0:W], AF.Square)
        self.wait("vector", "ach", self.cnt("ach"))
        self.vo("tt", rows[0:1, 2560:2560 + W], rows[0:1, 512:512 + W],
                rows[0:1, 2048:2048 + W], ALU.subtract)
        self.wait("scalar", "vch", self.cnt("vch"))
        self.ao(rows[0:1, 2048:2048 + W], rows[0:1, 2560:2560 + W], AF.Ln,
                bias=self.x.csb["lneps"][:])
        self.ao(rows[0:1, 1024:1024 + W], rows[0:1, 2048:2048 + W], AF.Exp, scale=-0.5)
        self.wait("vector", "ach", self.cnt("ach"))
        self.vo("tt", rows[0:1, 2560:2560 + W], rows[0:1, 0:W], rows[0:1, 1024:1024 + W],
                ALU.mult)
        self.ts(rows[0:1, 1536:1536 + W], rows[0:1, 2560:2560 + W], -1.0, ALU.mult,
                inc=self.inc("st0"))

    def _phase_a(self):
        x, c = self.x, self.c
        S, DM, H, DH = c["S"], c["DM"], c["H"], c["DH"]
        NQ, NBLK, NDC, TCW, NTC = c["NQ"], c["NBLK"], c["NDC"], c["TCW"], c["NTC"]
        ICW, NIC = c["ICW"], c["NIC"]
        ACT_B = c["ACT_B"]
        HALF = S // 2
        RR = 3
        TA = [x.ring[i] for i in range(3)]
        TV = [x.ring[3 + i] for i in range(3)]
        NQCH = max(1, NQ // TCW)
        QW_LAST = NQ - TCW * (NQCH - 1) if NQ >= TCW else NQ

        for h in range(self.c.get("HLIM", H)):
            hp, hq = h // 2, h % 2
            even = (hq == 0)

            self.wait("gpsimd", "qev", self.cnt("qev"))
            for ci in range(NDC):
                src = x.wqkvT[128 * ci:128 * ci + 128, DH * h:DH * h + DH]
                self.dma("gpsimd", x.wbuf[:, 128 * ci:128 * ci + 64], src, "qw")
                self.dma("gpsimd", x.wbuf[:, 128 * ci + 64:128 * ci + 128], src, "qw")
            for ci in range(NDC):
                src = x.wqkvT[128 * ci:128 * ci + 128, DM + DH * h:DM + DH * h + DH]
                self.dma("gpsimd", x.wbuf[:, DM + 128 * ci:DM + 128 * ci + 64], src, "qw")
                self.dma("gpsimd", x.wbuf[:, DM + 128 * ci + 64:DM + 128 * ci + 128], src, "qw")
            if even:
                self.wait("gpsimd", "pad", h)
                for ci in range(NDC):
                    src = x.wqkvT[128 * ci:128 * ci + 128,
                                  2 * DM + 128 * hp:2 * DM + 128 * hp + 128]
                    self.dma("gpsimd", x.kern[:, 128 * ci:128 * ci + 128],
                             src, "qw")
            self.dma("gpsimd", x.bq2[0:64, :], x.bqkv[DH * h:DH * h + DH, :], "qw")
            self.dma("gpsimd", x.bq2[64:128, :], x.bqkv[DH * h:DH * h + DH, :], "qw")
            self.dma("gpsimd", x.bk2[0:64, :], x.bqkv[DM + DH * h:DM + DH * h + DH, :], "qw")
            self.dma("gpsimd", x.bk2[64:128, :], x.bqkv[DM + DH * h:DM + DH * h + DH, :], "qw")
            if even:
                self.dma("gpsimd", x.bv2,
                         x.bqkv[2 * DM + 128 * hp:2 * DM + 128 * hp + 128, :], "qw")
            self.wait("gpsimd", "bgo", self.cnt("bgo"))
            self.dma("gpsimd", x.dwc_s[64 * hq:64 * hq + 64, :],
                     x.dwcwD[DH * h:DH * h + DH, :], "qw")
            self.dma("gpsimd", x.dwcb_s[64 * hq:64 * hq + 64, :],
                     x.dwcbD[DH * h:DH * h + DH, :], "qw")
            QW = self.cnt("qw")

            self.wait("tensor", "qw", QW)
            for tc in range(NTC):
                own = tc < NQCH
                qw_ = TCW if (tc < NQCH - 1 or NQ >= TCW * NQCH) else QW_LAST
                if NQ < TCW:
                    qw_ = NQ
                if tc > 0:
                    self.wait("tensor", "qev", self.cnt("qev"))
                for ci in range(NDC):
                    k = (h * NTC + tc) * NDC + ci
                    if k >= RR:
                        self.wait("sync", "qrhs", k - RR + 1)
                    slot = "qmm%d" % (k % RR)
                    self.dma("sync", x.rr[k % RR][:],
                             x.lnx1[128 * ci:128 * ci + 128, TCW * tc:TCW * tc + TCW], slot)
                    self.wait("tensor", slot, self.cnt(slot))
                    first, last = (ci == 0), (ci == NDC - 1)
                    roles = [("K", x.PS_A[:, 512:512 + TCW], x.wbuf[:, DM:2 * DM], TCW)]
                    if even:
                        roles.append(("V", x.PS_A[:, 1024:1024 + TCW],
                                      x.kern[:, 0:DM], TCW))
                    if own:
                        roles.append(("Q", x.PS_A[:, 0:qw_], x.wbuf[:, 0:DM], qw_))
                    for ri, (rn, pslice, wslab, w_) in enumerate(roles):
                        lastrole = ri == len(roles) - 1
                        inc = self.inc("qrhs") if lastrole else None
                        self.mm(pslice, wslab[:, 128 * ci:128 * ci + 128],
                                x.rr[k % RR][:, 0:w_], start=first, stop=last, inc=inc)
                self.wait("vector", "qrhs", self.cnt("qrhs"))
                nblk_tc = TCW // 128
                g0 = tc * nblk_tc
                ktop = x.nKT2[0:64, 64 * g0:64 * (g0 + nblk_tc)]
                ktop = ktop.rearrange("p (g s) -> p g s", g=nblk_tc)
                src_t = x.PS_A[0:64, 512:512 + TCW].rearrange(
                    "p (g u s) -> p g u s", g=nblk_tc, u=2)[:, :, 0, :]
                self.vo("ts", ktop, src_t, x.bk2[0:64, :], ALU.add, -1.0, ALU.mult)
                kbot = x.nKT2[64:128, 64 * g0:64 * (g0 + nblk_tc)]
                kbot = kbot.rearrange("p (g s) -> p g s", g=nblk_tc)
                src_b = x.PS_A[64:128, 512:512 + TCW].rearrange(
                    "p (g u s) -> p g u s", g=nblk_tc, u=2)[:, :, 1, :]
                self.vo("ts", kbot, src_b, x.bk2[64:128, :], ALU.add, -1.0, ALU.mult)
                if even:
                    self.vo("ts", x.VT[:, TCW * tc:TCW * tc + TCW],
                            x.PS_A[:, 1024:1024 + TCW], x.bv2, ALU.add)
                if own:
                    self.vo("ts", x.QT2[:, TCW * tc:TCW * tc + qw_], x.PS_A[:, 0:qw_],
                            x.bq2, ALU.add)
                self.emit("vector", lambda q: q.sem_inc(self.sem["qev"], 1))
                self.bump("qev")

            if self.c.get("ASTOP") == "qkv":
                return
            self.wait("tensor", "qev", self.cnt("qev"))
            for q0 in range(0, NQ, 512):
                qw2 = min(512, NQ - q0)
                self.mm(x.PS_A[0:1, q0:q0 + qw2], x.csb["halfones"][:],
                        x.QT2[:, q0:q0 + qw2], True, True,
                        inc=self.inc("prep_pe") if q0 + qw2 >= NQ else None)
            self.wait("scalar", "prep_pe", self.cnt("prep_pe"))
            self.act(x.epsp, x.PS_A[0:1, 0:NQ], AF.Exp, bias=x.csb["logeps"][:],
                     scale=-1.0 / 16.0, inc=self.inc("prep_act"))
            self.wait("tensor", "prep_act", self.cnt("prep_act"))
            for q0 in range(0, HALF, 512):
                qw2 = min(512, HALF - q0)
                self.mm(x.PS_A[0:1, q0:q0 + qw2], x.csb["halfones"][:],
                        x.nKT2[:, q0:q0 + qw2], True, True,
                        inc=self.inc("prep_pe") if q0 + qw2 >= HALF else None)
            self.wait("vector", "prep_pe", self.cnt("prep_pe"))
            ev_row = x.nskrow[0:1, :].rearrange("p (g u s) -> p g u s", g=NBLK, u=2)
            self.ts(ev_row[:, :, 0, :], x.PS_A[0:1, 0:HALF], 1.0 / 16.0, ALU.mult,
                    inc=self.inc("prep_dve"))
            self.wait("tensor", "prep_dve", self.cnt("prep_dve"))
            for q0 in range(0, HALF, 512):
                qw2 = min(512, HALF - q0)
                self.mm(x.PS_A[0:1, q0:q0 + qw2], x.csb["otherhalf"][:],
                        x.nKT2[:, q0:q0 + qw2], True, True,
                        inc=self.inc("prep_pe") if q0 + qw2 >= HALF else None)
            self.wait("vector", "prep_pe", self.cnt("prep_pe"))
            self.ts(ev_row[:, :, 1, :], x.PS_A[0:1, 0:HALF], 1.0 / 16.0, ALU.mult,
                    inc=self.inc("prep_dve"))
            self.wait("tensor", "prep_dve", self.cnt("prep_dve"))
            for b in range(NBLK):
                self.transp(x.PS_A[:, b:b + 1], x.nskrow[0:1, 128 * b:128 * b + 128],
                            x.csb["id1"][:],
                            inc=self.inc("prep_pe") if b == NBLK - 1 else None)
            self.wait("vector", "prep_pe", self.cnt("prep_pe"))
            self.cp(x.nskcol, x.PS_A[:, 0:NBLK], inc=self.inc("prep_dve"))
            self.wait("vector", "prep_dve", self.cnt("prep_dve"))
            vsrc = x.VT[64 * hq:64 * hq + 64, :]
            ngrp = (NBLK + 7) // 8
            for g in range(ngrp):
                nb = min(8, NBLK - 8 * g)
                self.wait("tensor", "prep_dve", self.cnt("prep_dve"))
                for j in range(nb):
                    b = 8 * g + j
                    self.transp(x.PS_A[:, 64 * j:64 * j + 64],
                                vsrc[:, 128 * b:128 * b + 128],
                                x.csb["id128"][64 * hq:64 * hq + 64, 64 * hq:64 * hq + 64],
                                inc=self.inc("prep_pe") if j == nb - 1 else None)
                self.wait("vector", "prep_pe", self.cnt("prep_pe"))
                dst = x.vp[:, 65 * 8 * g:65 * (8 * g + nb)]
                dst = dst.rearrange("p (b w) -> p b w", b=nb)[:, :, 0:64]
                src = x.PS_A[:, 0:64 * nb].rearrange("p (b w) -> p b w", b=nb)
                self.cp(dst, src, inc=self.inc("prep_dve"))
                self.wait("vector", "prep_dve", self.cnt("prep_dve"))
            ones_ap = x.vp[:, :].rearrange("p (b w) -> p b w", b=NBLK)[:, :, 64:65]
            self.mset(ones_ap, 1.0, inc=self.inc("prep_dve"))
            PREP = self.cnt("prep_dve")
            if self.c.get("ASTOP") == "prep":
                return

            headblk = h * NBLK
            na2 = self.cnt("tra")
            nv2 = self.cnt("trv")
            fa0 = self.cnt("tfa")
            fv0 = self.cnt("tfv")
            ACT_S = c["ACT_S"]

            NDVE = 64 - len(c["ACT_S"])

            def emit_produce(eng, b, s, kp, Tring, s_rdy, s_free, f0):
                R = len(Tring)
                if kp == 0:
                    self.wait(eng, "prep_dve", PREP)
                elif kp >= R:
                    pk = kp - R
                    if eng == "vector" and pk % NDVE == NDVE - 1:
                        self.wait(eng, "evd", headblk + pk // NDVE + 1)
                    else:
                        if eng == "vector":
                            nrel = pk + 1 - pk // NDVE
                        else:
                            nrel = pk + 1
                        self.wait(eng, s_free, f0 + nrel)
                tile = Tring[kp % R]
                inc = self.inc(s_rdy)
                if eng == "scalar":
                    self.act(tile[:, 0:NQ], x.QT2[:], AF.Relu,
                             bias=x.nKT2[:, 64 * b + s:64 * b + s + 1], inc=inc)
                else:
                    self.ts(tile[:, 0:NQ], x.QT2[:],
                            x.nKT2[:, 64 * b + s:64 * b + s + 1],
                            ALU.add, 0.0, ALU.max, inc=inc)

            def emit_evac(b):
                gbi = headblk + b
                if b == 0:
                    self.wait("scalar", "prep_dve", PREP)
                    self.wait("scalar", "pad", h)
                else:
                    self.wait("scalar", "pvd", gbi - (h + 1) + 1)
                self.wait("scalar", "kdn", gbi + 1)
                self.act(x.kern[:, 0:NQ], x.PS_A[:, 0:NQ], AF.Exp, bias=x.nskcol[:, b:b + 1],
                         scale=-1.0 / 8.0, inc=self.inc("evd"))

            actS = sorted(ACT_S)
            dveS = [s for s in range(64) if s not in ACT_S]

            ai = 0
            for g in range(NBLK):
                for si, s in enumerate(actS):
                    if si == 1 and g >= 1:
                        emit_evac(g - 1)
                    emit_produce("scalar", g, s, ai, TA, "tra", "tfa", fa0)
                    ai += 1
            emit_evac(NBLK - 1)

            vi = 0
            for g in range(NBLK):
                for s in dveS:
                    emit_produce("vector", g, s, vi, TV, "trv", "tfv", fv0)
                    vi += 1

            pa = 0
            pv = 0
            for b in range(NBLK):
                gbi = headblk + b
                for s in range(64):
                    isact = s in ACT_S
                    ringp = TA if isact else TV
                    s_rdy = "tra" if isact else "trv"
                    kp = pa if isact else pv
                    base = na2 if isact else nv2
                    self.wait("tensor", s_rdy, base + kp + 1)
                    if s == 0:
                        if gbi > 0:
                            self.wait("tensor", "evd", gbi)
                        if b == 0:
                            self.wait("tensor", "prep_dve", PREP)
                    tile = ringp[kp % len(ringp)]
                    for ic in range(NIC):
                        inc = None
                        if ic == NIC - 1:
                            if s == 63:
                                inc = self.inc("kdn")
                            else:
                                inc = self.inc("tfa" if isact else "tfv")
                        self.mm(x.PS_A[:, 512 * ic:512 * ic + ICW],
                                x.csb["wbase"][:, 64 - s:192 - s],
                                tile[:, ICW * ic:ICW * ic + ICW],
                                start=(s == 0), stop=(s == 63), inc=inc)
                    if isact:
                        pa += 1
                    else:
                        pv += 1
                if b > 0:
                    self._emit_av(b - 1, h)
            self._emit_av(NBLK - 1, h)

            if self.c.get("ASTOP") == "dist":
                return
            b0 = 64 * hq
            rb_v = x.ov1[b0:b0 + 64, 0:NQ]
            lo_v = x.ov2[b0:b0 + 64, 0:NQ]
            e_v = x.ov2[b0:b0 + 64, x.OV2W:x.OV2W + 2]
            dwc_v = x.dwc_s[b0:b0 + 64, :]
            dwcb_v = x.dwcb_s[b0:b0 + 64, :]
            mlr_v = x.mlrb[b0:b0 + 64, :]
            dst = x.AR[b0:b0 + 64, NQ * hp:NQ * hp + NQ]
            self.wait("vector", "pad", h + 1)
            self.vo("tt", x.epsp, x.PS_B[64:65, 0:NQ], x.epsp, ALU.add)
            self.recip(x.recrow, x.epsp, inc=self.inc("p3d"))
            self.wait("gpsimd", "p3d", self.cnt("p3d"))
            self.wait("gpsimd", "p3b", self.cnt("p3b"))
            if h == 0:
                self.wait("gpsimd", "bc0", self.cnt("bc0"))
            self.dma("gpsimd", x.rowscr[2:3, 0:NQ], x.recrow, "p3b")
            self.wait("gpsimd", "p3b", self.cnt("p3b"))
            self.dma("gpsimd", rb_v, x.rowscr[2:3, 0:NQ].broadcast_to((64, NQ)), "p3b")
            if self.c.get("ASTOP") == "p3rec":
                return
            self.wait("vector", "p3b", self.cnt("p3b"))
            self.tt(dst, x.PS_B[0:64, 0:NQ], rb_v, ALU.mult, inc=self.inc("pbf"))
            self.wait("vector", "pbf", self.cnt("pbf"))
            if self.c.get("ASTOP") == "p3div":
                return
            v64 = x.VT[:][b0:b0 + 64, :]
            self.vo("ts", lo_v, v64[:, 0:NQ], dwc_v[:, 1:2], ALU.mult)
            self.vo("stt", lo_v[:, 1:NQ], v64[:, 0:NQ - 1], dwc_v[:, 0:1],
                    lo_v[:, 1:NQ], ALU.mult, ALU.add)
            self.vo("stt", lo_v[:, 0:NQ - 1], v64[:, 1:NQ], dwc_v[:, 2:3],
                    lo_v[:, 0:NQ - 1], ALU.mult, ALU.add)
            if self.c.get("ASTOP") == "p3conv":
                return
            self.vo("ts", e_v[:, 0:1], v64[:, S - 1:S], mlr_v[:, 0:1], ALU.mult)
            self.vo("stt", lo_v[:, 0:1], e_v[:, 0:1], dwc_v[:, 0:1],
                    lo_v[:, 0:1], ALU.mult, ALU.add)
            self.vo("ts", e_v[:, 1:2], v64[:, NQ:NQ + 1], mlr_v[:, 1:2], ALU.mult)
            self.vo("stt", lo_v[:, NQ - 1:NQ], e_v[:, 1:2], dwc_v[:, 2:3],
                    lo_v[:, NQ - 1:NQ], ALU.mult, ALU.add)
            if self.c.get("ASTOP") == "p3edge":
                return
            self.vo("ts", lo_v, lo_v, dwcb_v, ALU.add)
            if self.c.get("ASTOP") == "p3bias":
                return
            v_ = self.c.get("P3VAR", 0)
            if v_ == 1:
                self.tt(dst, dst, lo_v, ALU.add, inc=None)
                self.bump("bgo")
                self.release("vector", "bgo")
            elif v_ == 2:
                self.tt(x.gs_sb[:, 0:NQ] if NQ <= 2 * 512 else lo_v, dst, lo_v,
                        ALU.add, inc=self.inc("bgo"))
                self.wait("vector", "bgo", self.cnt("bgo"))
            else:
                self.tt(dst, dst, lo_v, ALU.add, inc=self.inc("bgo"))
                self.wait("vector", "bgo", self.cnt("bgo"))

    def _emit_av(self, b, h):
        x, c = self.x, self.c
        NQ, ICW, NIC, NBLK = c["NQ"], c["ICW"], c["NIC"], c["NBLK"]
        gbi = h * NBLK + b
        self.wait("tensor", "evd", gbi + 1)
        if b == 0:
            self.wait("tensor", "pbf", h)
        for ic in range(NIC):
            inc = None
            if ic == NIC - 1:
                inc = self.inc("pad") if b == NBLK - 1 else self.inc("pvd")
            self.mm(x.PS_B[0:65, 512 * ic:512 * ic + ICW], x.vp[:, 65 * b:65 * b + 65],
                    x.kern[:, ICW * ic:ICW * ic + ICW],
                    start=(b == 0), stop=(b == NBLK - 1), inc=inc)

    def _phase_b(self):
        x, c = self.x, self.c
        DM, NQ, NDC, QCW, NQC, E = c["DM"], c["NQ"], c["NDC"], c["QCW"], c["NQC"], c["E"]
        TCW = c["TCW"]
        RR = 3
        BGO = self.cnt("bgo")
        for e in ("tensor", "vector", "scalar", "gpsimd", "sync"):
            self.wait(e, "bgo", BGO)
        for ci in range(NDC):
            self.dma("gpsimd", x.gw_sb[:, E * ci:E * ci + E],
                     x.gatewT[128 * ci:128 * ci + 128, :], "bdw")
        GWD = self.cnt("bdw")
        slotV = lambda oc: x.VT[:][:, QCW * oc:QCW * oc + QCW]

        for tc in range(NQC):
            self.wait("tensor", "cwt", self.cnt("cwt"))
            self.wait("vector", "x2w", self.cnt("x2w"))
            for oc in range(NDC):
                self.wait("gpsimd", "bev", self.cnt("bev"))
                self.wait("gpsimd", "bdw", self.cnt("bdw"))
                for ci in range(NDC):
                    self.dma("gpsimd", x.wbuf[:, 128 * ci:128 * ci + 128],
                             x.wprojT[128 * ci:128 * ci + 128, 128 * oc:128 * oc + 128],
                             "bdw")
                self.dma("gpsimd", x.bpj, x.bprojD[128 * oc:128 * oc + 128, :], "bdw")
                self.wait("tensor", "bdw", self.cnt("bdw"))
                slot = oc % 2
                key = ("b1", slot)
                if key in self.slotuse:
                    self.wait("tensor", "bev", self.slotuse[key])
                for ci in range(NDC):
                    self.mm(x.PS_A[:, 512 * slot:512 * slot + QCW],
                            x.wbuf[:, 128 * ci:128 * ci + 128],
                            x.AR[:, ci * NQ + QCW * tc:ci * NQ + QCW * tc + QCW],
                            start=(ci == 0), stop=(ci == NDC - 1),
                            inc=self.inc("bmm") if ci == NDC - 1 else None)
                self.wait("vector", "bmm", self.cnt("bmm"))
                self.ts(slotV(oc), x.PS_A[:, 512 * slot:512 * slot + QCW], x.bpj[:],
                        ALU.add, inc=self.inc("bev"))
                self.slotuse[("b1", slot)] = self.cnt("bev")
                self.wait("vector", "bev", self.cnt("bev"))
            if self.c.get("BSTOP") == "proj":
                return
            PREV = self.cnt("bev")
            self.wait("tensor", "bev", PREV)
            self.wait("scalar", "bev", PREV)
            for oc in range(NDC):
                kk = tc * NDC + oc
                if oc == 0:
                    self.wait("tensor", "st0", self.cnt("st0"))
                self.mm(x.PS_B[0:1, 0:QCW], x.csb["ones_c128"][:], slotV(oc),
                        start=(oc == 0), stop=(oc == NDC - 1),
                        inc=self.inc("bst") if oc == NDC - 1 else None)
                if kk >= 2:
                    self.wait("scalar", "sqf", self.cnt("sqf") - 1)
                self.act(x.sqb[:, (oc % 2) * TCW:(oc % 2) * TCW + QCW], slotV(oc),
                         AF.Square, inc=self.inc("sqd"))
                self.wait("tensor", "sqd", self.cnt("sqd"))
                self.mm(x.PS_B[0:1, 512:512 + QCW], x.csb["ones_c128"][:],
                        x.sqb[:, (oc % 2) * TCW:(oc % 2) * TCW + QCW],
                        start=(oc == 0), stop=(oc == NDC - 1), inc=self.inc("sqf"))
            if self.c.get("BSTOP") == "ln2stats":
                return
            self._ln_rows(x.PS_B, x.rows, QCW, srhs=("bst", self.cnt("bst")),
                          ssq=("sqf", self.cnt("sqf")))
            if self.c.get("BSTOP") == "ln2rows":
                return
            self.wait("gpsimd", "st0", self.cnt("st0"))
            self.wait("gpsimd", "bbc", self.cnt("bbc"))
            self.dma("gpsimd", x.rowscr[3:4, 0:QCW], x.rows[0:1, 1024:1024 + QCW], "bbc")
            self.dma("gpsimd", x.rowscr[4:5, 0:QCW], x.rows[0:1, 1536:1536 + QCW], "bbc")
            self.wait("gpsimd", "bbc", self.cnt("bbc"))
            self.dma("gpsimd", x.RBt[:, 0:QCW],
                     x.rowscr[3:4, 0:QCW].broadcast_to((128, QCW)), "bbc")
            self.dma("gpsimd", x.MBt[:, 0:QCW],
                     x.rowscr[4:5, 0:QCW].broadcast_to((128, QCW)), "bbc")
            BBC = self.cnt("bbc")
            self.wait("vector", "lnxh0", self.cnt("lnxh0"))
            for oc in range(NDC):
                kx = tc * NDC + oc
                self.wait("sync", "bev", self.cnt("bev"))
                slotx = "qmm%d" % (kx % RR)
                self.dma("sync", x.rr[kx % RR][:, 0:QCW],
                         x.xbT[128 * oc:128 * oc + 128, QCW * tc:QCW * tc + QCW], slotx)
                self.wait("vector", slotx, self.cnt(slotx))
                self.wait("vector", "bbc", BBC)
                self.vo("tt", x.gs_sb[:, 0:QCW], slotV(oc), x.RBt[:, 0:QCW], ALU.mult)
                self.vo("tt", x.gs_sb[:, 0:QCW], x.gs_sb[:, 0:QCW], x.MBt[:, 0:QCW],
                        ALU.add)
                self.tt(slotV(oc), x.gs_sb[:, 0:QCW], x.rr[kx % RR][:, 0:QCW], ALU.add,
                        inc=self.inc("bev"))
                self.wait("vector", "bev", self.cnt("bev"))
                self.wait("gpsimd", "bev", self.cnt("bev"))
                self.dma("gpsimd", x.x2d[128 * oc:128 * oc + 128, QCW * tc:QCW * tc + QCW],
                         slotV(oc), "x2w")
                self.wait("tensor", "bev", self.cnt("bev"))
                if oc == 0:
                    self.wait("tensor", "st0", self.cnt("st0"))
                self.mm(x.PS_B[0:1, 1024:1024 + QCW], x.csb["ones_c128"][:], slotV(oc),
                        start=(oc == 0), stop=(oc == NDC - 1),
                        inc=self.inc("bst") if oc == NDC - 1 else None)
                self.wait("scalar", "bev", self.cnt("bev"))
                kk = tc * NDC + oc
                if kk >= 2:
                    self.wait("scalar", "sqf", self.cnt("sqf") - 1)
                self.act(x.sqb[:, (oc % 2) * TCW:(oc % 2) * TCW + QCW], slotV(oc),
                         AF.Square, inc=self.inc("sqd"))
                self.wait("tensor", "sqd", self.cnt("sqd"))
                self.mm(x.PS_B[0:1, 1536:1536 + QCW], x.csb["ones_c128"][:],
                        x.sqb[:, (oc % 2) * TCW:(oc % 2) * TCW + QCW],
                        start=(oc == 0), stop=(oc == NDC - 1), inc=self.inc("sqf"))
            if self.c.get("BSTOP") == "apply":
                return
            self._ln_rows(x.PS_B[:, 1024:2048], x.rows, QCW,
                          srhs=("bst", self.cnt("bst")), ssq=("sqf", self.cnt("sqf")))
            self.wait("gpsimd", "st0", self.cnt("st0"))
            self.wait("gpsimd", "bbc", self.cnt("bbc"))
            self.dma("gpsimd", x.rowscr[5:6, 0:QCW], x.rows[0:1, 1024:1024 + QCW], "bbc")
            self.dma("gpsimd", x.rowscr[6:7, 0:QCW], x.rows[0:1, 1536:1536 + QCW], "bbc")
            self.wait("gpsimd", "bbc", self.cnt("bbc"))
            self.dma("gpsimd", x.RBt[:, 0:QCW],
                     x.rowscr[5:6, 0:QCW].broadcast_to((128, QCW)), "bbc")
            self.dma("gpsimd", x.MBt[:, 0:QCW],
                     x.rowscr[6:7, 0:QCW].broadcast_to((128, QCW)), "bbc")
            BBC3 = self.cnt("bbc")
            if tc == 0:
                self.wait("tensor", "bdw", GWD)
            for oc in range(NDC):
                slot = oc % 2
                self.wait("vector", "bbc", BBC3)
                if tc * NDC + oc >= 2:
                    self.wait("vector", "lnxh%d" % slot, self.cnt("lnxh%d" % slot))
                self.vo("tt", x.gs_sb[:, slot * TCW:slot * TCW + QCW], slotV(oc),
                        x.RBt[:, 0:QCW], ALU.mult)
                self.tt(x.gs_sb[:, slot * TCW:slot * TCW + QCW],
                        x.gs_sb[:, slot * TCW:slot * TCW + QCW], x.MBt[:, 0:QCW],
                        ALU.add, inc=self.inc("bev"))
                self.wait("vector", "bev", self.cnt("bev"))
                self.wait("gpsimd", "bev", self.cnt("bev"))
                self.dma("gpsimd", x.ln3d[128 * oc:128 * oc + 128, QCW * tc:QCW * tc + QCW],
                         x.gs_sb[:, slot * TCW:slot * TCW + QCW], "lnxh%d" % slot)
                self.wait("tensor", "bev", self.cnt("bev"))
                for gt in range(QCW // 128):
                    self.mm(x.PS_A[:, 512 * gt:512 * gt + E],
                            x.gs_sb[:, slot * TCW + 128 * gt:slot * TCW + 128 * gt + 128],
                            x.gw_sb[:, E * oc:E * oc + E],
                            start=(oc == 0), stop=(oc == NDC - 1),
                            inc=self.inc("bmm")
                            if (oc == NDC - 1 and gt == QCW // 128 - 1) else None)
            if self.c.get("BSTOP") == "ln3apply":
                return
            self.wait("vector", "bmm", self.cnt("bmm"))
            for gt in range(QCW // 128):
                g_all = tc * (QCW // 128) + gt
                self._softmax_top2(x.PS_A[:, 512 * gt:512 * gt + E],
                                   x.cwtm[:, E * g_all:E * g_all + E])
            if self.c.get("BSTOP") == "softmax":
                return
        NG = NQ // 128
        self.wait("tensor", "cwt", self.cnt("cwt"))
        for e in range(E):
            if e >= 1:
                self.wait("tensor", "cwt", self.cnt("cwt"))
            for g in range(NG):
                self.mm(x.PS_B[0:1, 128 * g:128 * g + 128],
                        x.cwtm[:, E * g + e:E * g + e + 1],
                        x.csb["id128"][:], True, True,
                        inc=self.inc("bmm") if g == NG - 1 else None)
            if self.c.get("BSTOP") == "cw0t" and e == 0:
                return
            self.wait("vector", "bmm", self.cnt("bmm"))
            if e >= 1:
                self.wait("vector", "cwr", self.cnt("cwr"))
            self.cp(x.ov2[0:1, 0:NQ], x.PS_B[0:1, 0:NQ], inc=self.inc("cwt"))
            self.wait("vector", "cwt", self.cnt("cwt"))
            if self.c.get("BSTOP") == "cw0c" and e == 0:
                return
            self.wait("gpsimd", "cwt", self.cnt("cwt"))
            self.dma("gpsimd", x.cwd[e:e + 1, :], x.ov2[0:1, 0:NQ], "cwr")
            if self.c.get("BSTOP") == "cw0d" and e == 0:
                return
    
    def _softmax_top2(self, ps, cw_out):
        E = self.c["E"]
        x = self.x
        s = x.scx
        self.vo("red", s[:, 0:1], ps, ALU.max)
        self.vo("ts", s[:, 1:1 + E], ps, s[:, 0:1], ALU.subtract)
        self.wait("scalar", "vch", self.cnt("vch"))
        self.ao(s[:, 1 + E:1 + 2 * E], s[:, 1:1 + E], AF.Exp)
        self.wait("vector", "ach", self.cnt("ach"))
        self.vo("red", s[:, 1 + 2 * E:2 + 2 * E], s[:, 1 + E:1 + 2 * E], ALU.add)
        self.vo("recip", s[:, 2 + 2 * E:3 + 2 * E], s[:, 1 + 2 * E:2 + 2 * E])
        sm = s[:, 3 + 2 * E:3 + 3 * E]
        self.vo("ts", sm, s[:, 1 + E:1 + 2 * E], s[:, 2 + 2 * E:3 + 2 * E], ALU.mult)
        self.vo("red", s[:, 0:1], sm, ALU.max)
        eq = s[:, 3 + 3 * E:3 + 4 * E]
        self.vo("ts", eq, sm, s[:, 0:1], ALU.is_ge)
        msk = s[:, 3 + 4 * E:3 + 5 * E]
        self.vo("ts", msk, eq, -1e30, ALU.mult)
        self.vo("tt", msk, msk, sm, ALU.add)
        self.vo("red", s[:, 1:2], msk, ALU.max)
        ge = s[:, 3 + 5 * E:3 + 6 * E]
        self.vo("ts", ge, sm, s[:, 1:2], ALU.is_ge)
        self.tt(cw_out, sm, ge, ALU.mult, inc=self.inc("cwt"))
        self.wait("vector", "cwt", self.cnt("cwt"))

    def _phase_c(self):
        x, c = self.x, self.c
        DM, NQ, NDC, QCW, NQC = c["DM"], c["NQ"], c["NDC"], c["QCW"], c["NQC"]
        INTER, E, NICH, TCW = c["INTER"], c["E"], c["NICH"], c["TCW"]
        RR = 3
        CWB = x.QT2
        for eng in ("tensor", "vector", "scalar", "gpsimd", "sync"):
            self.wait(eng, "cwr", 16 * E)
        for eng in ("vector", "tensor", "sync", "gpsimd"):
            self.wait(eng, "x2w", self.cnt("x2w"))
            self.wait(eng, "lnxh0", self.cnt("lnxh0"))
            self.wait(eng, "lnxh1", self.cnt("lnxh1"))

        def wg_tile(ci, ic):
            f = (ci * NICH + ic) * 128
            return x.VT[:][:, f:f + 128]

        VP_CAP = (max(c["NBLK"] * 65, NQ) // 128) * 128
        assert NICH * NDC * 128 <= VP_CAP + 2 * DM, "wu arena too small"

        def wu_tile(ci, ic):
            f = (ci * NICH + ic) * 128
            if f < VP_CAP:
                return x.vp[:][:, f:f + 128]
            return x.wbuf[:, f - VP_CAP:f - VP_CAP + 128]

        def wd_tile(half, ic):
            col = (half * NICH + ic) * 128
            return x.nKT2[:][:, col:col + 128]

        def hp_tile(ic, w):
            return x.kern[:, 512 * ic:512 * ic + w]

        for ei in range(-1, E):
            shared = ei < 0
            wgS = x.sgTD if shared else x.wgTD[ei]
            wuS = x.suTD if shared else x.wuTD[ei]
            wdS = x.sdTD if shared else x.wdTD[ei]
            self.wait("gpsimd", "frhs", self.cnt("frhs"))
            self.wait("gpsimd", "fw", self.cnt("fw"))
            for ci in range(NDC):
                for ic in range(NICH):
                    self.dma("gpsimd", wg_tile(ci, ic),
                             wgS[128 * ci:128 * ci + 128, 128 * ic:128 * ic + 128], "fw")
                    self.dma("gpsimd", wu_tile(ci, ic),
                             wuS[128 * ci:128 * ci + 128, 128 * ic:128 * ic + 128], "fw")
            FW = self.cnt("fw")
            if not shared:
                self.wait("gpsimd", "fhp", self.cnt("fhp"))
                self.wait("gpsimd", "l4", self.cnt("l4"))
                self.dma("gpsimd", CWB[:][:, 0:NQ],
                         x.cwd[ei:ei + 1, :].broadcast_to((128, NQ)), "l4")
            L4C = self.cnt("l4")

            self.wait("tensor", "fw", FW)
            FPD0 = self.cnt("fpd")
            for tc in range(NQC):
                for ci in range(NDC):
                    k = ((ei + 1) * NQC + tc) * NDC + ci
                    if k >= RR:
                        self.wait("sync", "frhs", k - RR + 1)
                    slot = "fmm%d" % (k % RR)
                    self.dma("sync", x.rr[k % RR][:, 0:QCW],
                             x.ln3d[128 * ci:128 * ci + 128, QCW * tc:QCW * tc + QCW],
                             slot)
                    self.wait("tensor", slot, self.cnt(slot))
                    if ci == 0:
                        self.wait("tensor", "fyw", self.cnt("fyw"))
                        self.wait("tensor", "fhp", self.cnt("fhp"))
                    for ic in range(NICH):
                        self.mm(x.PS_A[:, 512 * ic:512 * ic + QCW],
                                wg_tile(ci, ic), x.rr[k % RR][:, 0:QCW],
                                start=(ci == 0), stop=(ci == NDC - 1), inc=None)
                        inc = self.inc("frhs") if ic == NICH - 1 else None
                        self.mm(x.PS_B[:, 512 * ic:512 * ic + QCW],
                                wu_tile(ci, ic), x.rr[k % RR][:, 0:QCW],
                                start=(ci == 0), stop=(ci == NDC - 1), inc=inc)
                self.wait("scalar", "frhs", self.cnt("frhs"))
                self.wait("scalar", "l4", L4C)
                self.wait("vector", "l4", L4C)
                self.wait("vector", "fpd", self.cnt("fpd"))
                for ic in range(NICH):
                    self.wait("scalar", "vch", self.cnt("vch"))
                    self.ao(x.gs_sb[:, 0:QCW], x.PS_A[:, 512 * ic:512 * ic + QCW],
                            AF.Silu)
                    self.wait("vector", "ach", self.cnt("ach"))
                    self.vo("tt", x.gs_sb[:, TCW:TCW + QCW], x.gs_sb[:, 0:QCW],
                            x.PS_B[:, 512 * ic:512 * ic + QCW], ALU.mult)
                    if shared:
                        self.cp(hp_tile(ic, QCW), x.gs_sb[:, TCW:TCW + QCW],
                                inc=self.inc("fhp"))
                    else:
                        self.tt(hp_tile(ic, QCW), x.gs_sb[:, TCW:TCW + QCW],
                                CWB[:][:, QCW * tc:QCW * tc + QCW], ALU.mult,
                                inc=self.inc("fhp"))
                    self.wait("vector", "fhp", self.cnt("fhp"))
                FHPTC = self.cnt("fhp")
                self.wait("tensor", "fhp", FHPTC)
                for dmc in range(NDC):
                    half = dmc % 2
                    if ("wd", half) in self.slotuse:
                        self.wait("gpsimd", "fpd", self.slotuse[("wd", half)])
                    self.wait("gpsimd", "fw", self.cnt("fw"))
                    for ic in range(NICH):
                        self.dma("gpsimd", wd_tile(half, ic),
                                 wdS[128 * ic:128 * ic + 128, 128 * dmc:128 * dmc + 128],
                                 "fw")
                    FWD = self.cnt("fw")
                    self.wait("tensor", "fw", FWD)
                    yslot = dmc % 2
                    key = ("dn", yslot)
                    if key in self.slotuse:
                        self.wait("tensor", "fyw", self.slotuse[key])
                    for ic in range(NICH):
                        inc = self.inc("fpd") if ic == NICH - 1 else None
                        self.mm(x.PS_A[:, 512 * yslot:512 * yslot + QCW],
                                wd_tile(half, ic), hp_tile(ic, QCW),
                                start=(ic == 0), stop=(ic == NICH - 1), inc=inc)
                    self.slotuse[("wd", half)] = self.cnt("fpd")
                    self.wait("vector", "fpd", self.cnt("fpd"))
                    dstc = x.AR[:, dmc * NQ + QCW * tc:dmc * NQ + QCW * tc + QCW]
                    if shared:
                        self.cp(dstc, x.PS_A[:, 512 * yslot:512 * yslot + QCW],
                                inc=self.inc("fyw"))
                    else:
                        self.tt(dstc, dstc, x.PS_A[:, 512 * yslot:512 * yslot + QCW],
                                ALU.add, inc=self.inc("fyw"))
                    self.slotuse[("dn", yslot)] = self.cnt("fyw")
                    self.wait("vector", "fyw", self.cnt("fyw"))

        FYW = self.cnt("fyw")
        self.wait("tensor", "fyw", FYW)
        self.wait("scalar", "fyw", FYW)
        out_dma = []
        for tc in range(NQC):
            for oc in range(NDC):
                kk = tc * NDC + oc
                if oc == 0:
                    self.wait("tensor", "st0", self.cnt("st0"))
                self.mm(x.PS_B[0:1, 0:QCW], x.csb["ones_c128"][:],
                        x.AR[:, oc * NQ + QCW * tc:oc * NQ + QCW * tc + QCW],
                        start=(oc == 0), stop=(oc == NDC - 1),
                        inc=self.inc("bst") if oc == NDC - 1 else None)
                if kk >= 2:
                    self.wait("scalar", "sqf", self.cnt("sqf") - 1)
                self.act(x.sqb[:, (oc % 2) * TCW:(oc % 2) * TCW + QCW],
                         x.AR[:, oc * NQ + QCW * tc:oc * NQ + QCW * tc + QCW], AF.Square,
                         inc=self.inc("sqd"))
                self.wait("tensor", "sqd", self.cnt("sqd"))
                self.mm(x.PS_B[0:1, 512:512 + QCW], x.csb["ones_c128"][:],
                        x.sqb[:, (oc % 2) * TCW:(oc % 2) * TCW + QCW],
                        start=(oc == 0), stop=(oc == NDC - 1), inc=self.inc("sqf"))
            self._ln_rows(x.PS_B, x.rows, QCW, srhs=("bst", self.cnt("bst")),
                          ssq=("sqf", self.cnt("sqf")))
            self.wait("gpsimd", "st0", self.cnt("st0"))
            self.wait("gpsimd", "bbc", self.cnt("bbc"))
            self.dma("gpsimd", x.rowscr[0:1, 0:QCW], x.rows[0:1, 1024:1024 + QCW], "bbc")
            self.dma("gpsimd", x.rowscr[1:2, 0:QCW], x.rows[0:1, 1536:1536 + QCW], "bbc")
            self.wait("gpsimd", "bbc", self.cnt("bbc"))
            self.dma("gpsimd", x.RBt[:, 0:QCW],
                     x.rowscr[0:1, 0:QCW].broadcast_to((128, QCW)), "bbc")
            self.dma("gpsimd", x.MBt[:, 0:QCW],
                     x.rowscr[1:2, 0:QCW].broadcast_to((128, QCW)), "bbc")
            BBC = self.cnt("bbc")
            for oc in range(NDC):
                kx = tc * NDC + oc
                self.wait("sync", "frhs", self.cnt("frhs"))
                slotx = "fmm%d" % (kx % RR)
                self.dma("sync", x.rr[kx % RR][:, 0:QCW],
                         x.x2d[128 * oc:128 * oc + 128, QCW * tc:QCW * tc + QCW], slotx)
                self.wait("vector", slotx, self.cnt(slotx))
                self.wait("vector", "bbc", BBC)
                if out_dma:
                    self.wait("vector", "outw", out_dma[-1])
                arc = x.AR[:, oc * NQ + QCW * tc:oc * NQ + QCW * tc + QCW]
                self.vo("tt", x.gs_sb[:, 0:QCW], arc, x.RBt[:, 0:QCW], ALU.mult)
                self.vo("tt", x.gs_sb[:, 0:QCW], x.gs_sb[:, 0:QCW], x.MBt[:, 0:QCW],
                        ALU.add)
                self.tt(x.gs_sb[:, TCW:TCW + QCW],
                        x.gs_sb[:, 0:QCW], x.rr[kx % RR][:, 0:QCW], ALU.add,
                        inc=self.inc("frhs"))
                self.wait("vector", "frhs", self.cnt("frhs"))
                self.wait("gpsimd", "frhs", self.cnt("frhs"))
                self.dma("gpsimd", x.outD[128 * oc:128 * oc + 128, QCW * tc:QCW * tc + QCW],
                         x.gs_sb[:, TCW:TCW + QCW], "outw")
                out_dma.append(self.cnt("outw"))
        self.wait("gpsimd", "outw", self.cnt("outw"))



def stage_core_inputs(cfg, core, full):
    S, DM = cfg["S"], cfg["DM"]
    b, half = core // 2, core % 2
    x = np.asarray(full["L0_attn_x"][b], dtype=np.float32)
    NQ = S // 2
    xr = x if half == 0 else np.concatenate([x[NQ:], x[:NQ]], axis=0)
    d = {
        "xbT": np.ascontiguousarray(xr.T),
        "wqkvT": np.ascontiguousarray(np.asarray(full["L0_attn_qkv_w"], np.float32).T),
        "bqkv": np.ascontiguousarray(np.asarray(full["L0_attn_qkv_b"], np.float32).reshape(-1, 1)),
        "wprojT": np.ascontiguousarray(np.asarray(full["L0_attn_proj_w"], np.float32).T),
        "bproj": np.ascontiguousarray(np.asarray(full["L0_attn_proj_b"], np.float32).reshape(-1, 1)),
        "dwcw": np.ascontiguousarray(np.asarray(full["L0_attn_dwc_w"], np.float32).reshape(DM, 3)),
        "dwcb": np.ascontiguousarray(np.asarray(full["L0_attn_dwc_b"], np.float32).reshape(-1, 1)),
        "gatewT": np.ascontiguousarray(np.asarray(full["L0_ffn_gate_w"], np.float32).T),
        "wgT": np.ascontiguousarray(np.asarray(full["L0_ffn_wg"], np.float32).transpose(0, 2, 1)),
        "wuT": np.ascontiguousarray(np.asarray(full["L0_ffn_wu"], np.float32).transpose(0, 2, 1)),
        "wdT": np.ascontiguousarray(np.asarray(full["L0_ffn_wd"], np.float32).transpose(0, 2, 1)),
        "sgT": np.ascontiguousarray(np.asarray(full["L0_ffn_sg"], np.float32).T),
        "suT": np.ascontiguousarray(np.asarray(full["L0_ffn_su"], np.float32).T),
        "sdT": np.ascontiguousarray(np.asarray(full["L0_ffn_sd"], np.float32).T),
        "masks": np.array([[1.0 if half == 1 else 0.0,
                            1.0 if half == 0 else 0.0]], dtype=np.float32),
    }
    d.update(host_consts(cfg["LN_EPS"]))
    return d


_PROG_CACHE = {}


def kernel(**inputs):
    cfg = FULL_CFG
    if "full" not in _PROG_CACHE:
        _PROG_CACHE["full"] = Builder(cfg).build()
    nc = _PROG_CACHE["full"]
    in_maps = [stage_core_inputs(cfg, c, inputs) for c in range(8)]
    res = run_bass_kernel_spmd(nc, in_maps, core_ids=list(range(8)))
    S, DM = cfg["S"], cfg["DM"]
    NQ = S // 2
    out = np.empty((4, S, DM), dtype=np.float32)
    for c in range(8):
        b, half = c // 2, c % 2
        o = np.asarray(res.results[c]["out"])
        if half == 0:
            out[b, 0:NQ, :] = o.T
        else:
            out[b, NQ:S, :] = o.T
    return out


# version 10
# speedup vs baseline: 2.1391x; 1.0977x over previous
import math
from contextlib import ExitStack

import ml_dtypes
import numpy as np
import concourse.bass as bass
import concourse.mybir as mybir
from concourse.bass_utils import run_bass_kernel_spmd

F32 = mybir.dt.float32
F32R = mybir.dt.float32r
BF16 = mybir.dt.bfloat16
AF = mybir.ActivationFunctionType
ALU = mybir.AluOpType

LOG_EPS = float(np.log(1e-6))

FULL_CFG = dict(S=4096, DM=1024, H=16, DH=64, INTER=512, E=8,
                LN_EPS=1e-5, ACT_FRAC=0.375)

CONST_SHAPES = {
    "wbase": ([128, 192], BF16),
    "id128": ([128, 128], F32),
    "id1": ([1, 1], F32),
    "ones_c128": ([128, 1], F32),
    "halfones": ([128, 1], F32),
    "otherhalf": ([128, 1], F32),
    "logeps": ([1, 1], F32),
    "lneps": ([1, 1], F32),
}


def host_consts(ln_eps):
    wb = np.zeros((128, 192), dtype=np.float32)
    wb[0:64, 64] = 1.0
    wb[64:128, 128] = 1.0
    ho = np.zeros((128, 1), dtype=np.float32); ho[0:64] = 1.0
    oh = np.zeros((128, 1), dtype=np.float32); oh[64:128] = 1.0
    return {
        "wbase": wb.astype(ml_dtypes.bfloat16),
        "id128": np.eye(128, dtype=np.float32),
        "id1": np.ones((1, 1), dtype=np.float32),
        "ones_c128": np.ones((128, 1), dtype=np.float32),
        "halfones": ho,
        "otherhalf": oh,
        "logeps": np.full((1, 1), LOG_EPS, dtype=np.float32),
        "lneps": np.full((1, 1), ln_eps, dtype=np.float32),
    }


def pick_act_blocks(nblk, frac):
    n = int(round(frac * nblk))
    n = min(n, (nblk + 1) // 2)
    out = set()
    if n > 0:
        step = nblk / n
        prev = -2
        for i in range(n):
            b = min(nblk - 1, int(i * step))
            if b <= prev + 1:
                b = prev + 2
            if b >= nblk:
                break
            out.add(b)
            prev = b
    return out


class Builder:
    ENGINES = ("sync", "gpsimd", "tensor", "vector", "scalar")

    def __init__(self, cfg):
        c = dict(cfg)
        c["NQ"] = c["S"] // 2
        c["NBLK"] = c["S"] // 128
        c["NDC"] = c["DM"] // 128
        c["TCW"] = min(512, c["S"])
        c["NTC"] = c["S"] // c["TCW"]
        c["QCW"] = min(512, c["NQ"])
        c["NQC"] = max(1, c["NQ"] // c["QCW"])
        c["ICW"] = min(512, c["NQ"])
        c["NIC"] = max(1, c["NQ"] // c["ICW"])
        c["NICH"] = c["INTER"] // 128
        c["HP"] = c["H"] // 2
        c["ACT_B"] = pick_act_blocks(c["NBLK"], c["ACT_FRAC"])
        c["ACT_S"] = frozenset(s for s in range(64) if s % 8 in (1, 4, 6))
        self.c = c
        self.ops = {e: [] for e in self.ENGINES}
        self.ct = {}
        self.sem = {}
        self.pe_carry = []
        self.slotuse = {}

    def emit(self, eng, fn):
        self.ops[eng].append(fn)

    def wait(self, eng, s, v):
        if v > 0:
            self.emit(eng, lambda q, s=s, v=v: q.wait_ge(self.sem[s], v))

    def bump(self, s, n=1):
        self.ct[s] = self.ct.get(s, 0) + n
        return self.ct[s]

    def cnt(self, s):
        return self.ct.get(s, 0)

    def op(self, eng, fn, inc=None, carry_ok=False):
        if inc is None and carry_ok and self.pe_carry:
            inc = self.pe_carry.pop(0)
        if inc is not None:
            s, n = inc
            self.emit(eng, lambda q, fn=fn, s=s, n=n: fn(q).then_inc(self.sem[s], n))
        else:
            self.emit(eng, lambda q, fn=fn: fn(q))

    def inc(self, s, n=1):
        self.bump(s, n)
        return (s, n)

    def carry(self, s, n=1):
        self.bump(s, n)
        self.pe_carry.append((s, n))

    def release(self, eng, s, n=1):
        self.bump(s, n)
        self.emit(eng, lambda q, s=s, n=n: q.sem_inc(self.sem[s], n))

    def dma(self, eng, dst, src, s):
        v = self.bump(s, 16)
        self.emit(eng, lambda q, dst=dst, src=src, s=s: q.dma_start(out=dst, in_=src)
                  .then_inc(self.sem[s], 16))
        return v

    def mm(self, out, lhsT, rhs, start, stop, inc=None):
        self.op("tensor", lambda t, o=out, l=lhsT, r=rhs, st=start, sp=stop:
                t.matmul(out=o, lhsT=l, rhs=r, start=st, stop=sp), inc=inc,
                carry_ok=(inc is None))

    def transp(self, out, in_, ident, inc=None):
        self.op("tensor", lambda t, o=out, i=in_, d=ident:
                t.transpose(out=o, in_=i, identity=d), inc=inc, carry_ok=(inc is None))

    def act(self, out, in_, func, bias=0.0, scale=1.0, inc=None):
        self.op("scalar", lambda a, o=out, i=in_, f=func, b=bias, s=scale:
                a.activation(out=o, in_=i, func=f, bias=b, scale=s), inc=inc)

    def ts(self, out, in0, s1, op0, s2=None, op1=None, inc=None):
        if s2 is None:
            self.op("vector", lambda v, o=out, i=in0, a=s1, p=op0:
                    v.tensor_single_scalar(out=o, in_=i, scalar=a, op=p), inc=inc)
        else:
            self.op("vector", lambda v, o=out, i=in0, a=s1, b=s2, p=op0, p1=op1:
                    v.tensor_scalar(out=o, in0=i, scalar1=a, scalar2=b, op0=p, op1=p1),
                    inc=inc)

    def tt(self, out, in0, in1, alu, inc=None):
        self.op("vector", lambda v, o=out, a=in0, b=in1, p=alu:
                v.tensor_tensor(out=o, in0=a, in1=b, op=p), inc=inc)

    def stt(self, out, in0, scalar, in1, op0, op1, inc=None):
        self.op("vector", lambda v, o=out, a=in0, s=scalar, b=in1, p0=op0, p1=op1:
                v.scalar_tensor_tensor(out=o, in0=a, scalar=s, in1=b, op0=p0, op1=p1),
                inc=inc)

    def cp(self, out, in_, inc=None):
        self.op("vector", lambda v, o=out, i=in_: v.tensor_copy(out=o, in_=i), inc=inc)

    def mset(self, ap_, val, inc=None):
        self.op("vector", lambda v, a=ap_, x=val: v.memset(a, x), inc=inc)

    def red(self, out, in_, alu, inc=None):
        self.op("vector", lambda v, o=out, i=in_, p=alu:
                v.tensor_reduce(out=o, in_=i, axis=mybir.AxisListType.X, op=p), inc=inc)

    def recip(self, out, in_, inc=None):
        self.op("vector", lambda v, o=out, i=in_: v.reciprocal(out=o, in_=i), inc=inc)

    def vo(self, kind, *a, **kw):
        kw["inc"] = self.inc("vch")
        getattr(self, kind)(*a, **kw)
        self.wait("vector", "vch", self.cnt("vch"))

    def ao(self, *a, **kw):
        kw["inc"] = self.inc("ach")
        self.act(*a, **kw)
        self.wait("scalar", "ach", self.cnt("ach"))

    def build(self):
        c = self.c
        S, DM, H, DH, INTER, E = c["S"], c["DM"], c["H"], c["DH"], c["INTER"], c["E"]
        NQ, NBLK, NDC, TCW, NTC = c["NQ"], c["NBLK"], c["NDC"], c["TCW"], c["NTC"]
        QCW, NQC, ICW, NIC, NICH, HP = (c["QCW"], c["NQC"], c["ICW"], c["NIC"],
                                        c["NICH"], c["HP"])

        nc = bass.Bass("TRN2", target_bir_lowering=False, debug=False)
        self.nc = nc
        dram = {}

        def din(name, shape, dt_=F32):
            dram[name] = nc.dram_tensor(name, shape, dt_, kind="ExternalInput").ap()
            return dram[name]

        xbT = din("xbT", [DM, S])
        wqkvT = din("wqkvT", [DM, 3 * DM])
        bqkv = din("bqkv", [3 * DM, 1])
        wprojT = din("wprojT", [DM, DM])
        bprojD = din("bproj", [DM, 1])
        dwcwD = din("dwcw", [DM, 3])
        dwcbD = din("dwcb", [DM, 1])
        gatewT = din("gatewT", [DM, E])
        wgTD = din("wgT", [E, DM, INTER])
        wuTD = din("wuT", [E, DM, INTER])
        wdTD = din("wdT", [E, INTER, DM])
        sgTD = din("sgT", [DM, INTER])
        suTD = din("suT", [DM, INTER])
        sdTD = din("sdT", [INTER, DM])
        masksD = din("masks", [1, 2])
        for name, (shape, dt_) in CONST_SHAPES.items():
            din(name, shape, dt_)

        lnx1 = nc.dram_tensor("lnx1", [DM, S], F32, kind="Internal").ap()
        ln3d = nc.dram_tensor("ln3d", [DM, NQ], F32, kind="Internal").ap()
        x2d = nc.dram_tensor("x2d", [DM, NQ], F32, kind="Internal").ap()
        rowscr = nc.dram_tensor("rowscr", [8, max(S, 2048)], F32, kind="Internal").ap()
        cwd = nc.dram_tensor("cwd", [E, NQ], F32, kind="Internal").ap()
        outD = nc.dram_tensor("out", [DM, NQ], F32, kind="ExternalOutput").ap()

        es = ExitStack()
        with es:
            sbuf = lambda name, shape, dt_=F32: es.enter_context(
                nc.sbuf_tensor(name, shape, dt_))
            csb = {name: sbuf("c_" + name, shape, dt_)
                   for name, (shape, dt_) in CONST_SHAPES.items()}

            QT2 = sbuf("QT2", [128, NQ])
            nKT2 = sbuf("nKT2", [128, max(S // 2, 2 * NICH * 128)])
            VT = sbuf("VT", [128, max(S, NICH * NDC * 128, (c["DM"] // 128) * min(512, c["NQ"]))])
            vp = sbuf("vp", [128, max(NBLK * 65, NQ)])
            smallb = sbuf("smallb", [128, max(NBLK, 6 * c["E"] + 4) + 12])
            SMB = max(NBLK, 6 * c["E"] + 4)
            nskcol = smallb[:, 0:NBLK]
            mlrb = smallb[:, SMB + 8:SMB + 10]
            RN = 8
            ring = [sbuf(f"ring{i}", [128, NQ], BF16) for i in range(RN)]
            kern = sbuf("kern", [128, max(NQ, DM)])
            kern2 = sbuf("kern2", [128, NQ])
            AR = sbuf("AR", [128, NDC * NQ])
            wbuf = sbuf("wbuf", [128, 2 * DM])
            RR = 3
            rr = [sbuf(f"rr{i}", [128, TCW]) for i in range(RR)]
            bq2 = smallb[:, SMB:SMB + 1]
            bk2 = smallb[:, SMB + 1:SMB + 2]
            bv2 = smallb[:, SMB + 2:SMB + 3]
            bpj = smallb[:, SMB + 3:SMB + 4]
            dwc_s = smallb[:, SMB + 4:SMB + 7]
            dwcb_s = smallb[:, SMB + 7:SMB + 8]
            rowT_t = sbuf("rowT", [128, max(S, 4096)])
            nskrow = rowT_t[0:1, 0:S]
            rows = rowT_t[32:33, 0:4096]
            epsp = rowT_t[64:65, 0:NQ]
            recrow = rowT_t[64:65, 2048:2048 + NQ]
            ov1 = sbuf("ov1", [128, max(NQ, 2 * TCW)])
            RBt = ov1[:, 0:TCW]
            MBt = ov1[:, TCW:2 * TCW]
            OV2W = max(NQ, 4 * TCW)
            ov2 = sbuf("ov2", [128, OV2W + 2])
            sqb = ov2[:, 0:2 * TCW]
            gs_sb = ov2[:, 2 * TCW:4 * TCW]
            NG = NQ // 128
            cwtm = kern[:][:, NDC * E:NDC * E + NG * E]
            scx = smallb[:, :]

            PS_A = es.enter_context(nc.psum_tensor("PS_A", [128, 2048], F32))
            PS_B = es.enter_context(nc.psum_tensor("PS_B", [128, 2048], F32))

            semnames = ["cst", "x1rhs", "x2rhs", "sqd", "sqf",
                        "x1mm0", "x1mm1", "x1mm2", "x1mm3",
                        "qmm0", "qmm1", "qmm2", "qmm3",
                        "fmm0", "fmm1", "fmm2", "fmm3",
                        "lnxh0", "lnxh1",
                        "st0", "bc0", "lnxw", "lnxc",
                        "qw", "qrhs", "qgd", "qev",
                        "prep_pe", "prep_dve", "prep_act",
                        "tra", "tfa", "trv", "tfv",
                        "kdn", "evd", "pvd", "pad", "pbf", "p3d", "p3b",
                        "vch", "ach", "bgo", "x2w",
                        "bdw", "bmm", "bev", "bst", "bbc", "cwt", "cwr",
                        "fw", "frhs", "fgd", "fhp", "fpd", "fyw", "l4",
                        "outw"]
            for s in semnames:
                self.sem[s] = es.enter_context(nc.semaphore(s))

            ctx = dict(csb=csb, mlrb=mlrb, QT2=QT2, nKT2=nKT2, VT=VT, vp=vp,
                       nskcol=nskcol, nskrow=nskrow, epsp=epsp, recrow=recrow, ring=ring, NG=NG,
                       kern=kern, kern2=kern2, AR=AR, wbuf=wbuf, rr=rr, RR=RR,
                       bq2=bq2, bk2=bk2, bv2=bv2, bpj=bpj, dwc_s=dwc_s,
                       dwcb_s=dwcb_s, ov1=ov1, ov2=ov2, OV2W=OV2W,
                       rows=rows, RBt=RBt, MBt=MBt, sqb=sqb,
                       cwtm=cwtm, gs_sb=gs_sb, scx=scx, gw_sb=kern[:][:, 0:NDC * E],
                       PS_A=PS_A, PS_B=PS_B,
                       xbT=xbT, wqkvT=wqkvT, bqkv=bqkv, wprojT=wprojT,
                       bprojD=bprojD, dwcwD=dwcwD, dwcbD=dwcbD, gatewT=gatewT,
                       wgTD=wgTD, wuTD=wuTD, wdTD=wdTD, sgTD=sgTD, suTD=suTD,
                       sdTD=sdTD, masksD=masksD, lnx1=lnx1, ln3d=ln3d, x2d=x2d,
                       rowscr=rowscr, outD=outD, cwd=cwd)
            self.x = type("X", (), ctx)
            self.dram = dram

            phases = self.c.get("PHASES", "0ABC")
            self._consts()
            if "0" in phases:
                self._phase0()
            if "A" in phases:
                self._phase_a()
            if "B" in phases:
                self._phase_b()
            if "C" in phases:
                self._phase_c()
            if "C" not in phases:
                for s_ in ("bgo", "vch", "ach"):
                    self.wait("gpsimd", s_, self.cnt(s_))
                self.dma("gpsimd", outD[0:128, 0:min(512, NQ)],
                         self.x.gs_sb[:, 0:min(512, NQ)], "outw")
                self.wait("gpsimd", "outw", self.cnt("outw"))

            block = es.enter_context(nc.Block())

            @block.sync
            def _(q):
                for fn in self.ops["sync"]:
                    fn(q)

            @block.gpsimd
            def _(q):
                for fn in self.ops["gpsimd"]:
                    fn(q)

            @block.tensor
            def _(q):
                for fn in self.ops["tensor"]:
                    fn(q)

            @block.vector
            def _(q):
                for fn in self.ops["vector"]:
                    fn(q)

            @block.scalar
            def _(q):
                for fn in self.ops["scalar"]:
                    fn(q)

        return nc

    def _consts(self):
        x = self.x
        for name in CONST_SHAPES:
            self.dma("gpsimd", x.csb[name][:], self.dram[name], "cst")
        self.dma("gpsimd", x.rowscr[7:8, 0:2], x.masksD, "cst")
        self.wait("gpsimd", "cst", self.cnt("cst"))
        self.dma("gpsimd", x.mlrb, x.rowscr[7:8, 0:2].broadcast_to((128, 2)), "cst")
        CST = self.cnt("cst")
        for e in ("tensor", "vector", "scalar", "sync"):
            self.wait(e, "cst", CST)

    def _phase0(self):
        x, c = self.x, self.c
        DM, NDC, TCW, NTC, RR = c["DM"], c["NDC"], c["TCW"], c["NTC"], 3
        rel = []
        for tc in range(NTC):
            for ci in range(NDC):
                k = tc * 2 * NDC + ci
                if k >= RR:
                    self.wait("sync", *rel[k - RR])
                slot = "x1mm%d" % (k % RR)
                self.dma("sync", x.rr[k % RR][:],
                         x.xbT[128 * ci:128 * ci + 128, TCW * tc:TCW * tc + TCW], slot)
                self.wait("scalar", slot, self.cnt(slot))
                kk = tc * NDC + ci
                if kk >= 2:
                    self.wait("scalar", "x1rhs", kk - 1)
                rel.append(None)
                self.act(x.sqb[:, (ci % 2) * TCW:(ci % 2) * TCW + TCW], x.rr[k % RR][:],
                         AF.Square, inc=self.inc("sqd"))
                self.wait("tensor", slot, self.cnt(slot))
                if ci == 0:
                    self.wait("tensor", "st0", self.cnt("st0"))
                self.mm(x.PS_A[0:1, 0:TCW], x.csb["ones_c128"][:], x.rr[k % RR][:],
                        start=(ci == 0), stop=(ci == NDC - 1), inc=None)
                self.wait("tensor", "sqd", self.cnt("sqd"))
                self.mm(x.PS_A[0:1, 512:512 + TCW], x.csb["ones_c128"][:],
                        x.sqb[:, (ci % 2) * TCW:(ci % 2) * TCW + TCW],
                        start=(ci == 0), stop=(ci == NDC - 1), inc=self.inc("x1rhs"))
                rel[k] = ("x1rhs", self.cnt("x1rhs"))
            self._ln_rows(x.PS_A, x.rows, TCW, srhs=("x1rhs", self.cnt("x1rhs")))
            self.wait("gpsimd", "st0", self.cnt("st0"))
            self.wait("gpsimd", "bc0", self.cnt("bc0"))
            self.dma("gpsimd", x.rowscr[0:1, 0:TCW], x.rows[0:1, 1024:1024 + TCW], "bc0")
            self.dma("gpsimd", x.rowscr[1:2, 0:TCW], x.rows[0:1, 1536:1536 + TCW], "bc0")
            self.wait("gpsimd", "bc0", self.cnt("bc0"))
            self.dma("gpsimd", x.RBt, x.rowscr[0:1, 0:TCW].broadcast_to((128, TCW)), "bc0")
            self.dma("gpsimd", x.MBt, x.rowscr[1:2, 0:TCW].broadcast_to((128, TCW)), "bc0")
            BC = self.cnt("bc0")
            for ci in range(NDC):
                k2 = tc * 2 * NDC + NDC + ci
                if k2 >= RR:
                    self.wait("sync", *rel[k2 - RR])
                slot2 = "x1mm%d" % (k2 % RR)
                self.dma("sync", x.rr[k2 % RR][:],
                         x.xbT[128 * ci:128 * ci + 128, TCW * tc:TCW * tc + TCW], slot2)
                self.wait("vector", slot2, self.cnt(slot2))
                self.wait("vector", "bc0", BC)
                kk2 = tc * NDC + ci
                half = ci % 2
                if kk2 >= 2:
                    self.wait("vector", "lnxh%d" % half, self.cnt("lnxh%d" % half))
                self.vo("tt", x.gs_sb[:, half * TCW:half * TCW + TCW],
                        x.rr[k2 % RR][:], x.RBt, ALU.mult)
                self.release("vector", "x2rhs")
                rel.append(("x2rhs", self.cnt("x2rhs")))
                self.vo("tt", x.gs_sb[:, half * TCW:half * TCW + TCW],
                        x.gs_sb[:, half * TCW:half * TCW + TCW], x.MBt, ALU.add)
                self.release("vector", "lnxw")
                self.wait("gpsimd", "lnxw", self.cnt("lnxw"))
                self.dma("gpsimd", x.lnx1[128 * ci:128 * ci + 128, TCW * tc:TCW * tc + TCW],
                         x.gs_sb[:, half * TCW:half * TCW + TCW], "lnxh%d" % half)
        self.wait("sync", "lnxh0", self.cnt("lnxh0"))
        self.wait("sync", "lnxh1", self.cnt("lnxh1"))

    def _ln_rows(self, PS, rows, W, srhs=None, ssq=None):
        DM = self.c["DM"]
        if srhs:
            self.wait("vector", srhs[0], srhs[1])
        if ssq:
            self.wait("vector", ssq[0], ssq[1])
        self.vo("ts", rows[0:1, 0:W], PS[0:1, 0:W], 1.0 / DM, ALU.mult)
        self.vo("ts", rows[0:1, 512:512 + W], PS[0:1, 512:512 + W], 1.0 / DM, ALU.mult)
        self.wait("scalar", "vch", self.cnt("vch"))
        self.ao(rows[0:1, 2048:2048 + W], rows[0:1, 0:W], AF.Square)
        self.wait("vector", "ach", self.cnt("ach"))
        self.vo("tt", rows[0:1, 2560:2560 + W], rows[0:1, 512:512 + W],
                rows[0:1, 2048:2048 + W], ALU.subtract)
        self.wait("scalar", "vch", self.cnt("vch"))
        self.ao(rows[0:1, 2048:2048 + W], rows[0:1, 2560:2560 + W], AF.Ln,
                bias=self.x.csb["lneps"][:])
        self.ao(rows[0:1, 1024:1024 + W], rows[0:1, 2048:2048 + W], AF.Exp, scale=-0.5)
        self.wait("vector", "ach", self.cnt("ach"))
        self.vo("tt", rows[0:1, 2560:2560 + W], rows[0:1, 0:W], rows[0:1, 1024:1024 + W],
                ALU.mult)
        self.ts(rows[0:1, 1536:1536 + W], rows[0:1, 2560:2560 + W], -1.0, ALU.mult,
                inc=self.inc("st0"))

    def _phase_a(self):
        x, c = self.x, self.c
        S, DM, H, DH = c["S"], c["DM"], c["H"], c["DH"]
        NQ, NBLK, NDC, TCW, NTC = c["NQ"], c["NBLK"], c["NDC"], c["TCW"], c["NTC"]
        ICW, NIC = c["ICW"], c["NIC"]
        ACT_B = c["ACT_B"]
        HALF = S // 2
        RR = 3
        TA = [x.ring[i] for i in range(3)]
        TV = [x.ring[3 + i] for i in range(5)]
        NQCH = max(1, NQ // TCW)
        QW_LAST = NQ - TCW * (NQCH - 1) if NQ >= TCW else NQ

        for h in range(self.c.get("HLIM", H)):
            hp, hq = h // 2, h % 2
            even = (hq == 0)

            self.wait("gpsimd", "qev", self.cnt("qev"))
            for ci in range(NDC):
                src = x.wqkvT[128 * ci:128 * ci + 128, DH * h:DH * h + DH]
                self.dma("gpsimd", x.wbuf[:, 128 * ci:128 * ci + 64], src, "qw")
                self.dma("gpsimd", x.wbuf[:, 128 * ci + 64:128 * ci + 128], src, "qw")
            for ci in range(NDC):
                src = x.wqkvT[128 * ci:128 * ci + 128, DM + DH * h:DM + DH * h + DH]
                self.dma("gpsimd", x.wbuf[:, DM + 128 * ci:DM + 128 * ci + 64], src, "qw")
                self.dma("gpsimd", x.wbuf[:, DM + 128 * ci + 64:DM + 128 * ci + 128], src, "qw")
            if even:
                self.wait("gpsimd", "pad", h)
                for ci in range(NDC):
                    src = x.wqkvT[128 * ci:128 * ci + 128,
                                  2 * DM + 128 * hp:2 * DM + 128 * hp + 128]
                    self.dma("gpsimd", x.kern[:, 128 * ci:128 * ci + 128],
                             src, "qw")
            self.dma("gpsimd", x.bq2[0:64, :], x.bqkv[DH * h:DH * h + DH, :], "qw")
            self.dma("gpsimd", x.bq2[64:128, :], x.bqkv[DH * h:DH * h + DH, :], "qw")
            self.dma("gpsimd", x.bk2[0:64, :], x.bqkv[DM + DH * h:DM + DH * h + DH, :], "qw")
            self.dma("gpsimd", x.bk2[64:128, :], x.bqkv[DM + DH * h:DM + DH * h + DH, :], "qw")
            if even:
                self.dma("gpsimd", x.bv2,
                         x.bqkv[2 * DM + 128 * hp:2 * DM + 128 * hp + 128, :], "qw")
            self.wait("gpsimd", "bgo", self.cnt("bgo"))
            self.dma("gpsimd", x.dwc_s[64 * hq:64 * hq + 64, :],
                     x.dwcwD[DH * h:DH * h + DH, :], "qw")
            self.dma("gpsimd", x.dwcb_s[64 * hq:64 * hq + 64, :],
                     x.dwcbD[DH * h:DH * h + DH, :], "qw")
            QW = self.cnt("qw")

            self.wait("tensor", "qw", QW)
            for tc in range(NTC):
                own = tc < NQCH
                qw_ = TCW if (tc < NQCH - 1 or NQ >= TCW * NQCH) else QW_LAST
                if NQ < TCW:
                    qw_ = NQ
                if tc > 0:
                    self.wait("tensor", "qev", self.cnt("qev"))
                for ci in range(NDC):
                    k = (h * NTC + tc) * NDC + ci
                    if k >= RR:
                        self.wait("sync", "qrhs", k - RR + 1)
                    slot = "qmm%d" % (k % RR)
                    self.dma("sync", x.rr[k % RR][:],
                             x.lnx1[128 * ci:128 * ci + 128, TCW * tc:TCW * tc + TCW], slot)
                    self.wait("tensor", slot, self.cnt(slot))
                    first, last = (ci == 0), (ci == NDC - 1)
                    roles = [("K", x.PS_A[:, 512:512 + TCW], x.wbuf[:, DM:2 * DM], TCW)]
                    if even:
                        roles.append(("V", x.PS_A[:, 1024:1024 + TCW],
                                      x.kern[:, 0:DM], TCW))
                    if own:
                        roles.append(("Q", x.PS_A[:, 0:qw_], x.wbuf[:, 0:DM], qw_))
                    for ri, (rn, pslice, wslab, w_) in enumerate(roles):
                        lastrole = ri == len(roles) - 1
                        inc = self.inc("qrhs") if lastrole else None
                        self.mm(pslice, wslab[:, 128 * ci:128 * ci + 128],
                                x.rr[k % RR][:, 0:w_], start=first, stop=last, inc=inc)
                self.wait("vector", "qrhs", self.cnt("qrhs"))
                nblk_tc = TCW // 128
                g0 = tc * nblk_tc
                ktop = x.nKT2[0:64, 64 * g0:64 * (g0 + nblk_tc)]
                ktop = ktop.rearrange("p (g s) -> p g s", g=nblk_tc)
                src_t = x.PS_A[0:64, 512:512 + TCW].rearrange(
                    "p (g u s) -> p g u s", g=nblk_tc, u=2)[:, :, 0, :]
                self.vo("ts", ktop, src_t, x.bk2[0:64, :], ALU.add, -1.0, ALU.mult)
                kbot = x.nKT2[64:128, 64 * g0:64 * (g0 + nblk_tc)]
                kbot = kbot.rearrange("p (g s) -> p g s", g=nblk_tc)
                src_b = x.PS_A[64:128, 512:512 + TCW].rearrange(
                    "p (g u s) -> p g u s", g=nblk_tc, u=2)[:, :, 1, :]
                self.vo("ts", kbot, src_b, x.bk2[64:128, :], ALU.add, -1.0, ALU.mult)
                if even:
                    self.vo("ts", x.VT[:, TCW * tc:TCW * tc + TCW],
                            x.PS_A[:, 1024:1024 + TCW], x.bv2, ALU.add)
                if own:
                    self.vo("ts", x.QT2[:, TCW * tc:TCW * tc + qw_], x.PS_A[:, 0:qw_],
                            x.bq2, ALU.add)
                self.emit("vector", lambda q: q.sem_inc(self.sem["qev"], 1))
                self.bump("qev")

            if self.c.get("ASTOP") == "qkv":
                return
            self.wait("tensor", "qev", self.cnt("qev"))
            for q0 in range(0, NQ, 512):
                qw2 = min(512, NQ - q0)
                self.mm(x.PS_A[0:1, q0:q0 + qw2], x.csb["halfones"][:],
                        x.QT2[:, q0:q0 + qw2], True, True,
                        inc=self.inc("prep_pe") if q0 + qw2 >= NQ else None)
            self.wait("scalar", "prep_pe", self.cnt("prep_pe"))
            self.act(x.epsp, x.PS_A[0:1, 0:NQ], AF.Exp, bias=x.csb["logeps"][:],
                     scale=-1.0 / 16.0, inc=self.inc("prep_act"))
            self.wait("tensor", "prep_act", self.cnt("prep_act"))
            for q0 in range(0, HALF, 512):
                qw2 = min(512, HALF - q0)
                self.mm(x.PS_A[0:1, q0:q0 + qw2], x.csb["halfones"][:],
                        x.nKT2[:, q0:q0 + qw2], True, True,
                        inc=self.inc("prep_pe") if q0 + qw2 >= HALF else None)
            self.wait("vector", "prep_pe", self.cnt("prep_pe"))
            ev_row = x.nskrow[0:1, :].rearrange("p (g u s) -> p g u s", g=NBLK, u=2)
            self.ts(ev_row[:, :, 0, :], x.PS_A[0:1, 0:HALF], 1.0 / 16.0, ALU.mult,
                    inc=self.inc("prep_dve"))
            self.wait("tensor", "prep_dve", self.cnt("prep_dve"))
            for q0 in range(0, HALF, 512):
                qw2 = min(512, HALF - q0)
                self.mm(x.PS_A[0:1, q0:q0 + qw2], x.csb["otherhalf"][:],
                        x.nKT2[:, q0:q0 + qw2], True, True,
                        inc=self.inc("prep_pe") if q0 + qw2 >= HALF else None)
            self.wait("vector", "prep_pe", self.cnt("prep_pe"))
            self.ts(ev_row[:, :, 1, :], x.PS_A[0:1, 0:HALF], 1.0 / 16.0, ALU.mult,
                    inc=self.inc("prep_dve"))
            self.wait("tensor", "prep_dve", self.cnt("prep_dve"))
            for b in range(NBLK):
                self.transp(x.PS_A[:, b:b + 1], x.nskrow[0:1, 128 * b:128 * b + 128],
                            x.csb["id1"][:],
                            inc=self.inc("prep_pe") if b == NBLK - 1 else None)
            self.wait("vector", "prep_pe", self.cnt("prep_pe"))
            self.cp(x.nskcol, x.PS_A[:, 0:NBLK], inc=self.inc("prep_dve"))
            self.wait("vector", "prep_dve", self.cnt("prep_dve"))
            vsrc = x.VT[64 * hq:64 * hq + 64, :]
            ngrp = (NBLK + 7) // 8
            for g in range(ngrp):
                nb = min(8, NBLK - 8 * g)
                self.wait("tensor", "prep_dve", self.cnt("prep_dve"))
                for j in range(nb):
                    b = 8 * g + j
                    self.transp(x.PS_A[:, 64 * j:64 * j + 64],
                                vsrc[:, 128 * b:128 * b + 128],
                                x.csb["id128"][64 * hq:64 * hq + 64, 64 * hq:64 * hq + 64],
                                inc=self.inc("prep_pe") if j == nb - 1 else None)
                self.wait("vector", "prep_pe", self.cnt("prep_pe"))
                dst = x.vp[:, 65 * 8 * g:65 * (8 * g + nb)]
                dst = dst.rearrange("p (b w) -> p b w", b=nb)[:, :, 0:64]
                src = x.PS_A[:, 0:64 * nb].rearrange("p (b w) -> p b w", b=nb)
                self.cp(dst, src, inc=self.inc("prep_dve"))
                self.wait("vector", "prep_dve", self.cnt("prep_dve"))
            ones_ap = x.vp[:, :].rearrange("p (b w) -> p b w", b=NBLK)[:, :, 64:65]
            self.mset(ones_ap, 1.0, inc=self.inc("prep_dve"))
            PREP = self.cnt("prep_dve")
            if self.c.get("ASTOP") == "prep":
                return

            headblk = h * NBLK
            na2 = self.cnt("tra")
            nv2 = self.cnt("trv")
            fa0 = self.cnt("tfa")
            fv0 = self.cnt("tfv")
            ACT_S = c["ACT_S"]

            NDVE = 64 - len(c["ACT_S"])

            def emit_produce(eng, b, s, kp, Tring, s_rdy, s_free, f0):
                R = len(Tring)
                if kp == 0:
                    self.wait(eng, "prep_dve", PREP)
                elif kp >= R:
                    pk = kp - R
                    if eng == "vector" and pk % NDVE == NDVE - 1:
                        self.wait(eng, "evd", headblk + pk // NDVE + 1)
                    else:
                        if eng == "vector":
                            nrel = pk + 1 - pk // NDVE
                        else:
                            nrel = pk + 1
                        self.wait(eng, s_free, f0 + nrel)
                tile = Tring[kp % R]
                inc = self.inc(s_rdy)
                if eng == "scalar":
                    self.act(tile[:, 0:NQ], x.QT2[:], AF.Relu,
                             bias=x.nKT2[:, 64 * b + s:64 * b + s + 1], inc=inc)
                else:
                    self.ts(tile[:, 0:NQ], x.QT2[:],
                            x.nKT2[:, 64 * b + s:64 * b + s + 1],
                            ALU.add, 0.0, ALU.max, inc=inc)

            def emit_evac(b):
                gbi = headblk + b
                if b <= 1:
                    if b == 0:
                        self.wait("scalar", "prep_dve", PREP)
                    self.wait("scalar", "pad", h)
                else:
                    self.wait("scalar", "pvd", gbi - (h + 1))
                self.wait("scalar", "kdn", gbi + 1)
                dst = x.kern if b % 2 == 0 else x.kern2
                self.act(dst[:, 0:NQ], x.PS_A[:, 0:NQ], AF.Exp, bias=x.nskcol[:, b:b + 1],
                         scale=-1.0 / 8.0, inc=self.inc("evd"))

            actS = sorted(ACT_S)
            dveS = [s for s in range(64) if s not in ACT_S]

            ai = 0
            for g in range(NBLK):
                if g >= 1:
                    emit_evac(g - 1)
                for s in actS:
                    emit_produce("scalar", g, s, ai, TA, "tra", "tfa", fa0)
                    ai += 1
            emit_evac(NBLK - 1)

            vi = 0
            for g in range(NBLK):
                for s in dveS:
                    emit_produce("vector", g, s, vi, TV, "trv", "tfv", fv0)
                    vi += 1

            pa = 0
            pv = 0
            for b in range(NBLK):
                gbi = headblk + b
                for s in range(64):
                    isact = s in ACT_S
                    ringp = TA if isact else TV
                    s_rdy = "tra" if isact else "trv"
                    kp = pa if isact else pv
                    base = na2 if isact else nv2
                    self.wait("tensor", s_rdy, base + kp + 1)
                    if s == 0:
                        if gbi > 0:
                            self.wait("tensor", "evd", gbi)
                        if b == 0:
                            self.wait("tensor", "prep_dve", PREP)
                    tile = ringp[kp % len(ringp)]
                    for ic in range(NIC):
                        inc = None
                        if ic == NIC - 1:
                            if s == 63:
                                inc = self.inc("kdn")
                            else:
                                inc = self.inc("tfa" if isact else "tfv")
                        self.mm(x.PS_A[:, 512 * ic:512 * ic + ICW],
                                x.csb["wbase"][:, 64 - s:192 - s],
                                tile[:, ICW * ic:ICW * ic + ICW],
                                start=(s == 0), stop=(s == 63), inc=inc)
                    if isact:
                        pa += 1
                    else:
                        pv += 1
                if b > 0:
                    self._emit_av(b - 1, h)
            self._emit_av(NBLK - 1, h)

            if self.c.get("ASTOP") == "dist":
                return
            b0 = 64 * hq
            rb_v = x.ov1[b0:b0 + 64, 0:NQ]
            lo_v = x.ov2[b0:b0 + 64, 0:NQ]
            e_v = x.ov2[b0:b0 + 64, x.OV2W:x.OV2W + 2]
            dwc_v = x.dwc_s[b0:b0 + 64, :]
            dwcb_v = x.dwcb_s[b0:b0 + 64, :]
            mlr_v = x.mlrb[b0:b0 + 64, :]
            dst = x.AR[b0:b0 + 64, NQ * hp:NQ * hp + NQ]
            self.wait("vector", "pad", h + 1)
            self.vo("tt", x.epsp, x.PS_B[64:65, 0:NQ], x.epsp, ALU.add)
            self.recip(x.recrow, x.epsp, inc=self.inc("p3d"))
            self.wait("gpsimd", "p3d", self.cnt("p3d"))
            self.wait("gpsimd", "p3b", self.cnt("p3b"))
            if h == 0:
                self.wait("gpsimd", "bc0", self.cnt("bc0"))
            self.dma("gpsimd", x.rowscr[2:3, 0:NQ], x.recrow, "p3b")
            self.wait("gpsimd", "p3b", self.cnt("p3b"))
            self.dma("gpsimd", rb_v, x.rowscr[2:3, 0:NQ].broadcast_to((64, NQ)), "p3b")
            if self.c.get("ASTOP") == "p3rec":
                return
            self.wait("vector", "p3b", self.cnt("p3b"))
            self.tt(dst, x.PS_B[0:64, 0:NQ], rb_v, ALU.mult, inc=self.inc("pbf"))
            self.wait("vector", "pbf", self.cnt("pbf"))
            if self.c.get("ASTOP") == "p3div":
                return
            v64 = x.VT[:][b0:b0 + 64, :]
            self.vo("ts", lo_v, v64[:, 0:NQ], dwc_v[:, 1:2], ALU.mult)
            self.vo("stt", lo_v[:, 1:NQ], v64[:, 0:NQ - 1], dwc_v[:, 0:1],
                    lo_v[:, 1:NQ], ALU.mult, ALU.add)
            self.vo("stt", lo_v[:, 0:NQ - 1], v64[:, 1:NQ], dwc_v[:, 2:3],
                    lo_v[:, 0:NQ - 1], ALU.mult, ALU.add)
            if self.c.get("ASTOP") == "p3conv":
                return
            self.vo("ts", e_v[:, 0:1], v64[:, S - 1:S], mlr_v[:, 0:1], ALU.mult)
            self.vo("stt", lo_v[:, 0:1], e_v[:, 0:1], dwc_v[:, 0:1],
                    lo_v[:, 0:1], ALU.mult, ALU.add)
            self.vo("ts", e_v[:, 1:2], v64[:, NQ:NQ + 1], mlr_v[:, 1:2], ALU.mult)
            self.vo("stt", lo_v[:, NQ - 1:NQ], e_v[:, 1:2], dwc_v[:, 2:3],
                    lo_v[:, NQ - 1:NQ], ALU.mult, ALU.add)
            if self.c.get("ASTOP") == "p3edge":
                return
            self.vo("ts", lo_v, lo_v, dwcb_v, ALU.add)
            if self.c.get("ASTOP") == "p3bias":
                return
            v_ = self.c.get("P3VAR", 0)
            if v_ == 1:
                self.tt(dst, dst, lo_v, ALU.add, inc=None)
                self.bump("bgo")
                self.release("vector", "bgo")
            elif v_ == 2:
                self.tt(x.gs_sb[:, 0:NQ] if NQ <= 2 * 512 else lo_v, dst, lo_v,
                        ALU.add, inc=self.inc("bgo"))
                self.wait("vector", "bgo", self.cnt("bgo"))
            else:
                self.tt(dst, dst, lo_v, ALU.add, inc=self.inc("bgo"))
                self.wait("vector", "bgo", self.cnt("bgo"))

    def _emit_av(self, b, h):
        x, c = self.x, self.c
        NQ, ICW, NIC, NBLK = c["NQ"], c["ICW"], c["NIC"], c["NBLK"]
        gbi = h * NBLK + b
        self.wait("tensor", "evd", gbi + 1)
        if b == 0:
            self.wait("tensor", "pbf", h)
        src = x.kern if b % 2 == 0 else x.kern2
        for ic in range(NIC):
            inc = None
            if ic == NIC - 1:
                inc = self.inc("pad") if b == NBLK - 1 else self.inc("pvd")
            self.mm(x.PS_B[0:65, 512 * ic:512 * ic + ICW], x.vp[:, 65 * b:65 * b + 65],
                    src[:, ICW * ic:ICW * ic + ICW],
                    start=(b == 0), stop=(b == NBLK - 1), inc=inc)

    def _phase_b(self):
        x, c = self.x, self.c
        DM, NQ, NDC, QCW, NQC, E = c["DM"], c["NQ"], c["NDC"], c["QCW"], c["NQC"], c["E"]
        TCW = c["TCW"]
        RR = 3
        BGO = self.cnt("bgo")
        for e in ("tensor", "vector", "scalar", "gpsimd", "sync"):
            self.wait(e, "bgo", BGO)
        for ci in range(NDC):
            self.dma("gpsimd", x.gw_sb[:, E * ci:E * ci + E],
                     x.gatewT[128 * ci:128 * ci + 128, :], "bdw")
        GWD = self.cnt("bdw")
        slotV = lambda oc: x.VT[:][:, QCW * oc:QCW * oc + QCW]

        for tc in range(NQC):
            self.wait("tensor", "cwt", self.cnt("cwt"))
            self.wait("vector", "x2w", self.cnt("x2w"))
            for oc in range(NDC):
                self.wait("gpsimd", "bev", self.cnt("bev"))
                self.wait("gpsimd", "bdw", self.cnt("bdw"))
                for ci in range(NDC):
                    self.dma("gpsimd", x.wbuf[:, 128 * ci:128 * ci + 128],
                             x.wprojT[128 * ci:128 * ci + 128, 128 * oc:128 * oc + 128],
                             "bdw")
                self.dma("gpsimd", x.bpj, x.bprojD[128 * oc:128 * oc + 128, :], "bdw")
                self.wait("tensor", "bdw", self.cnt("bdw"))
                slot = oc % 2
                key = ("b1", slot)
                if key in self.slotuse:
                    self.wait("tensor", "bev", self.slotuse[key])
                for ci in range(NDC):
                    self.mm(x.PS_A[:, 512 * slot:512 * slot + QCW],
                            x.wbuf[:, 128 * ci:128 * ci + 128],
                            x.AR[:, ci * NQ + QCW * tc:ci * NQ + QCW * tc + QCW],
                            start=(ci == 0), stop=(ci == NDC - 1),
                            inc=self.inc("bmm") if ci == NDC - 1 else None)
                self.wait("vector", "bmm", self.cnt("bmm"))
                self.ts(slotV(oc), x.PS_A[:, 512 * slot:512 * slot + QCW], x.bpj[:],
                        ALU.add, inc=self.inc("bev"))
                self.slotuse[("b1", slot)] = self.cnt("bev")
                self.wait("vector", "bev", self.cnt("bev"))
            if self.c.get("BSTOP") == "proj":
                return
            PREV = self.cnt("bev")
            self.wait("tensor", "bev", PREV)
            self.wait("scalar", "bev", PREV)
            for oc in range(NDC):
                kk = tc * NDC + oc
                if oc == 0:
                    self.wait("tensor", "st0", self.cnt("st0"))
                self.mm(x.PS_B[0:1, 0:QCW], x.csb["ones_c128"][:], slotV(oc),
                        start=(oc == 0), stop=(oc == NDC - 1),
                        inc=self.inc("bst") if oc == NDC - 1 else None)
                if kk >= 2:
                    self.wait("scalar", "sqf", self.cnt("sqf") - 1)
                self.act(x.sqb[:, (oc % 2) * TCW:(oc % 2) * TCW + QCW], slotV(oc),
                         AF.Square, inc=self.inc("sqd"))
                self.wait("tensor", "sqd", self.cnt("sqd"))
                self.mm(x.PS_B[0:1, 512:512 + QCW], x.csb["ones_c128"][:],
                        x.sqb[:, (oc % 2) * TCW:(oc % 2) * TCW + QCW],
                        start=(oc == 0), stop=(oc == NDC - 1), inc=self.inc("sqf"))
            if self.c.get("BSTOP") == "ln2stats":
                return
            self._ln_rows(x.PS_B, x.rows, QCW, srhs=("bst", self.cnt("bst")),
                          ssq=("sqf", self.cnt("sqf")))
            if self.c.get("BSTOP") == "ln2rows":
                return
            self.wait("gpsimd", "st0", self.cnt("st0"))
            self.wait("gpsimd", "bbc", self.cnt("bbc"))
            self.dma("gpsimd", x.rowscr[3:4, 0:QCW], x.rows[0:1, 1024:1024 + QCW], "bbc")
            self.dma("gpsimd", x.rowscr[4:5, 0:QCW], x.rows[0:1, 1536:1536 + QCW], "bbc")
            self.wait("gpsimd", "bbc", self.cnt("bbc"))
            self.dma("gpsimd", x.RBt[:, 0:QCW],
                     x.rowscr[3:4, 0:QCW].broadcast_to((128, QCW)), "bbc")
            self.dma("gpsimd", x.MBt[:, 0:QCW],
                     x.rowscr[4:5, 0:QCW].broadcast_to((128, QCW)), "bbc")
            BBC = self.cnt("bbc")
            self.wait("vector", "lnxh0", self.cnt("lnxh0"))
            for oc in range(NDC):
                kx = tc * NDC + oc
                self.wait("sync", "bev", self.cnt("bev"))
                slotx = "qmm%d" % (kx % RR)
                self.dma("sync", x.rr[kx % RR][:, 0:QCW],
                         x.xbT[128 * oc:128 * oc + 128, QCW * tc:QCW * tc + QCW], slotx)
                self.wait("vector", slotx, self.cnt(slotx))
                self.wait("vector", "bbc", BBC)
                self.vo("tt", x.gs_sb[:, 0:QCW], slotV(oc), x.RBt[:, 0:QCW], ALU.mult)
                self.vo("tt", x.gs_sb[:, 0:QCW], x.gs_sb[:, 0:QCW], x.MBt[:, 0:QCW],
                        ALU.add)
                self.tt(slotV(oc), x.gs_sb[:, 0:QCW], x.rr[kx % RR][:, 0:QCW], ALU.add,
                        inc=self.inc("bev"))
                self.wait("vector", "bev", self.cnt("bev"))
                self.wait("gpsimd", "bev", self.cnt("bev"))
                self.dma("gpsimd", x.x2d[128 * oc:128 * oc + 128, QCW * tc:QCW * tc + QCW],
                         slotV(oc), "x2w")
                self.wait("tensor", "bev", self.cnt("bev"))
                if oc == 0:
                    self.wait("tensor", "st0", self.cnt("st0"))
                self.mm(x.PS_B[0:1, 1024:1024 + QCW], x.csb["ones_c128"][:], slotV(oc),
                        start=(oc == 0), stop=(oc == NDC - 1),
                        inc=self.inc("bst") if oc == NDC - 1 else None)
                self.wait("scalar", "bev", self.cnt("bev"))
                kk = tc * NDC + oc
                if kk >= 2:
                    self.wait("scalar", "sqf", self.cnt("sqf") - 1)
                self.act(x.sqb[:, (oc % 2) * TCW:(oc % 2) * TCW + QCW], slotV(oc),
                         AF.Square, inc=self.inc("sqd"))
                self.wait("tensor", "sqd", self.cnt("sqd"))
                self.mm(x.PS_B[0:1, 1536:1536 + QCW], x.csb["ones_c128"][:],
                        x.sqb[:, (oc % 2) * TCW:(oc % 2) * TCW + QCW],
                        start=(oc == 0), stop=(oc == NDC - 1), inc=self.inc("sqf"))
            if self.c.get("BSTOP") == "apply":
                return
            self._ln_rows(x.PS_B[:, 1024:2048], x.rows, QCW,
                          srhs=("bst", self.cnt("bst")), ssq=("sqf", self.cnt("sqf")))
            self.wait("gpsimd", "st0", self.cnt("st0"))
            self.wait("gpsimd", "bbc", self.cnt("bbc"))
            self.dma("gpsimd", x.rowscr[5:6, 0:QCW], x.rows[0:1, 1024:1024 + QCW], "bbc")
            self.dma("gpsimd", x.rowscr[6:7, 0:QCW], x.rows[0:1, 1536:1536 + QCW], "bbc")
            self.wait("gpsimd", "bbc", self.cnt("bbc"))
            self.dma("gpsimd", x.RBt[:, 0:QCW],
                     x.rowscr[5:6, 0:QCW].broadcast_to((128, QCW)), "bbc")
            self.dma("gpsimd", x.MBt[:, 0:QCW],
                     x.rowscr[6:7, 0:QCW].broadcast_to((128, QCW)), "bbc")
            BBC3 = self.cnt("bbc")
            if tc == 0:
                self.wait("tensor", "bdw", GWD)
            for oc in range(NDC):
                slot = oc % 2
                self.wait("vector", "bbc", BBC3)
                if tc * NDC + oc >= 2:
                    self.wait("vector", "lnxh%d" % slot, self.cnt("lnxh%d" % slot))
                self.vo("tt", x.gs_sb[:, slot * TCW:slot * TCW + QCW], slotV(oc),
                        x.RBt[:, 0:QCW], ALU.mult)
                self.tt(x.gs_sb[:, slot * TCW:slot * TCW + QCW],
                        x.gs_sb[:, slot * TCW:slot * TCW + QCW], x.MBt[:, 0:QCW],
                        ALU.add, inc=self.inc("bev"))
                self.wait("vector", "bev", self.cnt("bev"))
                self.wait("gpsimd", "bev", self.cnt("bev"))
                self.dma("gpsimd", x.ln3d[128 * oc:128 * oc + 128, QCW * tc:QCW * tc + QCW],
                         x.gs_sb[:, slot * TCW:slot * TCW + QCW], "lnxh%d" % slot)
                self.wait("tensor", "bev", self.cnt("bev"))
                for gt in range(QCW // 128):
                    self.mm(x.PS_A[:, 512 * gt:512 * gt + E],
                            x.gs_sb[:, slot * TCW + 128 * gt:slot * TCW + 128 * gt + 128],
                            x.gw_sb[:, E * oc:E * oc + E],
                            start=(oc == 0), stop=(oc == NDC - 1),
                            inc=self.inc("bmm")
                            if (oc == NDC - 1 and gt == QCW // 128 - 1) else None)
            if self.c.get("BSTOP") == "ln3apply":
                return
            self.wait("vector", "bmm", self.cnt("bmm"))
            for gt in range(QCW // 128):
                g_all = tc * (QCW // 128) + gt
                self._softmax_top2(x.PS_A[:, 512 * gt:512 * gt + E],
                                   x.cwtm[:, E * g_all:E * g_all + E])
            if self.c.get("BSTOP") == "softmax":
                return
        NG = NQ // 128
        self.wait("tensor", "cwt", self.cnt("cwt"))
        for e in range(E):
            if e >= 1:
                self.wait("tensor", "cwt", self.cnt("cwt"))
            for g in range(NG):
                self.mm(x.PS_B[0:1, 128 * g:128 * g + 128],
                        x.cwtm[:, E * g + e:E * g + e + 1],
                        x.csb["id128"][:], True, True,
                        inc=self.inc("bmm") if g == NG - 1 else None)
            if self.c.get("BSTOP") == "cw0t" and e == 0:
                return
            self.wait("vector", "bmm", self.cnt("bmm"))
            if e >= 1:
                self.wait("vector", "cwr", self.cnt("cwr"))
            self.cp(x.ov2[0:1, 0:NQ], x.PS_B[0:1, 0:NQ], inc=self.inc("cwt"))
            self.wait("vector", "cwt", self.cnt("cwt"))
            if self.c.get("BSTOP") == "cw0c" and e == 0:
                return
            self.wait("gpsimd", "cwt", self.cnt("cwt"))
            self.dma("gpsimd", x.cwd[e:e + 1, :], x.ov2[0:1, 0:NQ], "cwr")
            if self.c.get("BSTOP") == "cw0d" and e == 0:
                return
    
    def _softmax_top2(self, ps, cw_out):
        E = self.c["E"]
        x = self.x
        s = x.scx
        self.vo("red", s[:, 0:1], ps, ALU.max)
        self.vo("ts", s[:, 1:1 + E], ps, s[:, 0:1], ALU.subtract)
        self.wait("scalar", "vch", self.cnt("vch"))
        self.ao(s[:, 1 + E:1 + 2 * E], s[:, 1:1 + E], AF.Exp)
        self.wait("vector", "ach", self.cnt("ach"))
        self.vo("red", s[:, 1 + 2 * E:2 + 2 * E], s[:, 1 + E:1 + 2 * E], ALU.add)
        self.vo("recip", s[:, 2 + 2 * E:3 + 2 * E], s[:, 1 + 2 * E:2 + 2 * E])
        sm = s[:, 3 + 2 * E:3 + 3 * E]
        self.vo("ts", sm, s[:, 1 + E:1 + 2 * E], s[:, 2 + 2 * E:3 + 2 * E], ALU.mult)
        self.vo("red", s[:, 0:1], sm, ALU.max)
        eq = s[:, 3 + 3 * E:3 + 4 * E]
        self.vo("ts", eq, sm, s[:, 0:1], ALU.is_ge)
        msk = s[:, 3 + 4 * E:3 + 5 * E]
        self.vo("ts", msk, eq, -1e30, ALU.mult)
        self.vo("tt", msk, msk, sm, ALU.add)
        self.vo("red", s[:, 1:2], msk, ALU.max)
        ge = s[:, 3 + 5 * E:3 + 6 * E]
        self.vo("ts", ge, sm, s[:, 1:2], ALU.is_ge)
        self.tt(cw_out, sm, ge, ALU.mult, inc=self.inc("cwt"))
        self.wait("vector", "cwt", self.cnt("cwt"))

    def _phase_c(self):
        x, c = self.x, self.c
        DM, NQ, NDC, QCW, NQC = c["DM"], c["NQ"], c["NDC"], c["QCW"], c["NQC"]
        INTER, E, NICH, TCW = c["INTER"], c["E"], c["NICH"], c["TCW"]
        RR = 3
        CWB = x.QT2
        for eng in ("tensor", "vector", "scalar", "gpsimd", "sync"):
            self.wait(eng, "cwr", 16 * E)
        for eng in ("vector", "tensor", "sync", "gpsimd"):
            self.wait(eng, "x2w", self.cnt("x2w"))
            self.wait(eng, "lnxh0", self.cnt("lnxh0"))
            self.wait(eng, "lnxh1", self.cnt("lnxh1"))

        def wg_tile(ci, ic):
            f = (ci * NICH + ic) * 128
            return x.VT[:][:, f:f + 128]

        VP_CAP = (max(c["NBLK"] * 65, NQ) // 128) * 128
        assert NICH * NDC * 128 <= VP_CAP + 2 * DM, "wu arena too small"

        def wu_tile(ci, ic):
            f = (ci * NICH + ic) * 128
            if f < VP_CAP:
                return x.vp[:][:, f:f + 128]
            return x.wbuf[:, f - VP_CAP:f - VP_CAP + 128]

        def wd_tile(half, ic):
            col = (half * NICH + ic) * 128
            return x.nKT2[:][:, col:col + 128]

        def hp_tile(ic, w):
            return x.kern[:, 512 * ic:512 * ic + w]

        for ei in range(-1, E):
            shared = ei < 0
            wgS = x.sgTD if shared else x.wgTD[ei]
            wuS = x.suTD if shared else x.wuTD[ei]
            wdS = x.sdTD if shared else x.wdTD[ei]
            self.wait("gpsimd", "frhs", self.cnt("frhs"))
            self.wait("gpsimd", "fw", self.cnt("fw"))
            for ci in range(NDC):
                for ic in range(NICH):
                    self.dma("gpsimd", wg_tile(ci, ic),
                             wgS[128 * ci:128 * ci + 128, 128 * ic:128 * ic + 128], "fw")
                    self.dma("gpsimd", wu_tile(ci, ic),
                             wuS[128 * ci:128 * ci + 128, 128 * ic:128 * ic + 128], "fw")
            FW = self.cnt("fw")
            if not shared:
                self.wait("gpsimd", "fhp", self.cnt("fhp"))
                self.wait("gpsimd", "l4", self.cnt("l4"))
                self.dma("gpsimd", CWB[:][:, 0:NQ],
                         x.cwd[ei:ei + 1, :].broadcast_to((128, NQ)), "l4")
            L4C = self.cnt("l4")

            self.wait("tensor", "fw", FW)
            FPD0 = self.cnt("fpd")
            for tc in range(NQC):
                for ci in range(NDC):
                    k = ((ei + 1) * NQC + tc) * NDC + ci
                    if k >= RR:
                        self.wait("sync", "frhs", k - RR + 1)
                    slot = "fmm%d" % (k % RR)
                    self.dma("sync", x.rr[k % RR][:, 0:QCW],
                             x.ln3d[128 * ci:128 * ci + 128, QCW * tc:QCW * tc + QCW],
                             slot)
                    self.wait("tensor", slot, self.cnt(slot))
                    if ci == 0:
                        self.wait("tensor", "fyw", self.cnt("fyw"))
                        self.wait("tensor", "fhp", self.cnt("fhp"))
                    for ic in range(NICH):
                        self.mm(x.PS_A[:, 512 * ic:512 * ic + QCW],
                                wg_tile(ci, ic), x.rr[k % RR][:, 0:QCW],
                                start=(ci == 0), stop=(ci == NDC - 1), inc=None)
                        inc = self.inc("frhs") if ic == NICH - 1 else None
                        self.mm(x.PS_B[:, 512 * ic:512 * ic + QCW],
                                wu_tile(ci, ic), x.rr[k % RR][:, 0:QCW],
                                start=(ci == 0), stop=(ci == NDC - 1), inc=inc)
                self.wait("scalar", "frhs", self.cnt("frhs"))
                self.wait("scalar", "l4", L4C)
                self.wait("vector", "l4", L4C)
                self.wait("vector", "fpd", self.cnt("fpd"))
                for ic in range(NICH):
                    self.wait("scalar", "vch", self.cnt("vch"))
                    self.ao(x.gs_sb[:, 0:QCW], x.PS_A[:, 512 * ic:512 * ic + QCW],
                            AF.Silu)
                    self.wait("vector", "ach", self.cnt("ach"))
                    self.vo("tt", x.gs_sb[:, TCW:TCW + QCW], x.gs_sb[:, 0:QCW],
                            x.PS_B[:, 512 * ic:512 * ic + QCW], ALU.mult)
                    if shared:
                        self.cp(hp_tile(ic, QCW), x.gs_sb[:, TCW:TCW + QCW],
                                inc=self.inc("fhp"))
                    else:
                        self.tt(hp_tile(ic, QCW), x.gs_sb[:, TCW:TCW + QCW],
                                CWB[:][:, QCW * tc:QCW * tc + QCW], ALU.mult,
                                inc=self.inc("fhp"))
                    self.wait("vector", "fhp", self.cnt("fhp"))
                FHPTC = self.cnt("fhp")
                self.wait("tensor", "fhp", FHPTC)
                for dmc in range(NDC):
                    half = dmc % 2
                    if ("wd", half) in self.slotuse:
                        self.wait("gpsimd", "fpd", self.slotuse[("wd", half)])
                    self.wait("gpsimd", "fw", self.cnt("fw"))
                    for ic in range(NICH):
                        self.dma("gpsimd", wd_tile(half, ic),
                                 wdS[128 * ic:128 * ic + 128, 128 * dmc:128 * dmc + 128],
                                 "fw")
                    FWD = self.cnt("fw")
                    self.wait("tensor", "fw", FWD)
                    yslot = dmc % 2
                    key = ("dn", yslot)
                    if key in self.slotuse:
                        self.wait("tensor", "fyw", self.slotuse[key])
                    for ic in range(NICH):
                        inc = self.inc("fpd") if ic == NICH - 1 else None
                        self.mm(x.PS_A[:, 512 * yslot:512 * yslot + QCW],
                                wd_tile(half, ic), hp_tile(ic, QCW),
                                start=(ic == 0), stop=(ic == NICH - 1), inc=inc)
                    self.slotuse[("wd", half)] = self.cnt("fpd")
                    self.wait("vector", "fpd", self.cnt("fpd"))
                    dstc = x.AR[:, dmc * NQ + QCW * tc:dmc * NQ + QCW * tc + QCW]
                    if shared:
                        self.cp(dstc, x.PS_A[:, 512 * yslot:512 * yslot + QCW],
                                inc=self.inc("fyw"))
                    else:
                        self.tt(dstc, dstc, x.PS_A[:, 512 * yslot:512 * yslot + QCW],
                                ALU.add, inc=self.inc("fyw"))
                    self.slotuse[("dn", yslot)] = self.cnt("fyw")
                    self.wait("vector", "fyw", self.cnt("fyw"))

        FYW = self.cnt("fyw")
        self.wait("tensor", "fyw", FYW)
        self.wait("scalar", "fyw", FYW)
        out_dma = []
        for tc in range(NQC):
            for oc in range(NDC):
                kk = tc * NDC + oc
                if oc == 0:
                    self.wait("tensor", "st0", self.cnt("st0"))
                self.mm(x.PS_B[0:1, 0:QCW], x.csb["ones_c128"][:],
                        x.AR[:, oc * NQ + QCW * tc:oc * NQ + QCW * tc + QCW],
                        start=(oc == 0), stop=(oc == NDC - 1),
                        inc=self.inc("bst") if oc == NDC - 1 else None)
                if kk >= 2:
                    self.wait("scalar", "sqf", self.cnt("sqf") - 1)
                self.act(x.sqb[:, (oc % 2) * TCW:(oc % 2) * TCW + QCW],
                         x.AR[:, oc * NQ + QCW * tc:oc * NQ + QCW * tc + QCW], AF.Square,
                         inc=self.inc("sqd"))
                self.wait("tensor", "sqd", self.cnt("sqd"))
                self.mm(x.PS_B[0:1, 512:512 + QCW], x.csb["ones_c128"][:],
                        x.sqb[:, (oc % 2) * TCW:(oc % 2) * TCW + QCW],
                        start=(oc == 0), stop=(oc == NDC - 1), inc=self.inc("sqf"))
            self._ln_rows(x.PS_B, x.rows, QCW, srhs=("bst", self.cnt("bst")),
                          ssq=("sqf", self.cnt("sqf")))
            self.wait("gpsimd", "st0", self.cnt("st0"))
            self.wait("gpsimd", "bbc", self.cnt("bbc"))
            self.dma("gpsimd", x.rowscr[0:1, 0:QCW], x.rows[0:1, 1024:1024 + QCW], "bbc")
            self.dma("gpsimd", x.rowscr[1:2, 0:QCW], x.rows[0:1, 1536:1536 + QCW], "bbc")
            self.wait("gpsimd", "bbc", self.cnt("bbc"))
            self.dma("gpsimd", x.RBt[:, 0:QCW],
                     x.rowscr[0:1, 0:QCW].broadcast_to((128, QCW)), "bbc")
            self.dma("gpsimd", x.MBt[:, 0:QCW],
                     x.rowscr[1:2, 0:QCW].broadcast_to((128, QCW)), "bbc")
            BBC = self.cnt("bbc")
            for oc in range(NDC):
                kx = tc * NDC + oc
                self.wait("sync", "frhs", self.cnt("frhs"))
                slotx = "fmm%d" % (kx % RR)
                self.dma("sync", x.rr[kx % RR][:, 0:QCW],
                         x.x2d[128 * oc:128 * oc + 128, QCW * tc:QCW * tc + QCW], slotx)
                self.wait("vector", slotx, self.cnt(slotx))
                self.wait("vector", "bbc", BBC)
                if out_dma:
                    self.wait("vector", "outw", out_dma[-1])
                arc = x.AR[:, oc * NQ + QCW * tc:oc * NQ + QCW * tc + QCW]
                self.vo("tt", x.gs_sb[:, 0:QCW], arc, x.RBt[:, 0:QCW], ALU.mult)
                self.vo("tt", x.gs_sb[:, 0:QCW], x.gs_sb[:, 0:QCW], x.MBt[:, 0:QCW],
                        ALU.add)
                self.tt(x.gs_sb[:, TCW:TCW + QCW],
                        x.gs_sb[:, 0:QCW], x.rr[kx % RR][:, 0:QCW], ALU.add,
                        inc=self.inc("frhs"))
                self.wait("vector", "frhs", self.cnt("frhs"))
                self.wait("gpsimd", "frhs", self.cnt("frhs"))
                self.dma("gpsimd", x.outD[128 * oc:128 * oc + 128, QCW * tc:QCW * tc + QCW],
                         x.gs_sb[:, TCW:TCW + QCW], "outw")
                out_dma.append(self.cnt("outw"))
        self.wait("gpsimd", "outw", self.cnt("outw"))



def stage_core_inputs(cfg, core, full):
    S, DM = cfg["S"], cfg["DM"]
    b, half = core // 2, core % 2
    x = np.asarray(full["L0_attn_x"][b], dtype=np.float32)
    NQ = S // 2
    xr = x if half == 0 else np.concatenate([x[NQ:], x[:NQ]], axis=0)
    d = {
        "xbT": np.ascontiguousarray(xr.T),
        "wqkvT": np.ascontiguousarray(np.asarray(full["L0_attn_qkv_w"], np.float32).T),
        "bqkv": np.ascontiguousarray(np.asarray(full["L0_attn_qkv_b"], np.float32).reshape(-1, 1)),
        "wprojT": np.ascontiguousarray(np.asarray(full["L0_attn_proj_w"], np.float32).T),
        "bproj": np.ascontiguousarray(np.asarray(full["L0_attn_proj_b"], np.float32).reshape(-1, 1)),
        "dwcw": np.ascontiguousarray(np.asarray(full["L0_attn_dwc_w"], np.float32).reshape(DM, 3)),
        "dwcb": np.ascontiguousarray(np.asarray(full["L0_attn_dwc_b"], np.float32).reshape(-1, 1)),
        "gatewT": np.ascontiguousarray(np.asarray(full["L0_ffn_gate_w"], np.float32).T),
        "wgT": np.ascontiguousarray(np.asarray(full["L0_ffn_wg"], np.float32).transpose(0, 2, 1)),
        "wuT": np.ascontiguousarray(np.asarray(full["L0_ffn_wu"], np.float32).transpose(0, 2, 1)),
        "wdT": np.ascontiguousarray(np.asarray(full["L0_ffn_wd"], np.float32).transpose(0, 2, 1)),
        "sgT": np.ascontiguousarray(np.asarray(full["L0_ffn_sg"], np.float32).T),
        "suT": np.ascontiguousarray(np.asarray(full["L0_ffn_su"], np.float32).T),
        "sdT": np.ascontiguousarray(np.asarray(full["L0_ffn_sd"], np.float32).T),
        "masks": np.array([[1.0 if half == 1 else 0.0,
                            1.0 if half == 0 else 0.0]], dtype=np.float32),
    }
    d.update(host_consts(cfg["LN_EPS"]))
    return d


_PROG_CACHE = {}


def kernel(**inputs):
    cfg = FULL_CFG
    if "full" not in _PROG_CACHE:
        _PROG_CACHE["full"] = Builder(cfg).build()
    nc = _PROG_CACHE["full"]
    in_maps = [stage_core_inputs(cfg, c, inputs) for c in range(8)]
    res = run_bass_kernel_spmd(nc, in_maps, core_ids=list(range(8)))
    S, DM = cfg["S"], cfg["DM"]
    NQ = S // 2
    out = np.empty((4, S, DM), dtype=np.float32)
    for c in range(8):
        b, half = c // 2, c % 2
        o = np.asarray(res.results[c]["out"])
        if half == 0:
            out[b, 0:NQ, :] = o.T
        else:
            out[b, NQ:S, :] = o.T
    return out


# version 14 (final)
# speedup vs baseline: 2.2755x; 1.0638x over previous
import math
from contextlib import ExitStack

import ml_dtypes
import numpy as np
import concourse.bass as bass
import concourse.mybir as mybir
from concourse.bass_utils import run_bass_kernel_spmd

F32 = mybir.dt.float32
F32R = mybir.dt.float32r
BF16 = mybir.dt.bfloat16
AF = mybir.ActivationFunctionType
ALU = mybir.AluOpType

LOG_EPS = float(np.log(1e-6))

FULL_CFG = dict(S=4096, DM=1024, H=16, DH=64, INTER=512, E=8,
                LN_EPS=1e-5, ACT_FRAC=0.375)

CONST_SHAPES = {
    "wbase": ([128, 192], BF16),
    "id128": ([128, 128], F32),
    "id1": ([1, 1], F32),
    "ones_c128": ([128, 1], F32),
    "halfones": ([128, 1], F32),
    "otherhalf": ([128, 1], F32),
    "logeps": ([1, 1], F32),
    "lneps": ([1, 1], F32),
}


def host_consts(ln_eps):
    wb = np.zeros((128, 192), dtype=np.float32)
    wb[0:64, 64] = 1.0
    wb[64:128, 128] = 1.0
    ho = np.zeros((128, 1), dtype=np.float32); ho[0:64] = 1.0
    oh = np.zeros((128, 1), dtype=np.float32); oh[64:128] = 1.0
    return {
        "wbase": wb.astype(ml_dtypes.bfloat16),
        "id128": np.eye(128, dtype=np.float32),
        "id1": np.ones((1, 1), dtype=np.float32),
        "ones_c128": np.ones((128, 1), dtype=np.float32),
        "halfones": ho,
        "otherhalf": oh,
        "logeps": np.full((1, 1), LOG_EPS, dtype=np.float32),
        "lneps": np.full((1, 1), ln_eps, dtype=np.float32),
    }


def pick_act_blocks(nblk, frac):
    n = int(round(frac * nblk))
    n = min(n, (nblk + 1) // 2)
    out = set()
    if n > 0:
        step = nblk / n
        prev = -2
        for i in range(n):
            b = min(nblk - 1, int(i * step))
            if b <= prev + 1:
                b = prev + 2
            if b >= nblk:
                break
            out.add(b)
            prev = b
    return out


class Builder:
    ENGINES = ("sync", "gpsimd", "tensor", "vector", "scalar")

    def __init__(self, cfg):
        c = dict(cfg)
        c["NQ"] = c["S"] // 2
        c["NBLK"] = c["S"] // 128
        c["NDC"] = c["DM"] // 128
        c["TCW"] = min(512, c["S"])
        c["NTC"] = c["S"] // c["TCW"]
        c["QCW"] = min(512, c["NQ"])
        c["NQC"] = max(1, c["NQ"] // c["QCW"])
        c["ICW"] = min(512, c["NQ"])
        c["NIC"] = max(1, c["NQ"] // c["ICW"])
        c["NICH"] = c["INTER"] // 128
        c["HP"] = c["H"] // 2
        c["ACT_B"] = pick_act_blocks(c["NBLK"], c["ACT_FRAC"])
        c["ACT_S"] = frozenset(s for s in range(64) if s % 8 in (1, 4, 6))
        self.c = c
        self.ops = {e: [] for e in self.ENGINES}
        self.ct = {}
        self.sem = {}
        self.pe_carry = []
        self.slotuse = {}

    def emit(self, eng, fn):
        self.ops[eng].append(fn)

    def wait(self, eng, s, v):
        if v > 0:
            self.emit(eng, lambda q, s=s, v=v: q.wait_ge(self.sem[s], v))

    def bump(self, s, n=1):
        self.ct[s] = self.ct.get(s, 0) + n
        return self.ct[s]

    def cnt(self, s):
        return self.ct.get(s, 0)

    def op(self, eng, fn, inc=None, carry_ok=False):
        if inc is None and carry_ok and self.pe_carry:
            inc = self.pe_carry.pop(0)
        if inc is not None:
            s, n = inc
            self.emit(eng, lambda q, fn=fn, s=s, n=n: fn(q).then_inc(self.sem[s], n))
        else:
            self.emit(eng, lambda q, fn=fn: fn(q))

    def inc(self, s, n=1):
        self.bump(s, n)
        return (s, n)

    def carry(self, s, n=1):
        self.bump(s, n)
        self.pe_carry.append((s, n))

    def release(self, eng, s, n=1):
        self.bump(s, n)
        self.emit(eng, lambda q, s=s, n=n: q.sem_inc(self.sem[s], n))

    def dma(self, eng, dst, src, s):
        v = self.bump(s, 16)
        self.emit(eng, lambda q, dst=dst, src=src, s=s: q.dma_start(out=dst, in_=src)
                  .then_inc(self.sem[s], 16))
        return v

    def mm(self, out, lhsT, rhs, start, stop, inc=None):
        self.op("tensor", lambda t, o=out, l=lhsT, r=rhs, st=start, sp=stop:
                t.matmul(out=o, lhsT=l, rhs=r, start=st, stop=sp), inc=inc,
                carry_ok=(inc is None))

    def transp(self, out, in_, ident, inc=None):
        self.op("tensor", lambda t, o=out, i=in_, d=ident:
                t.transpose(out=o, in_=i, identity=d), inc=inc, carry_ok=(inc is None))

    def act(self, out, in_, func, bias=0.0, scale=1.0, inc=None):
        self.op("scalar", lambda a, o=out, i=in_, f=func, b=bias, s=scale:
                a.activation(out=o, in_=i, func=f, bias=b, scale=s), inc=inc)

    def ts(self, out, in0, s1, op0, s2=None, op1=None, inc=None):
        if s2 is None:
            self.op("vector", lambda v, o=out, i=in0, a=s1, p=op0:
                    v.tensor_single_scalar(out=o, in_=i, scalar=a, op=p), inc=inc)
        else:
            self.op("vector", lambda v, o=out, i=in0, a=s1, b=s2, p=op0, p1=op1:
                    v.tensor_scalar(out=o, in0=i, scalar1=a, scalar2=b, op0=p, op1=p1),
                    inc=inc)

    def tt(self, out, in0, in1, alu, inc=None):
        self.op("vector", lambda v, o=out, a=in0, b=in1, p=alu:
                v.tensor_tensor(out=o, in0=a, in1=b, op=p), inc=inc)

    def stt(self, out, in0, scalar, in1, op0, op1, inc=None):
        self.op("vector", lambda v, o=out, a=in0, s=scalar, b=in1, p0=op0, p1=op1:
                v.scalar_tensor_tensor(out=o, in0=a, scalar=s, in1=b, op0=p0, op1=p1),
                inc=inc)

    def cp(self, out, in_, inc=None):
        self.op("vector", lambda v, o=out, i=in_: v.tensor_copy(out=o, in_=i), inc=inc)

    def mset(self, ap_, val, inc=None):
        self.op("vector", lambda v, a=ap_, x=val: v.memset(a, x), inc=inc)

    def red(self, out, in_, alu, inc=None):
        self.op("vector", lambda v, o=out, i=in_, p=alu:
                v.tensor_reduce(out=o, in_=i, axis=mybir.AxisListType.X, op=p), inc=inc)

    def recip(self, out, in_, inc=None):
        self.op("vector", lambda v, o=out, i=in_: v.reciprocal(out=o, in_=i), inc=inc)

    def vo(self, kind, *a, **kw):
        kw["inc"] = self.inc("vch")
        getattr(self, kind)(*a, **kw)
        self.wait("vector", "vch", self.cnt("vch"))

    def ao(self, *a, **kw):
        kw["inc"] = self.inc("ach")
        self.act(*a, **kw)
        self.wait("scalar", "ach", self.cnt("ach"))

    def build(self):
        c = self.c
        S, DM, H, DH, INTER, E = c["S"], c["DM"], c["H"], c["DH"], c["INTER"], c["E"]
        NQ, NBLK, NDC, TCW, NTC = c["NQ"], c["NBLK"], c["NDC"], c["TCW"], c["NTC"]
        QCW, NQC, ICW, NIC, NICH, HP = (c["QCW"], c["NQC"], c["ICW"], c["NIC"],
                                        c["NICH"], c["HP"])

        nc = bass.Bass("TRN2", target_bir_lowering=False, debug=False)
        self.nc = nc
        dram = {}

        def din(name, shape, dt_=F32):
            dram[name] = nc.dram_tensor(name, shape, dt_, kind="ExternalInput").ap()
            return dram[name]

        xbT = din("xbT", [DM, S])
        wqkvT = din("wqkvT", [DM, 3 * DM])
        bqkv = din("bqkv", [3 * DM, 1])
        wprojT = din("wprojT", [DM, DM])
        bprojD = din("bproj", [DM, 1])
        dwcwD = din("dwcw", [DM, 3])
        dwcbD = din("dwcb", [DM, 1])
        gatewT = din("gatewT", [DM, E])
        wgTD = din("wgT", [E, DM, INTER])
        wuTD = din("wuT", [E, DM, INTER])
        wdTD = din("wdT", [E, INTER, DM])
        sgTD = din("sgT", [DM, INTER])
        suTD = din("suT", [DM, INTER])
        sdTD = din("sdT", [INTER, DM])
        masksD = din("masks", [1, 2])
        for name, (shape, dt_) in CONST_SHAPES.items():
            din(name, shape, dt_)

        lnx1 = nc.dram_tensor("lnx1", [DM, S], F32, kind="Internal").ap()
        ln3d = nc.dram_tensor("ln3d", [DM, NQ], F32, kind="Internal").ap()
        x2d = nc.dram_tensor("x2d", [DM, NQ], F32, kind="Internal").ap()
        rowscr = nc.dram_tensor("rowscr", [8, max(S, 2048)], F32, kind="Internal").ap()
        cwd = nc.dram_tensor("cwd", [E, NQ], F32, kind="Internal").ap()
        outD = nc.dram_tensor("out", [DM, NQ], F32, kind="ExternalOutput").ap()

        es = ExitStack()
        with es:
            sbuf = lambda name, shape, dt_=F32: es.enter_context(
                nc.sbuf_tensor(name, shape, dt_))
            csb = {name: sbuf("c_" + name, shape, dt_)
                   for name, (shape, dt_) in CONST_SHAPES.items()}

            QT2 = sbuf("QT2", [128, NQ])
            nKT2 = sbuf("nKT2", [128, max(S // 2, 2 * NICH * 128)])
            VT = sbuf("VT", [128, max(S, NICH * NDC * 128, (c["DM"] // 128) * min(512, c["NQ"]))])
            vp = sbuf("vp", [128, max(NBLK * 65, NQ)])
            smallb = sbuf("smallb", [128, max(NBLK, 6 * c["E"] + 4) + 12])
            SMB = max(NBLK, 6 * c["E"] + 4)
            nskcol = smallb[:, 0:NBLK]
            mlrb = smallb[:, SMB + 8:SMB + 10]
            RN = 8
            ring = [sbuf(f"ring{i}", [128, NQ], BF16) for i in range(RN)]
            kern = sbuf("kern", [128, max(NQ, DM)])
            kern2 = sbuf("kern2", [128, NQ])
            AR = sbuf("AR", [128, NDC * NQ])
            wbuf = sbuf("wbuf", [128, 2 * DM])
            RR = 3
            rr = [sbuf(f"rr{i}", [128, TCW]) for i in range(RR)]
            bq2 = smallb[:, SMB:SMB + 1]
            bk2 = smallb[:, SMB + 1:SMB + 2]
            bv2 = smallb[:, SMB + 2:SMB + 3]
            bpj = smallb[:, SMB + 3:SMB + 4]
            dwc_s = smallb[:, SMB + 4:SMB + 7]
            dwcb_s = smallb[:, SMB + 7:SMB + 8]
            rowT_t = sbuf("rowT", [128, max(S, 4096)])
            nskrow = rowT_t[0:1, 0:S]
            rows = rowT_t[32:33, 0:4096]
            epsp = rowT_t[64:65, 0:NQ]
            recrow = rowT_t[64:65, 2048:2048 + NQ]
            ov1 = sbuf("ov1", [128, max(NQ, 2 * TCW)])
            RBt = ov1[:, 0:TCW]
            MBt = ov1[:, TCW:2 * TCW]
            OV2W = max(NQ, 4 * TCW)
            ov2 = sbuf("ov2", [128, OV2W + 2])
            sqb = ov2[:, 0:2 * TCW]
            gs_sb = ov2[:, 2 * TCW:4 * TCW]
            NG = NQ // 128
            cwtm = kern[:][:, NDC * E:NDC * E + NG * E]
            scx = smallb[:, :]

            PS_A = es.enter_context(nc.psum_tensor("PS_A", [128, 2048], F32))
            PS_B = es.enter_context(nc.psum_tensor("PS_B", [128, 2048], F32))

            semnames = ["cst", "x1rhs", "x2rhs", "sqd", "sqf",
                        "x1mm0", "x1mm1", "x1mm2", "x1mm3",
                        "qmm0", "qmm1", "qmm2", "qmm3",
                        "fmm0", "fmm1", "fmm2", "fmm3",
                        "lnxh0", "lnxh1",
                        "st0", "bc0", "lnxw", "lnxc",
                        "qw", "qrhs", "qgd", "qev",
                        "prep_pe", "prep_dve", "prep_act",
                        "tra", "tfa", "trv", "tfv",
                        "kdn", "evd", "pvd", "pad", "pbf", "p3d", "p3b",
                        "vch", "ach", "bgo", "x2w",
                        "bdw", "bmm", "bev", "bst", "bbc", "cwt", "cwr",
                        "fw", "frhs", "fgd", "fhp", "fpd", "fyw", "l4",
                        "outw"]
            for s in semnames:
                self.sem[s] = es.enter_context(nc.semaphore(s))

            ctx = dict(csb=csb, mlrb=mlrb, QT2=QT2, nKT2=nKT2, VT=VT, vp=vp,
                       nskcol=nskcol, nskrow=nskrow, epsp=epsp, recrow=recrow, ring=ring, NG=NG,
                       kern=kern, kern2=kern2, AR=AR, wbuf=wbuf, rr=rr, RR=RR,
                       bq2=bq2, bk2=bk2, bv2=bv2, bpj=bpj, dwc_s=dwc_s,
                       dwcb_s=dwcb_s, ov1=ov1, ov2=ov2, OV2W=OV2W,
                       rows=rows, RBt=RBt, MBt=MBt, sqb=sqb,
                       cwtm=cwtm, gs_sb=gs_sb, scx=scx, gw_sb=kern[:][:, 0:NDC * E],
                       PS_A=PS_A, PS_B=PS_B,
                       xbT=xbT, wqkvT=wqkvT, bqkv=bqkv, wprojT=wprojT,
                       bprojD=bprojD, dwcwD=dwcwD, dwcbD=dwcbD, gatewT=gatewT,
                       wgTD=wgTD, wuTD=wuTD, wdTD=wdTD, sgTD=sgTD, suTD=suTD,
                       sdTD=sdTD, masksD=masksD, lnx1=lnx1, ln3d=ln3d, x2d=x2d,
                       rowscr=rowscr, outD=outD, cwd=cwd)
            self.x = type("X", (), ctx)
            self.dram = dram

            phases = self.c.get("PHASES", "0ABC")
            self._consts()
            if "0" in phases:
                self._phase0()
            if "A" in phases:
                self._phase_a()
            if "B" in phases:
                self._phase_b()
            if "C" in phases:
                self._phase_c()
            if "C" not in phases:
                for s_ in ("bgo", "vch", "ach"):
                    self.wait("gpsimd", s_, self.cnt(s_))
                self.dma("gpsimd", outD[0:128, 0:min(512, NQ)],
                         self.x.gs_sb[:, 0:min(512, NQ)], "outw")
                self.wait("gpsimd", "outw", self.cnt("outw"))

            block = es.enter_context(nc.Block())

            @block.sync
            def _(q):
                for fn in self.ops["sync"]:
                    fn(q)

            @block.gpsimd
            def _(q):
                for fn in self.ops["gpsimd"]:
                    fn(q)

            @block.tensor
            def _(q):
                for fn in self.ops["tensor"]:
                    fn(q)

            @block.vector
            def _(q):
                for fn in self.ops["vector"]:
                    fn(q)

            @block.scalar
            def _(q):
                for fn in self.ops["scalar"]:
                    fn(q)

        return nc

    def _consts(self):
        x = self.x
        for name in CONST_SHAPES:
            self.dma("gpsimd", x.csb[name][:], self.dram[name], "cst")
        self.dma("gpsimd", x.rowscr[7:8, 0:2], x.masksD, "cst")
        self.wait("gpsimd", "cst", self.cnt("cst"))
        self.dma("gpsimd", x.mlrb, x.rowscr[7:8, 0:2].broadcast_to((128, 2)), "cst")
        CST = self.cnt("cst")
        for e in ("tensor", "vector", "scalar", "sync"):
            self.wait(e, "cst", CST)

    def _phase0(self):
        x, c = self.x, self.c
        DM, NDC, TCW, NTC, RR = c["DM"], c["NDC"], c["TCW"], c["NTC"], 3
        rel = []
        for tc in range(NTC):
            for ci in range(NDC):
                k = tc * 2 * NDC + ci
                if k >= RR:
                    self.wait("sync", *rel[k - RR])
                slot = "x1mm%d" % (k % RR)
                self.dma("sync", x.rr[k % RR][:].bitcast(F32R),
                         x.xbT[128 * ci:128 * ci + 128,
                               TCW * tc:TCW * tc + TCW].bitcast(F32R), slot)
                self.wait("scalar", slot, self.cnt(slot))
                kk = tc * NDC + ci
                if kk >= 2:
                    self.wait("scalar", "x1rhs", kk - 1)
                rel.append(None)
                self.act(x.sqb[:, (ci % 2) * TCW:(ci % 2) * TCW + TCW], x.rr[k % RR][:],
                         AF.Square, inc=self.inc("sqd"))
                self.wait("tensor", slot, self.cnt(slot))
                if ci == 0:
                    self.wait("tensor", "st0", self.cnt("st0"))
                self.mm(x.PS_A[0:1, 0:TCW], x.csb["ones_c128"][:], x.rr[k % RR][:],
                        start=(ci == 0), stop=(ci == NDC - 1), inc=None)
                self.wait("tensor", "sqd", self.cnt("sqd"))
                self.mm(x.PS_A[0:1, 512:512 + TCW], x.csb["ones_c128"][:],
                        x.sqb[:, (ci % 2) * TCW:(ci % 2) * TCW + TCW],
                        start=(ci == 0), stop=(ci == NDC - 1), inc=self.inc("x1rhs"))
                rel[k] = ("x1rhs", self.cnt("x1rhs"))
            self._ln_rows(x.PS_A, x.rows, TCW, srhs=("x1rhs", self.cnt("x1rhs")))
            self.wait("gpsimd", "st0", self.cnt("st0"))
            self.wait("gpsimd", "bc0", self.cnt("bc0"))
            self.dma("gpsimd", x.rowscr[0:1, 0:TCW], x.rows[0:1, 1024:1024 + TCW], "bc0")
            self.dma("gpsimd", x.rowscr[1:2, 0:TCW], x.rows[0:1, 1536:1536 + TCW], "bc0")
            self.wait("gpsimd", "bc0", self.cnt("bc0"))
            self.dma("gpsimd", x.RBt, x.rowscr[0:1, 0:TCW].broadcast_to((128, TCW)), "bc0")
            self.dma("gpsimd", x.MBt, x.rowscr[1:2, 0:TCW].broadcast_to((128, TCW)), "bc0")
            BC = self.cnt("bc0")
            for ci in range(NDC):
                k2 = tc * 2 * NDC + NDC + ci
                if k2 >= RR:
                    self.wait("sync", *rel[k2 - RR])
                slot2 = "x1mm%d" % (k2 % RR)
                self.dma("sync", x.rr[k2 % RR][:].bitcast(F32R),
                         x.xbT[128 * ci:128 * ci + 128,
                               TCW * tc:TCW * tc + TCW].bitcast(F32R), slot2)
                self.wait("vector", slot2, self.cnt(slot2))
                self.wait("vector", "bc0", BC)
                kk2 = tc * NDC + ci
                half = ci % 2
                if kk2 >= 2:
                    self.wait("vector", "lnxh%d" % half, self.cnt("lnxh%d" % half))
                self.vo("tt", x.gs_sb[:, half * TCW:half * TCW + TCW],
                        x.rr[k2 % RR][:], x.RBt, ALU.mult)
                self.release("vector", "x2rhs")
                rel.append(("x2rhs", self.cnt("x2rhs")))
                self.vo("tt", x.gs_sb[:, half * TCW:half * TCW + TCW],
                        x.gs_sb[:, half * TCW:half * TCW + TCW], x.MBt, ALU.add)
                self.release("vector", "lnxw")
                self.wait("gpsimd", "lnxw", self.cnt("lnxw"))
                self.dma("gpsimd", x.lnx1[128 * ci:128 * ci + 128, TCW * tc:TCW * tc + TCW],
                         x.gs_sb[:, half * TCW:half * TCW + TCW], "lnxh%d" % half)
        self.wait("sync", "lnxh0", self.cnt("lnxh0"))
        self.wait("sync", "lnxh1", self.cnt("lnxh1"))

    def _ln_rows(self, PS, rows, W, srhs=None, ssq=None):
        DM = self.c["DM"]
        if srhs:
            self.wait("vector", srhs[0], srhs[1])
        if ssq:
            self.wait("vector", ssq[0], ssq[1])
        self.vo("ts", rows[0:1, 0:W], PS[0:1, 0:W], 1.0 / DM, ALU.mult)
        self.vo("ts", rows[0:1, 512:512 + W], PS[0:1, 512:512 + W], 1.0 / DM, ALU.mult)
        self.wait("scalar", "vch", self.cnt("vch"))
        self.ao(rows[0:1, 2048:2048 + W], rows[0:1, 0:W], AF.Square)
        self.wait("vector", "ach", self.cnt("ach"))
        self.vo("tt", rows[0:1, 2560:2560 + W], rows[0:1, 512:512 + W],
                rows[0:1, 2048:2048 + W], ALU.subtract)
        self.wait("scalar", "vch", self.cnt("vch"))
        self.ao(rows[0:1, 2048:2048 + W], rows[0:1, 2560:2560 + W], AF.Ln,
                bias=self.x.csb["lneps"][:])
        self.ao(rows[0:1, 1024:1024 + W], rows[0:1, 2048:2048 + W], AF.Exp, scale=-0.5)
        self.wait("vector", "ach", self.cnt("ach"))
        self.vo("tt", rows[0:1, 2560:2560 + W], rows[0:1, 0:W], rows[0:1, 1024:1024 + W],
                ALU.mult)
        self.ts(rows[0:1, 1536:1536 + W], rows[0:1, 2560:2560 + W], -1.0, ALU.mult,
                inc=self.inc("st0"))

    def _phase_a(self):
        x, c = self.x, self.c
        S, DM, H, DH = c["S"], c["DM"], c["H"], c["DH"]
        NQ, NBLK, NDC, TCW, NTC = c["NQ"], c["NBLK"], c["NDC"], c["TCW"], c["NTC"]
        ICW, NIC = c["ICW"], c["NIC"]
        ACT_B = c["ACT_B"]
        HALF = S // 2
        RR = 3
        TA = [x.ring[i] for i in range(3)]
        TV = [x.ring[3 + i] for i in range(5)]
        NQCH = max(1, NQ // TCW)
        QW_LAST = NQ - TCW * (NQCH - 1) if NQ >= TCW else NQ

        for h in range(self.c.get("HLIM", H)):
            hp, hq = h // 2, h % 2
            even = (hq == 0)

            self.wait("gpsimd", "qev", self.cnt("qev"))
            for ci in range(NDC):
                src = x.wqkvT[128 * ci:128 * ci + 128, DH * h:DH * h + DH].bitcast(F32R)
                self.dma("gpsimd", x.wbuf[:, 128 * ci:128 * ci + 64].bitcast(F32R), src, "qw")
                self.dma("gpsimd", x.wbuf[:, 128 * ci + 64:128 * ci + 128].bitcast(F32R), src, "qw")
            for ci in range(NDC):
                src = x.wqkvT[128 * ci:128 * ci + 128, DM + DH * h:DM + DH * h + DH].bitcast(F32R)
                self.dma("gpsimd", x.wbuf[:, DM + 128 * ci:DM + 128 * ci + 64].bitcast(F32R), src, "qw")
                self.dma("gpsimd", x.wbuf[:, DM + 128 * ci + 64:DM + 128 * ci + 128].bitcast(F32R), src, "qw")
            if even:
                self.wait("gpsimd", "pad", h)
                for ci in range(NDC):
                    src = x.wqkvT[128 * ci:128 * ci + 128,
                                  2 * DM + 128 * hp:2 * DM + 128 * hp + 128]
                    self.dma("gpsimd", x.kern[:, 128 * ci:128 * ci + 128].bitcast(F32R),
                             src.bitcast(F32R), "qw")
            self.dma("gpsimd", x.bq2[0:64, :], x.bqkv[DH * h:DH * h + DH, :], "qw")
            self.dma("gpsimd", x.bq2[64:128, :], x.bqkv[DH * h:DH * h + DH, :], "qw")
            self.dma("gpsimd", x.bk2[0:64, :], x.bqkv[DM + DH * h:DM + DH * h + DH, :], "qw")
            self.dma("gpsimd", x.bk2[64:128, :], x.bqkv[DM + DH * h:DM + DH * h + DH, :], "qw")
            if even:
                self.dma("gpsimd", x.bv2,
                         x.bqkv[2 * DM + 128 * hp:2 * DM + 128 * hp + 128, :], "qw")
            self.wait("gpsimd", "bgo", self.cnt("bgo"))
            self.dma("gpsimd", x.dwc_s[64 * hq:64 * hq + 64, :],
                     x.dwcwD[DH * h:DH * h + DH, :], "qw")
            self.dma("gpsimd", x.dwcb_s[64 * hq:64 * hq + 64, :],
                     x.dwcbD[DH * h:DH * h + DH, :], "qw")
            QW = self.cnt("qw")

            self.wait("tensor", "qw", QW)
            for tc in range(NTC):
                own = tc < NQCH
                qw_ = TCW if (tc < NQCH - 1 or NQ >= TCW * NQCH) else QW_LAST
                if NQ < TCW:
                    qw_ = NQ
                if tc > 0:
                    self.wait("tensor", "qev", self.cnt("qev"))
                for ci in range(NDC):
                    k = (h * NTC + tc) * NDC + ci
                    if k >= RR:
                        self.wait("sync", "qrhs", k - RR + 1)
                    slot = "qmm%d" % (k % RR)
                    self.dma("sync", x.rr[k % RR][:].bitcast(F32R),
                             x.lnx1[128 * ci:128 * ci + 128,
                                    TCW * tc:TCW * tc + TCW].bitcast(F32R), slot)
                    self.wait("tensor", slot, self.cnt(slot))
                    first, last = (ci == 0), (ci == NDC - 1)
                    roles = [("K", x.PS_A[:, 512:512 + TCW], x.wbuf[:, DM:2 * DM], TCW)]
                    if even:
                        roles.append(("V", x.PS_A[:, 1024:1024 + TCW],
                                      x.kern[:, 0:DM], TCW))
                    if own:
                        roles.append(("Q", x.PS_A[:, 0:qw_], x.wbuf[:, 0:DM], qw_))
                    for ri, (rn, pslice, wslab, w_) in enumerate(roles):
                        lastrole = ri == len(roles) - 1
                        inc = self.inc("qrhs") if lastrole else None
                        self.mm(pslice, wslab[:, 128 * ci:128 * ci + 128].bitcast(F32R),
                                x.rr[k % RR][:, 0:w_].bitcast(F32R),
                                start=first, stop=last, inc=inc)
                self.wait("vector", "qrhs", self.cnt("qrhs"))
                nblk_tc = TCW // 128
                g0 = tc * nblk_tc
                ktop = x.nKT2[0:64, 64 * g0:64 * (g0 + nblk_tc)].bitcast(F32R)
                ktop = ktop.rearrange("p (g s) -> p g s", g=nblk_tc)
                src_t = x.PS_A[0:64, 512:512 + TCW].rearrange(
                    "p (g u s) -> p g u s", g=nblk_tc, u=2)[:, :, 0, :]
                self.vo("ts", ktop, src_t, x.bk2[0:64, :], ALU.add, -1.0, ALU.mult)
                kbot = x.nKT2[64:128, 64 * g0:64 * (g0 + nblk_tc)].bitcast(F32R)
                kbot = kbot.rearrange("p (g s) -> p g s", g=nblk_tc)
                src_b = x.PS_A[64:128, 512:512 + TCW].rearrange(
                    "p (g u s) -> p g u s", g=nblk_tc, u=2)[:, :, 1, :]
                self.vo("ts", kbot, src_b, x.bk2[64:128, :], ALU.add, -1.0, ALU.mult)
                if even:
                    self.vo("ts", x.VT[:, TCW * tc:TCW * tc + TCW].bitcast(F32R),
                            x.PS_A[:, 1024:1024 + TCW], x.bv2, ALU.add)
                if own:
                    self.vo("ts", x.QT2[:, TCW * tc:TCW * tc + qw_], x.PS_A[:, 0:qw_],
                            x.bq2, ALU.add)
                self.emit("vector", lambda q: q.sem_inc(self.sem["qev"], 1))
                self.bump("qev")

            if self.c.get("ASTOP") == "qkv":
                return
            self.wait("tensor", "qev", self.cnt("qev"))
            for q0 in range(0, NQ, 512):
                qw2 = min(512, NQ - q0)
                self.mm(x.PS_A[0:1, q0:q0 + qw2], x.csb["halfones"][:],
                        x.QT2[:, q0:q0 + qw2], True, True,
                        inc=self.inc("prep_pe") if q0 + qw2 >= NQ else None)
            self.wait("scalar", "prep_pe", self.cnt("prep_pe"))
            self.act(x.epsp, x.PS_A[0:1, 0:NQ], AF.Exp, bias=x.csb["logeps"][:],
                     scale=-1.0 / 16.0, inc=self.inc("prep_act"))
            self.wait("tensor", "prep_act", self.cnt("prep_act"))
            for q0 in range(0, HALF, 512):
                qw2 = min(512, HALF - q0)
                self.mm(x.PS_A[0:1, q0:q0 + qw2], x.csb["halfones"][:],
                        x.nKT2[:, q0:q0 + qw2], True, True,
                        inc=self.inc("prep_pe") if q0 + qw2 >= HALF else None)
            self.wait("vector", "prep_pe", self.cnt("prep_pe"))
            ev_row = x.nskrow[0:1, :].rearrange("p (g u s) -> p g u s", g=NBLK, u=2)
            self.ts(ev_row[:, :, 0, :], x.PS_A[0:1, 0:HALF], 1.0 / 16.0, ALU.mult,
                    inc=self.inc("prep_dve"))
            self.wait("tensor", "prep_dve", self.cnt("prep_dve"))
            for q0 in range(0, HALF, 512):
                qw2 = min(512, HALF - q0)
                self.mm(x.PS_A[0:1, q0:q0 + qw2], x.csb["otherhalf"][:],
                        x.nKT2[:, q0:q0 + qw2], True, True,
                        inc=self.inc("prep_pe") if q0 + qw2 >= HALF else None)
            self.wait("vector", "prep_pe", self.cnt("prep_pe"))
            self.ts(ev_row[:, :, 1, :], x.PS_A[0:1, 0:HALF], 1.0 / 16.0, ALU.mult,
                    inc=self.inc("prep_dve"))
            self.wait("tensor", "prep_dve", self.cnt("prep_dve"))
            for b in range(NBLK):
                self.transp(x.PS_A[:, b:b + 1], x.nskrow[0:1, 128 * b:128 * b + 128],
                            x.csb["id1"][:],
                            inc=self.inc("prep_pe") if b == NBLK - 1 else None)
            self.wait("vector", "prep_pe", self.cnt("prep_pe"))
            self.cp(x.nskcol, x.PS_A[:, 0:NBLK], inc=self.inc("prep_dve"))
            self.wait("vector", "prep_dve", self.cnt("prep_dve"))
            vsrc = x.VT[64 * hq:64 * hq + 64, :]
            ngrp = (NBLK + 7) // 8
            for g in range(ngrp):
                nb = min(8, NBLK - 8 * g)
                self.wait("tensor", "prep_dve", self.cnt("prep_dve"))
                for j in range(nb):
                    b = 8 * g + j
                    self.transp(x.PS_A[:, 64 * j:64 * j + 64],
                                vsrc[:, 128 * b:128 * b + 128],
                                x.csb["id128"][64 * hq:64 * hq + 64, 64 * hq:64 * hq + 64],
                                inc=self.inc("prep_pe") if j == nb - 1 else None)
                self.wait("vector", "prep_pe", self.cnt("prep_pe"))
                dst = x.vp[:, 65 * 8 * g:65 * (8 * g + nb)]
                dst = dst.rearrange("p (b w) -> p b w", b=nb)[:, :, 0:64].bitcast(F32R)
                src = x.PS_A[:, 0:64 * nb].rearrange("p (b w) -> p b w", b=nb)
                self.cp(dst, src, inc=self.inc("prep_dve"))
                self.wait("vector", "prep_dve", self.cnt("prep_dve"))
            if h == 0:
                ones_ap = x.vp[:, :].rearrange("p (b w) -> p b w", b=NBLK)[:, :, 64:65]
                ones_in = x.csb["id128"][:, 0:NBLK].rearrange("p (b w) -> p b w", w=1)
                self.ts(ones_ap.bitcast(F32R), ones_in, 0.0, ALU.mult,
                        1.0, ALU.add, inc=self.inc("prep_dve"))
            PREP = self.cnt("prep_dve")
            if self.c.get("ASTOP") == "prep":
                return

            headblk = h * NBLK
            na2 = self.cnt("tra")
            nv2 = self.cnt("trv")
            fa0 = self.cnt("tfa")
            fv0 = self.cnt("tfv")
            ACT_S = c["ACT_S"]

            NDVE = 64 - len(c["ACT_S"])

            def emit_produce(eng, b, s, kp, Tring, s_rdy, s_free, f0):
                R = len(Tring)
                if kp == 0:
                    self.wait(eng, "prep_dve", PREP)
                elif kp >= R:
                    pk = kp - R
                    if eng == "vector" and pk % NDVE == NDVE - 1:
                        self.wait(eng, "evd", headblk + pk // NDVE + 1)
                    else:
                        if eng == "vector":
                            nrel = pk + 1 - pk // NDVE
                        else:
                            nrel = pk + 1
                        self.wait(eng, s_free, f0 + nrel)
                tile = Tring[kp % R]
                inc = self.inc(s_rdy)
                if eng == "scalar":
                    self.act(tile[:, 0:NQ], x.QT2[:], AF.Relu,
                             bias=x.nKT2[:, 64 * b + s:64 * b + s + 1], inc=inc)
                else:
                    self.ts(tile[:, 0:NQ], x.QT2[:],
                            x.nKT2[:, 64 * b + s:64 * b + s + 1],
                            ALU.add, 0.0, ALU.max, inc=inc)

            def emit_evac(b):
                gbi = headblk + b
                if b <= 1:
                    if b == 0:
                        self.wait("scalar", "prep_dve", PREP)
                    self.wait("scalar", "pad", h)
                else:
                    self.wait("scalar", "pvd", gbi - (h + 1))
                self.wait("scalar", "kdn", gbi + 1)
                dst = x.kern if b % 2 == 0 else x.kern2
                self.act(dst[:, 0:NQ].bitcast(F32R), x.PS_A[:, 0:NQ], AF.Exp,
                         bias=x.nskcol[:, b:b + 1],
                         scale=-1.0 / 8.0, inc=self.inc("evd"))

            actS = sorted(ACT_S)
            dveS = [s for s in range(64) if s not in ACT_S]

            ai = 0
            for g in range(NBLK):
                if g >= 1:
                    emit_evac(g - 1)
                for s in actS:
                    emit_produce("scalar", g, s, ai, TA, "tra", "tfa", fa0)
                    ai += 1
            emit_evac(NBLK - 1)

            vi = 0
            for g in range(NBLK):
                for s in dveS:
                    emit_produce("vector", g, s, vi, TV, "trv", "tfv", fv0)
                    vi += 1

            pa = 0
            pv = 0
            for b in range(NBLK):
                gbi = headblk + b
                for s in range(64):
                    isact = s in ACT_S
                    ringp = TA if isact else TV
                    s_rdy = "tra" if isact else "trv"
                    kp = pa if isact else pv
                    base = na2 if isact else nv2
                    self.wait("tensor", s_rdy, base + kp + 1)
                    if s == 0:
                        if gbi > 0:
                            self.wait("tensor", "evd", gbi)
                        if b == 0:
                            self.wait("tensor", "prep_dve", PREP)
                    tile = ringp[kp % len(ringp)]
                    for ic in range(NIC):
                        inc = None
                        if ic == NIC - 1:
                            if s == 63:
                                inc = self.inc("kdn")
                            else:
                                inc = self.inc("tfa" if isact else "tfv")
                        self.mm(x.PS_A[:, 512 * ic:512 * ic + ICW],
                                x.csb["wbase"][:, 64 - s:192 - s],
                                tile[:, ICW * ic:ICW * ic + ICW],
                                start=(s == 0), stop=(s == 63), inc=inc)
                    if isact:
                        pa += 1
                    else:
                        pv += 1
                if b > 0:
                    self._emit_av(b - 1, h)
            self._emit_av(NBLK - 1, h)

            if self.c.get("ASTOP") == "dist":
                return
            b0 = 64 * hq
            rb_v = x.ov1[b0:b0 + 64, 0:NQ]
            lo_v = x.ov2[b0:b0 + 64, 0:NQ]
            e_v = x.ov2[b0:b0 + 64, x.OV2W:x.OV2W + 2]
            dwc_v = x.dwc_s[b0:b0 + 64, :]
            dwcb_v = x.dwcb_s[b0:b0 + 64, :]
            mlr_v = x.mlrb[b0:b0 + 64, :]
            dst = x.AR[b0:b0 + 64, NQ * hp:NQ * hp + NQ]
            self.wait("vector", "pad", h + 1)
            self.vo("tt", x.epsp, x.PS_B[64:65, 0:NQ], x.epsp, ALU.add)
            self.recip(x.recrow, x.epsp, inc=self.inc("p3d"))
            self.wait("gpsimd", "p3d", self.cnt("p3d"))
            self.wait("gpsimd", "p3b", self.cnt("p3b"))
            if h == 0:
                self.wait("gpsimd", "bc0", self.cnt("bc0"))
            self.dma("gpsimd", x.rowscr[2:3, 0:NQ], x.recrow, "p3b")
            self.wait("gpsimd", "p3b", self.cnt("p3b"))
            self.dma("gpsimd", rb_v, x.rowscr[2:3, 0:NQ].broadcast_to((64, NQ)), "p3b")
            if self.c.get("ASTOP") == "p3rec":
                return
            self.wait("vector", "p3b", self.cnt("p3b"))
            self.tt(dst, x.PS_B[0:64, 0:NQ], rb_v, ALU.mult, inc=self.inc("pbf"))
            self.wait("vector", "pbf", self.cnt("pbf"))
            if self.c.get("ASTOP") == "p3div":
                return
            v64 = x.VT[:][b0:b0 + 64, :]
            self.vo("ts", lo_v, v64[:, 0:NQ], dwc_v[:, 1:2], ALU.mult)
            self.vo("stt", lo_v[:, 1:NQ], v64[:, 0:NQ - 1], dwc_v[:, 0:1],
                    lo_v[:, 1:NQ], ALU.mult, ALU.add)
            self.vo("stt", lo_v[:, 0:NQ - 1], v64[:, 1:NQ], dwc_v[:, 2:3],
                    lo_v[:, 0:NQ - 1], ALU.mult, ALU.add)
            if self.c.get("ASTOP") == "p3conv":
                return
            self.vo("ts", e_v[:, 0:1], v64[:, S - 1:S], mlr_v[:, 0:1], ALU.mult)
            self.vo("stt", lo_v[:, 0:1], e_v[:, 0:1], dwc_v[:, 0:1],
                    lo_v[:, 0:1], ALU.mult, ALU.add)
            self.vo("ts", e_v[:, 1:2], v64[:, NQ:NQ + 1], mlr_v[:, 1:2], ALU.mult)
            self.vo("stt", lo_v[:, NQ - 1:NQ], e_v[:, 1:2], dwc_v[:, 2:3],
                    lo_v[:, NQ - 1:NQ], ALU.mult, ALU.add)
            if self.c.get("ASTOP") == "p3edge":
                return
            self.vo("ts", lo_v, lo_v, dwcb_v, ALU.add)
            if self.c.get("ASTOP") == "p3bias":
                return
            v_ = self.c.get("P3VAR", 0)
            if v_ == 1:
                self.tt(dst, dst, lo_v, ALU.add, inc=None)
                self.bump("bgo")
                self.release("vector", "bgo")
            elif v_ == 2:
                self.tt(x.gs_sb[:, 0:NQ] if NQ <= 2 * 512 else lo_v, dst, lo_v,
                        ALU.add, inc=self.inc("bgo"))
                self.wait("vector", "bgo", self.cnt("bgo"))
            else:
                self.tt(dst, dst, lo_v, ALU.add, inc=self.inc("bgo"))
                self.wait("vector", "bgo", self.cnt("bgo"))

    def _emit_av(self, b, h):
        x, c = self.x, self.c
        NQ, ICW, NIC, NBLK = c["NQ"], c["ICW"], c["NIC"], c["NBLK"]
        gbi = h * NBLK + b
        self.wait("tensor", "evd", gbi + 1)
        if b == 0:
            self.wait("tensor", "pbf", h)
        src = x.kern if b % 2 == 0 else x.kern2
        for ic in range(NIC):
            inc = None
            if ic == NIC - 1:
                inc = self.inc("pad") if b == NBLK - 1 else self.inc("pvd")
            self.mm(x.PS_B[0:65, 512 * ic:512 * ic + ICW],
                    x.vp[:, 65 * b:65 * b + 65].bitcast(F32R),
                    src[:, ICW * ic:ICW * ic + ICW].bitcast(F32R),
                    start=(b == 0), stop=(b == NBLK - 1), inc=inc)

    def _phase_b(self):
        x, c = self.x, self.c
        DM, NQ, NDC, QCW, NQC, E = c["DM"], c["NQ"], c["NDC"], c["QCW"], c["NQC"], c["E"]
        TCW = c["TCW"]
        RR = 3
        BGO = self.cnt("bgo")
        for e in ("tensor", "vector", "scalar", "gpsimd", "sync"):
            self.wait(e, "bgo", BGO)
        for ci in range(NDC):
            self.dma("gpsimd", x.gw_sb[:, E * ci:E * ci + E].bitcast(F32R),
                     x.gatewT[128 * ci:128 * ci + 128, :].bitcast(F32R), "bdw")
        GWD = self.cnt("bdw")
        slotV = lambda oc: x.VT[:][:, QCW * oc:QCW * oc + QCW]

        for tc in range(NQC):
            self.wait("tensor", "cwt", self.cnt("cwt"))
            self.wait("vector", "x2w", self.cnt("x2w"))
            for oc in range(NDC):
                self.wait("gpsimd", "bev", self.cnt("bev"))
                self.wait("gpsimd", "bdw", self.cnt("bdw"))
                for ci in range(NDC):
                    self.dma("gpsimd", x.wbuf[:, 128 * ci:128 * ci + 128].bitcast(F32R),
                             x.wprojT[128 * ci:128 * ci + 128,
                                      128 * oc:128 * oc + 128].bitcast(F32R),
                             "bdw")
                self.dma("gpsimd", x.bpj, x.bprojD[128 * oc:128 * oc + 128, :], "bdw")
                self.wait("tensor", "bdw", self.cnt("bdw"))
                slot = oc % 2
                key = ("b1", slot)
                if key in self.slotuse:
                    self.wait("tensor", "bev", self.slotuse[key])
                for ci in range(NDC):
                    self.mm(x.PS_A[:, 512 * slot:512 * slot + QCW],
                            x.wbuf[:, 128 * ci:128 * ci + 128],
                            x.AR[:, ci * NQ + QCW * tc:ci * NQ + QCW * tc + QCW],
                            start=(ci == 0), stop=(ci == NDC - 1),
                            inc=self.inc("bmm") if ci == NDC - 1 else None)
                self.wait("vector", "bmm", self.cnt("bmm"))
                self.ts(slotV(oc).bitcast(F32R),
                        x.PS_A[:, 512 * slot:512 * slot + QCW], x.bpj[:],
                        ALU.add, inc=self.inc("bev"))
                self.slotuse[("b1", slot)] = self.cnt("bev")
                self.wait("vector", "bev", self.cnt("bev"))
            if self.c.get("BSTOP") == "proj":
                return
            PREV = self.cnt("bev")
            self.wait("tensor", "bev", PREV)
            self.wait("scalar", "bev", PREV)
            for oc in range(NDC):
                kk = tc * NDC + oc
                if oc == 0:
                    self.wait("tensor", "st0", self.cnt("st0"))
                self.mm(x.PS_B[0:1, 0:QCW], x.csb["ones_c128"][:], slotV(oc),
                        start=(oc == 0), stop=(oc == NDC - 1),
                        inc=self.inc("bst") if oc == NDC - 1 else None)
                if kk >= 2:
                    self.wait("scalar", "sqf", self.cnt("sqf") - 1)
                self.act(x.sqb[:, (oc % 2) * TCW:(oc % 2) * TCW + QCW], slotV(oc),
                         AF.Square, inc=self.inc("sqd"))
                self.wait("tensor", "sqd", self.cnt("sqd"))
                self.mm(x.PS_B[0:1, 512:512 + QCW], x.csb["ones_c128"][:],
                        x.sqb[:, (oc % 2) * TCW:(oc % 2) * TCW + QCW],
                        start=(oc == 0), stop=(oc == NDC - 1), inc=self.inc("sqf"))
            if self.c.get("BSTOP") == "ln2stats":
                return
            self._ln_rows(x.PS_B, x.rows, QCW, srhs=("bst", self.cnt("bst")),
                          ssq=("sqf", self.cnt("sqf")))
            if self.c.get("BSTOP") == "ln2rows":
                return
            self.wait("gpsimd", "st0", self.cnt("st0"))
            self.wait("gpsimd", "bbc", self.cnt("bbc"))
            self.dma("gpsimd", x.rowscr[3:4, 0:QCW], x.rows[0:1, 1024:1024 + QCW], "bbc")
            self.dma("gpsimd", x.rowscr[4:5, 0:QCW], x.rows[0:1, 1536:1536 + QCW], "bbc")
            self.wait("gpsimd", "bbc", self.cnt("bbc"))
            self.dma("gpsimd", x.RBt[:, 0:QCW],
                     x.rowscr[3:4, 0:QCW].broadcast_to((128, QCW)), "bbc")
            self.dma("gpsimd", x.MBt[:, 0:QCW],
                     x.rowscr[4:5, 0:QCW].broadcast_to((128, QCW)), "bbc")
            BBC = self.cnt("bbc")
            self.wait("vector", "lnxh0", self.cnt("lnxh0"))
            for oc in range(NDC):
                kx = tc * NDC + oc
                self.wait("sync", "bev", self.cnt("bev"))
                slotx = "qmm%d" % (kx % RR)
                self.dma("sync", x.rr[kx % RR][:, 0:QCW].bitcast(F32R),
                         x.xbT[128 * oc:128 * oc + 128,
                               QCW * tc:QCW * tc + QCW].bitcast(F32R), slotx)
                self.wait("vector", slotx, self.cnt(slotx))
                self.wait("vector", "bbc", BBC)
                self.vo("tt", x.gs_sb[:, 0:QCW], slotV(oc), x.RBt[:, 0:QCW], ALU.mult)
                self.vo("tt", x.gs_sb[:, 0:QCW], x.gs_sb[:, 0:QCW], x.MBt[:, 0:QCW],
                        ALU.add)
                self.tt(slotV(oc).bitcast(F32R),
                        x.gs_sb[:, 0:QCW], x.rr[kx % RR][:, 0:QCW], ALU.add,
                        inc=self.inc("bev"))
                self.wait("vector", "bev", self.cnt("bev"))
                self.wait("gpsimd", "bev", self.cnt("bev"))
                self.dma("gpsimd", x.x2d[128 * oc:128 * oc + 128, QCW * tc:QCW * tc + QCW],
                         slotV(oc), "x2w")
                self.wait("tensor", "bev", self.cnt("bev"))
                if oc == 0:
                    self.wait("tensor", "st0", self.cnt("st0"))
                self.mm(x.PS_B[0:1, 1024:1024 + QCW], x.csb["ones_c128"][:], slotV(oc),
                        start=(oc == 0), stop=(oc == NDC - 1),
                        inc=self.inc("bst") if oc == NDC - 1 else None)
                self.wait("scalar", "bev", self.cnt("bev"))
                kk = tc * NDC + oc
                if kk >= 2:
                    self.wait("scalar", "sqf", self.cnt("sqf") - 1)
                self.act(x.sqb[:, (oc % 2) * TCW:(oc % 2) * TCW + QCW], slotV(oc),
                         AF.Square, inc=self.inc("sqd"))
                self.wait("tensor", "sqd", self.cnt("sqd"))
                self.mm(x.PS_B[0:1, 1536:1536 + QCW], x.csb["ones_c128"][:],
                        x.sqb[:, (oc % 2) * TCW:(oc % 2) * TCW + QCW],
                        start=(oc == 0), stop=(oc == NDC - 1), inc=self.inc("sqf"))
            if self.c.get("BSTOP") == "apply":
                return
            self._ln_rows(x.PS_B[:, 1024:2048], x.rows, QCW,
                          srhs=("bst", self.cnt("bst")), ssq=("sqf", self.cnt("sqf")))
            self.wait("gpsimd", "st0", self.cnt("st0"))
            self.wait("gpsimd", "bbc", self.cnt("bbc"))
            self.dma("gpsimd", x.rowscr[5:6, 0:QCW], x.rows[0:1, 1024:1024 + QCW], "bbc")
            self.dma("gpsimd", x.rowscr[6:7, 0:QCW], x.rows[0:1, 1536:1536 + QCW], "bbc")
            self.wait("gpsimd", "bbc", self.cnt("bbc"))
            self.dma("gpsimd", x.RBt[:, 0:QCW],
                     x.rowscr[5:6, 0:QCW].broadcast_to((128, QCW)), "bbc")
            self.dma("gpsimd", x.MBt[:, 0:QCW],
                     x.rowscr[6:7, 0:QCW].broadcast_to((128, QCW)), "bbc")
            BBC3 = self.cnt("bbc")
            if tc == 0:
                self.wait("tensor", "bdw", GWD)
            for oc in range(NDC):
                slot = oc % 2
                self.wait("vector", "bbc", BBC3)
                if tc * NDC + oc >= 2:
                    self.wait("vector", "lnxh%d" % slot, self.cnt("lnxh%d" % slot))
                self.vo("tt", x.gs_sb[:, slot * TCW:slot * TCW + QCW], slotV(oc),
                        x.RBt[:, 0:QCW], ALU.mult)
                self.tt(x.gs_sb[:, slot * TCW:slot * TCW + QCW],
                        x.gs_sb[:, slot * TCW:slot * TCW + QCW], x.MBt[:, 0:QCW],
                        ALU.add, inc=self.inc("bev"))
                self.wait("vector", "bev", self.cnt("bev"))
                self.wait("gpsimd", "bev", self.cnt("bev"))
                self.dma("gpsimd", x.ln3d[128 * oc:128 * oc + 128, QCW * tc:QCW * tc + QCW],
                         x.gs_sb[:, slot * TCW:slot * TCW + QCW], "lnxh%d" % slot)
                self.wait("tensor", "bev", self.cnt("bev"))
                for gt in range(QCW // 128):
                    self.mm(x.PS_A[:, 512 * gt:512 * gt + E],
                            x.gs_sb[:, slot * TCW + 128 * gt:slot * TCW + 128 * gt + 128],
                            x.gw_sb[:, E * oc:E * oc + E],
                            start=(oc == 0), stop=(oc == NDC - 1),
                            inc=self.inc("bmm")
                            if (oc == NDC - 1 and gt == QCW // 128 - 1) else None)
            if self.c.get("BSTOP") == "ln3apply":
                return
            self.wait("vector", "bmm", self.cnt("bmm"))
            for gt in range(QCW // 128):
                g_all = tc * (QCW // 128) + gt
                self._softmax_top2(x.PS_A[:, 512 * gt:512 * gt + E],
                                   x.cwtm[:, E * g_all:E * g_all + E])
            if self.c.get("BSTOP") == "softmax":
                return
        NG = NQ // 128
        self.wait("tensor", "cwt", self.cnt("cwt"))
        for e in range(E):
            if e >= 1:
                self.wait("tensor", "cwt", self.cnt("cwt"))
            for g in range(NG):
                self.mm(x.PS_B[0:1, 128 * g:128 * g + 128],
                        x.cwtm[:, E * g + e:E * g + e + 1],
                        x.csb["id128"][:], True, True,
                        inc=self.inc("bmm") if g == NG - 1 else None)
            if self.c.get("BSTOP") == "cw0t" and e == 0:
                return
            self.wait("vector", "bmm", self.cnt("bmm"))
            if e >= 1:
                self.wait("vector", "cwr", self.cnt("cwr"))
            self.cp(x.ov2[0:1, 0:NQ], x.PS_B[0:1, 0:NQ], inc=self.inc("cwt"))
            self.wait("vector", "cwt", self.cnt("cwt"))
            if self.c.get("BSTOP") == "cw0c" and e == 0:
                return
            self.wait("gpsimd", "cwt", self.cnt("cwt"))
            self.dma("gpsimd", x.cwd[e:e + 1, :], x.ov2[0:1, 0:NQ], "cwr")
            if self.c.get("BSTOP") == "cw0d" and e == 0:
                return
    
    def _softmax_top2(self, ps, cw_out):
        E = self.c["E"]
        x = self.x
        s = x.scx
        self.vo("red", s[:, 0:1], ps, ALU.max)
        self.vo("ts", s[:, 1:1 + E], ps, s[:, 0:1], ALU.subtract)
        self.wait("scalar", "vch", self.cnt("vch"))
        self.ao(s[:, 1 + E:1 + 2 * E], s[:, 1:1 + E], AF.Exp)
        self.wait("vector", "ach", self.cnt("ach"))
        self.vo("red", s[:, 1 + 2 * E:2 + 2 * E], s[:, 1 + E:1 + 2 * E], ALU.add)
        self.vo("recip", s[:, 2 + 2 * E:3 + 2 * E], s[:, 1 + 2 * E:2 + 2 * E])
        sm = s[:, 3 + 2 * E:3 + 3 * E]
        self.vo("ts", sm, s[:, 1 + E:1 + 2 * E], s[:, 2 + 2 * E:3 + 2 * E], ALU.mult)
        self.vo("red", s[:, 0:1], sm, ALU.max)
        eq = s[:, 3 + 3 * E:3 + 4 * E]
        self.vo("ts", eq, sm, s[:, 0:1], ALU.is_ge)
        msk = s[:, 3 + 4 * E:3 + 5 * E]
        self.vo("ts", msk, eq, -1e30, ALU.mult)
        self.vo("tt", msk, msk, sm, ALU.add)
        self.vo("red", s[:, 1:2], msk, ALU.max)
        ge = s[:, 3 + 5 * E:3 + 6 * E]
        self.vo("ts", ge, sm, s[:, 1:2], ALU.is_ge)
        self.tt(cw_out.bitcast(mybir.dt.float32r), sm, ge, ALU.mult,
                inc=self.inc("cwt"))
        self.wait("vector", "cwt", self.cnt("cwt"))

    def _phase_c(self):
        x, c = self.x, self.c
        DM, NQ, NDC, QCW, NQC = c["DM"], c["NQ"], c["NDC"], c["QCW"], c["NQC"]
        INTER, E, NICH, TCW = c["INTER"], c["E"], c["NICH"], c["TCW"]
        RR = 3
        CWB = x.QT2
        for eng in ("tensor", "vector", "scalar", "gpsimd", "sync"):
            self.wait(eng, "cwr", 16 * E)
        for eng in ("vector", "tensor", "sync", "gpsimd"):
            self.wait(eng, "x2w", self.cnt("x2w"))
            self.wait(eng, "lnxh0", self.cnt("lnxh0"))
            self.wait(eng, "lnxh1", self.cnt("lnxh1"))

        def wg_tile(ci, ic):
            f = (ci * NICH + ic) * 128
            return x.VT[:][:, f:f + 128].bitcast(F32R)

        VP_CAP = (max(c["NBLK"] * 65, NQ) // 128) * 128
        assert NICH * NDC * 128 <= VP_CAP + 2 * DM, "wu arena too small"

        def wu_tile(ci, ic):
            f = (ci * NICH + ic) * 128
            if f < VP_CAP:
                return x.vp[:][:, f:f + 128].bitcast(F32R)
            return x.wbuf[:, f - VP_CAP:f - VP_CAP + 128].bitcast(F32R)

        def wd_tile(half, ic):
            col = (half * NICH + ic) * 128
            return x.nKT2[:][:, col:col + 128].bitcast(F32R)

        def hp_tile(ic, w):
            return x.kern[:, 512 * ic:512 * ic + w].bitcast(F32R)

        for ei in range(-1, E):
            shared = ei < 0
            wgS = x.sgTD if shared else x.wgTD[ei]
            wuS = x.suTD if shared else x.wuTD[ei]
            wdS = x.sdTD if shared else x.wdTD[ei]
            self.wait("gpsimd", "frhs", self.cnt("frhs"))
            self.wait("gpsimd", "fw", self.cnt("fw"))
            for ci in range(NDC):
                for ic in range(NICH):
                    self.dma("gpsimd", wg_tile(ci, ic),
                             wgS[128 * ci:128 * ci + 128,
                                 128 * ic:128 * ic + 128].bitcast(F32R), "fw")
                    self.dma("gpsimd", wu_tile(ci, ic),
                             wuS[128 * ci:128 * ci + 128,
                                 128 * ic:128 * ic + 128].bitcast(F32R), "fw")
            FW = self.cnt("fw")
            if not shared:
                self.wait("gpsimd", "fhp", self.cnt("fhp"))
                self.wait("gpsimd", "l4", self.cnt("l4"))
                self.dma("gpsimd", CWB[:][:, 0:NQ],
                         x.cwd[ei:ei + 1, :].broadcast_to((128, NQ)), "l4")
            L4C = self.cnt("l4")

            self.wait("tensor", "fw", FW)
            FPD0 = self.cnt("fpd")
            for tc in range(NQC):
                for ci in range(NDC):
                    k = ((ei + 1) * NQC + tc) * NDC + ci
                    if k >= RR:
                        self.wait("sync", "frhs", k - RR + 1)
                    slot = "fmm%d" % (k % RR)
                    self.dma("sync", x.rr[k % RR][:, 0:QCW].bitcast(F32R),
                             x.ln3d[128 * ci:128 * ci + 128,
                                    QCW * tc:QCW * tc + QCW].bitcast(F32R),
                             slot)
                    self.wait("tensor", slot, self.cnt(slot))
                    if ci == 0:
                        self.wait("tensor", "fyw", self.cnt("fyw"))
                        self.wait("tensor", "fhp", self.cnt("fhp"))
                    for ic in range(NICH):
                        self.mm(x.PS_A[:, 512 * ic:512 * ic + QCW],
                                wg_tile(ci, ic), x.rr[k % RR][:, 0:QCW].bitcast(F32R),
                                start=(ci == 0), stop=(ci == NDC - 1), inc=None)
                        inc = self.inc("frhs") if ic == NICH - 1 else None
                        self.mm(x.PS_B[:, 512 * ic:512 * ic + QCW],
                                wu_tile(ci, ic), x.rr[k % RR][:, 0:QCW].bitcast(F32R),
                                start=(ci == 0), stop=(ci == NDC - 1), inc=inc)
                self.wait("scalar", "frhs", self.cnt("frhs"))
                self.wait("scalar", "l4", L4C)
                self.wait("vector", "l4", L4C)
                self.wait("vector", "fpd", self.cnt("fpd"))
                for ic in range(NICH):
                    self.wait("scalar", "vch", self.cnt("vch"))
                    self.ao(x.gs_sb[:, 0:QCW], x.PS_A[:, 512 * ic:512 * ic + QCW],
                            AF.Silu)
                    self.wait("vector", "ach", self.cnt("ach"))
                    self.vo("tt", x.gs_sb[:, TCW:TCW + QCW], x.gs_sb[:, 0:QCW],
                            x.PS_B[:, 512 * ic:512 * ic + QCW], ALU.mult)
                    if shared:
                        self.cp(hp_tile(ic, QCW), x.gs_sb[:, TCW:TCW + QCW],
                                inc=self.inc("fhp"))
                    else:
                        self.tt(hp_tile(ic, QCW), x.gs_sb[:, TCW:TCW + QCW],
                                CWB[:][:, QCW * tc:QCW * tc + QCW], ALU.mult,
                                inc=self.inc("fhp"))
                    self.wait("vector", "fhp", self.cnt("fhp"))
                FHPTC = self.cnt("fhp")
                self.wait("tensor", "fhp", FHPTC)
                for dmc in range(NDC):
                    half = dmc % 2
                    if ("wd", half) in self.slotuse:
                        self.wait("gpsimd", "fpd", self.slotuse[("wd", half)])
                    self.wait("gpsimd", "fw", self.cnt("fw"))
                    for ic in range(NICH):
                        self.dma("gpsimd", wd_tile(half, ic),
                                 wdS[128 * ic:128 * ic + 128,
                                     128 * dmc:128 * dmc + 128].bitcast(F32R),
                                 "fw")
                    FWD = self.cnt("fw")
                    self.wait("tensor", "fw", FWD)
                    yslot = dmc % 2
                    key = ("dn", yslot)
                    if key in self.slotuse:
                        self.wait("tensor", "fyw", self.slotuse[key])
                    for ic in range(NICH):
                        inc = self.inc("fpd") if ic == NICH - 1 else None
                        self.mm(x.PS_A[:, 512 * yslot:512 * yslot + QCW],
                                wd_tile(half, ic), hp_tile(ic, QCW),
                                start=(ic == 0), stop=(ic == NICH - 1), inc=inc)
                    self.slotuse[("wd", half)] = self.cnt("fpd")
                    self.wait("vector", "fpd", self.cnt("fpd"))
                    dstc = x.AR[:, dmc * NQ + QCW * tc:dmc * NQ + QCW * tc + QCW]
                    if shared:
                        self.cp(dstc, x.PS_A[:, 512 * yslot:512 * yslot + QCW],
                                inc=self.inc("fyw"))
                    else:
                        self.tt(dstc, dstc, x.PS_A[:, 512 * yslot:512 * yslot + QCW],
                                ALU.add, inc=self.inc("fyw"))
                    self.slotuse[("dn", yslot)] = self.cnt("fyw")
                    self.wait("vector", "fyw", self.cnt("fyw"))

        FYW = self.cnt("fyw")
        self.wait("tensor", "fyw", FYW)
        self.wait("scalar", "fyw", FYW)
        out_dma = []
        for tc in range(NQC):
            for oc in range(NDC):
                kk = tc * NDC + oc
                if oc == 0:
                    self.wait("tensor", "st0", self.cnt("st0"))
                self.mm(x.PS_B[0:1, 0:QCW], x.csb["ones_c128"][:],
                        x.AR[:, oc * NQ + QCW * tc:oc * NQ + QCW * tc + QCW],
                        start=(oc == 0), stop=(oc == NDC - 1),
                        inc=self.inc("bst") if oc == NDC - 1 else None)
                if kk >= 2:
                    self.wait("scalar", "sqf", self.cnt("sqf") - 1)
                self.act(x.sqb[:, (oc % 2) * TCW:(oc % 2) * TCW + QCW],
                         x.AR[:, oc * NQ + QCW * tc:oc * NQ + QCW * tc + QCW], AF.Square,
                         inc=self.inc("sqd"))
                self.wait("tensor", "sqd", self.cnt("sqd"))
                self.mm(x.PS_B[0:1, 512:512 + QCW], x.csb["ones_c128"][:],
                        x.sqb[:, (oc % 2) * TCW:(oc % 2) * TCW + QCW],
                        start=(oc == 0), stop=(oc == NDC - 1), inc=self.inc("sqf"))
            self._ln_rows(x.PS_B, x.rows, QCW, srhs=("bst", self.cnt("bst")),
                          ssq=("sqf", self.cnt("sqf")))
            self.wait("gpsimd", "st0", self.cnt("st0"))
            self.wait("gpsimd", "bbc", self.cnt("bbc"))
            self.dma("gpsimd", x.rowscr[0:1, 0:QCW], x.rows[0:1, 1024:1024 + QCW], "bbc")
            self.dma("gpsimd", x.rowscr[1:2, 0:QCW], x.rows[0:1, 1536:1536 + QCW], "bbc")
            self.wait("gpsimd", "bbc", self.cnt("bbc"))
            self.dma("gpsimd", x.RBt[:, 0:QCW],
                     x.rowscr[0:1, 0:QCW].broadcast_to((128, QCW)), "bbc")
            self.dma("gpsimd", x.MBt[:, 0:QCW],
                     x.rowscr[1:2, 0:QCW].broadcast_to((128, QCW)), "bbc")
            BBC = self.cnt("bbc")
            for oc in range(NDC):
                kx = tc * NDC + oc
                self.wait("sync", "frhs", self.cnt("frhs"))
                slotx = "fmm%d" % (kx % RR)
                self.dma("sync", x.rr[kx % RR][:, 0:QCW].bitcast(F32R),
                         x.x2d[128 * oc:128 * oc + 128,
                               QCW * tc:QCW * tc + QCW].bitcast(F32R), slotx)
                self.wait("vector", slotx, self.cnt(slotx))
                self.wait("vector", "bbc", BBC)
                if out_dma:
                    self.wait("vector", "outw", out_dma[-1])
                arc = x.AR[:, oc * NQ + QCW * tc:oc * NQ + QCW * tc + QCW]
                self.vo("tt", x.gs_sb[:, 0:QCW], arc, x.RBt[:, 0:QCW], ALU.mult)
                self.vo("tt", x.gs_sb[:, 0:QCW], x.gs_sb[:, 0:QCW], x.MBt[:, 0:QCW],
                        ALU.add)
                self.tt(x.gs_sb[:, TCW:TCW + QCW],
                        x.gs_sb[:, 0:QCW], x.rr[kx % RR][:, 0:QCW], ALU.add,
                        inc=self.inc("frhs"))
                self.wait("vector", "frhs", self.cnt("frhs"))
                self.wait("gpsimd", "frhs", self.cnt("frhs"))
                self.dma("gpsimd", x.outD[128 * oc:128 * oc + 128, QCW * tc:QCW * tc + QCW],
                         x.gs_sb[:, TCW:TCW + QCW], "outw")
                out_dma.append(self.cnt("outw"))
        self.wait("gpsimd", "outw", self.cnt("outw"))



def stage_core_inputs(cfg, core, full):
    S, DM = cfg["S"], cfg["DM"]
    b, half = core // 2, core % 2
    x = np.asarray(full["L0_attn_x"][b], dtype=np.float32)
    NQ = S // 2
    xr = x if half == 0 else np.concatenate([x[NQ:], x[:NQ]], axis=0)
    d = {
        "xbT": np.ascontiguousarray(xr.T),
        "wqkvT": np.ascontiguousarray(np.asarray(full["L0_attn_qkv_w"], np.float32).T),
        "bqkv": np.ascontiguousarray(np.asarray(full["L0_attn_qkv_b"], np.float32).reshape(-1, 1)),
        "wprojT": np.ascontiguousarray(np.asarray(full["L0_attn_proj_w"], np.float32).T),
        "bproj": np.ascontiguousarray(np.asarray(full["L0_attn_proj_b"], np.float32).reshape(-1, 1)),
        "dwcw": np.ascontiguousarray(np.asarray(full["L0_attn_dwc_w"], np.float32).reshape(DM, 3)),
        "dwcb": np.ascontiguousarray(np.asarray(full["L0_attn_dwc_b"], np.float32).reshape(-1, 1)),
        "gatewT": np.ascontiguousarray(np.asarray(full["L0_ffn_gate_w"], np.float32).T),
        "wgT": np.ascontiguousarray(np.asarray(full["L0_ffn_wg"], np.float32).transpose(0, 2, 1)),
        "wuT": np.ascontiguousarray(np.asarray(full["L0_ffn_wu"], np.float32).transpose(0, 2, 1)),
        "wdT": np.ascontiguousarray(np.asarray(full["L0_ffn_wd"], np.float32).transpose(0, 2, 1)),
        "sgT": np.ascontiguousarray(np.asarray(full["L0_ffn_sg"], np.float32).T),
        "suT": np.ascontiguousarray(np.asarray(full["L0_ffn_su"], np.float32).T),
        "sdT": np.ascontiguousarray(np.asarray(full["L0_ffn_sd"], np.float32).T),
        "masks": np.array([[1.0 if half == 1 else 0.0,
                            1.0 if half == 0 else 0.0]], dtype=np.float32),
    }
    d.update(host_consts(cfg["LN_EPS"]))
    return d


_PROG_CACHE = {}


def kernel(**inputs):
    cfg = FULL_CFG
    if "full" not in _PROG_CACHE:
        _PROG_CACHE["full"] = Builder(cfg).build()
    nc = _PROG_CACHE["full"]
    in_maps = [stage_core_inputs(cfg, c, inputs) for c in range(8)]
    res = run_bass_kernel_spmd(nc, in_maps, core_ids=list(range(8)))
    S, DM = cfg["S"], cfg["DM"]
    NQ = S // 2
    out = np.empty((4, S, DM), dtype=np.float32)
    for c in range(8):
        b, half = c // 2, c % 2
        o = np.asarray(res.results[c]["out"])
        if half == 0:
            out[b, 0:NQ, :] = o.T
        else:
            out[b, NQ:S, :] = o.T
    return out


# version 15
# speedup vs baseline: 2.2775x; 1.0009x over previous
import math
from contextlib import ExitStack

import ml_dtypes
import numpy as np
import concourse.bass as bass
import concourse.mybir as mybir
from concourse.bass_utils import run_bass_kernel_spmd

F32 = mybir.dt.float32
F32R = mybir.dt.float32r
BF16 = mybir.dt.bfloat16
AF = mybir.ActivationFunctionType
ALU = mybir.AluOpType

LOG_EPS = float(np.log(1e-6))

FULL_CFG = dict(S=4096, DM=1024, H=16, DH=64, INTER=512, E=8,
                LN_EPS=1e-5, ACT_FRAC=0.375)

CONST_SHAPES = {
    "wbase": ([128, 192], BF16),
    "id128": ([128, 128], F32),
    "id1": ([1, 1], F32),
    "ones_c128": ([128, 1], F32),
    "halfones": ([128, 1], F32),
    "otherhalf": ([128, 1], F32),
    "logeps": ([1, 1], F32),
    "lneps": ([1, 1], F32),
}


def host_consts(ln_eps):
    wb = np.zeros((128, 192), dtype=np.float32)
    wb[0:64, 64] = 1.0
    wb[64:128, 128] = 1.0
    ho = np.zeros((128, 1), dtype=np.float32); ho[0:64] = 1.0
    oh = np.zeros((128, 1), dtype=np.float32); oh[64:128] = 1.0
    return {
        "wbase": wb.astype(ml_dtypes.bfloat16),
        "id128": np.eye(128, dtype=np.float32),
        "id1": np.ones((1, 1), dtype=np.float32),
        "ones_c128": np.ones((128, 1), dtype=np.float32),
        "halfones": ho,
        "otherhalf": oh,
        "logeps": np.full((1, 1), LOG_EPS, dtype=np.float32),
        "lneps": np.full((1, 1), ln_eps, dtype=np.float32),
    }


def pick_act_blocks(nblk, frac):
    n = int(round(frac * nblk))
    n = min(n, (nblk + 1) // 2)
    out = set()
    if n > 0:
        step = nblk / n
        prev = -2
        for i in range(n):
            b = min(nblk - 1, int(i * step))
            if b <= prev + 1:
                b = prev + 2
            if b >= nblk:
                break
            out.add(b)
            prev = b
    return out


class Builder:
    ENGINES = ("sync", "gpsimd", "tensor", "vector", "scalar")

    def __init__(self, cfg):
        c = dict(cfg)
        c["NQ"] = c["S"] // 2
        c["NBLK"] = c["S"] // 128
        c["NDC"] = c["DM"] // 128
        c["TCW"] = min(512, c["S"])
        c["NTC"] = c["S"] // c["TCW"]
        c["QCW"] = min(512, c["NQ"])
        c["NQC"] = max(1, c["NQ"] // c["QCW"])
        c["ICW"] = min(512, c["NQ"])
        c["NIC"] = max(1, c["NQ"] // c["ICW"])
        c["NICH"] = c["INTER"] // 128
        c["HP"] = c["H"] // 2
        c["ACT_B"] = pick_act_blocks(c["NBLK"], c["ACT_FRAC"])
        c["ACT_S"] = frozenset(s for s in range(64) if s % 8 in (1, 4, 6))
        self.c = c
        self.ops = {e: [] for e in self.ENGINES}
        self.ct = {}
        self.sem = {}
        self.pe_carry = []
        self.slotuse = {}

    def emit(self, eng, fn):
        self.ops[eng].append(fn)

    def wait(self, eng, s, v):
        if v > 0:
            self.emit(eng, lambda q, s=s, v=v: q.wait_ge(self.sem[s], v))

    def bump(self, s, n=1):
        self.ct[s] = self.ct.get(s, 0) + n
        return self.ct[s]

    def cnt(self, s):
        return self.ct.get(s, 0)

    def op(self, eng, fn, inc=None, carry_ok=False):
        if inc is None and carry_ok and self.pe_carry:
            inc = self.pe_carry.pop(0)
        if inc is not None:
            s, n = inc
            self.emit(eng, lambda q, fn=fn, s=s, n=n: fn(q).then_inc(self.sem[s], n))
        else:
            self.emit(eng, lambda q, fn=fn: fn(q))

    def inc(self, s, n=1):
        self.bump(s, n)
        return (s, n)

    def carry(self, s, n=1):
        self.bump(s, n)
        self.pe_carry.append((s, n))

    def release(self, eng, s, n=1):
        self.bump(s, n)
        self.emit(eng, lambda q, s=s, n=n: q.sem_inc(self.sem[s], n))

    def dma(self, eng, dst, src, s):
        v = self.bump(s, 16)
        self.emit(eng, lambda q, dst=dst, src=src, s=s: q.dma_start(out=dst, in_=src)
                  .then_inc(self.sem[s], 16))
        return v

    def mm(self, out, lhsT, rhs, start, stop, inc=None):
        self.op("tensor", lambda t, o=out, l=lhsT, r=rhs, st=start, sp=stop:
                t.matmul(out=o, lhsT=l, rhs=r, start=st, stop=sp), inc=inc,
                carry_ok=(inc is None))

    def transp(self, out, in_, ident, inc=None):
        self.op("tensor", lambda t, o=out, i=in_, d=ident:
                t.transpose(out=o, in_=i, identity=d), inc=inc, carry_ok=(inc is None))

    def act(self, out, in_, func, bias=0.0, scale=1.0, inc=None):
        self.op("scalar", lambda a, o=out, i=in_, f=func, b=bias, s=scale:
                a.activation(out=o, in_=i, func=f, bias=b, scale=s), inc=inc)

    def ts(self, out, in0, s1, op0, s2=None, op1=None, inc=None):
        if s2 is None:
            self.op("vector", lambda v, o=out, i=in0, a=s1, p=op0:
                    v.tensor_single_scalar(out=o, in_=i, scalar=a, op=p), inc=inc)
        else:
            self.op("vector", lambda v, o=out, i=in0, a=s1, b=s2, p=op0, p1=op1:
                    v.tensor_scalar(out=o, in0=i, scalar1=a, scalar2=b, op0=p, op1=p1),
                    inc=inc)

    def tt(self, out, in0, in1, alu, inc=None):
        self.op("vector", lambda v, o=out, a=in0, b=in1, p=alu:
                v.tensor_tensor(out=o, in0=a, in1=b, op=p), inc=inc)

    def stt(self, out, in0, scalar, in1, op0, op1, inc=None):
        self.op("vector", lambda v, o=out, a=in0, s=scalar, b=in1, p0=op0, p1=op1:
                v.scalar_tensor_tensor(out=o, in0=a, scalar=s, in1=b, op0=p0, op1=p1),
                inc=inc)

    def cp(self, out, in_, inc=None):
        self.op("vector", lambda v, o=out, i=in_: v.tensor_copy(out=o, in_=i), inc=inc)

    def mset(self, ap_, val, inc=None):
        self.op("vector", lambda v, a=ap_, x=val: v.memset(a, x), inc=inc)

    def red(self, out, in_, alu, inc=None):
        self.op("vector", lambda v, o=out, i=in_, p=alu:
                v.tensor_reduce(out=o, in_=i, axis=mybir.AxisListType.X, op=p), inc=inc)

    def recip(self, out, in_, inc=None):
        self.op("vector", lambda v, o=out, i=in_: v.reciprocal(out=o, in_=i), inc=inc)

    def vo(self, kind, *a, **kw):
        kw["inc"] = self.inc("vch")
        getattr(self, kind)(*a, **kw)
        self.wait("vector", "vch", self.cnt("vch"))

    def ao(self, *a, **kw):
        kw["inc"] = self.inc("ach")
        self.act(*a, **kw)
        self.wait("scalar", "ach", self.cnt("ach"))

    def build(self):
        c = self.c
        S, DM, H, DH, INTER, E = c["S"], c["DM"], c["H"], c["DH"], c["INTER"], c["E"]
        NQ, NBLK, NDC, TCW, NTC = c["NQ"], c["NBLK"], c["NDC"], c["TCW"], c["NTC"]
        QCW, NQC, ICW, NIC, NICH, HP = (c["QCW"], c["NQC"], c["ICW"], c["NIC"],
                                        c["NICH"], c["HP"])

        nc = bass.Bass("TRN2", target_bir_lowering=False, debug=False)
        self.nc = nc
        dram = {}

        def din(name, shape, dt_=F32):
            dram[name] = nc.dram_tensor(name, shape, dt_, kind="ExternalInput").ap()
            return dram[name]

        xbT = din("xbT", [DM, S])
        wqkvT = din("wqkvT", [DM, 3 * DM])
        bqkv = din("bqkv", [3 * DM, 1])
        wprojT = din("wprojT", [DM, DM])
        bprojD = din("bproj", [DM, 1])
        dwcwD = din("dwcw", [DM, 3])
        dwcbD = din("dwcb", [DM, 1])
        gatewT = din("gatewT", [DM, E])
        wgTD = din("wgT", [E, DM, INTER])
        wuTD = din("wuT", [E, DM, INTER])
        wdTD = din("wdT", [E, INTER, DM])
        sgTD = din("sgT", [DM, INTER])
        suTD = din("suT", [DM, INTER])
        sdTD = din("sdT", [INTER, DM])
        masksD = din("masks", [1, 2])
        for name, (shape, dt_) in CONST_SHAPES.items():
            din(name, shape, dt_)

        lnx1 = nc.dram_tensor("lnx1", [DM, S], F32, kind="Internal").ap()
        ln3d = nc.dram_tensor("ln3d", [DM, NQ], F32, kind="Internal").ap()
        x2d = nc.dram_tensor("x2d", [DM, NQ], F32, kind="Internal").ap()
        rowscr = nc.dram_tensor("rowscr", [8, max(S, 2048)], F32, kind="Internal").ap()
        cwd = nc.dram_tensor("cwd", [E, NQ], F32, kind="Internal").ap()
        outD = nc.dram_tensor("out", [DM, NQ], F32, kind="ExternalOutput").ap()

        es = ExitStack()
        with es:
            sbuf = lambda name, shape, dt_=F32: es.enter_context(
                nc.sbuf_tensor(name, shape, dt_))
            csb = {name: sbuf("c_" + name, shape, dt_)
                   for name, (shape, dt_) in CONST_SHAPES.items()}

            QT2 = sbuf("QT2", [128, NQ])
            nKT2 = sbuf("nKT2", [128, max(S // 2, 2 * NICH * 128)])
            VT = sbuf("VT", [128, max(S, NICH * NDC * 128, (c["DM"] // 128) * min(512, c["NQ"]))])
            vp = sbuf("vp", [128, max(NBLK * 65, NQ)])
            smallb = sbuf("smallb", [128, max(NBLK, 6 * c["E"] + 4) + 12])
            SMB = max(NBLK, 6 * c["E"] + 4)
            nskcol = smallb[:, 0:NBLK]
            mlrb = smallb[:, SMB + 8:SMB + 10]
            RN = 8
            ring = [sbuf(f"ring{i}", [128, NQ], BF16) for i in range(RN)]
            kern = sbuf("kern", [128, max(NQ, DM)])
            kern2 = sbuf("kern2", [128, NQ])
            wv = sbuf("wv", [128, DM])
            AR = sbuf("AR", [128, NDC * NQ])
            wbuf = sbuf("wbuf", [128, 2 * DM])
            RR = 3
            rr = [sbuf(f"rr{i}", [128, TCW]) for i in range(RR)]
            bq2 = smallb[:, SMB:SMB + 1]
            bk2 = smallb[:, SMB + 1:SMB + 2]
            bv2 = smallb[:, SMB + 2:SMB + 3]
            bpj = smallb[:, SMB + 3:SMB + 4]
            dwc_s = smallb[:, SMB + 4:SMB + 7]
            dwcb_s = smallb[:, SMB + 7:SMB + 8]
            rowT_t = sbuf("rowT", [128, max(S, 4096)])
            nskrow = rowT_t[0:1, 0:S]
            rows = rowT_t[32:33, 0:4096]
            epsp = rowT_t[64:65, 0:NQ]
            recrow = rowT_t[64:65, 2048:2048 + NQ]
            ov1 = sbuf("ov1", [128, max(NQ, 2 * TCW)])
            RBt = ov1[:, 0:TCW]
            MBt = ov1[:, TCW:2 * TCW]
            OV2W = max(NQ, 4 * TCW)
            ov2 = sbuf("ov2", [128, OV2W + 2])
            sqb = ov2[:, 0:2 * TCW]
            gs_sb = ov2[:, 2 * TCW:4 * TCW]
            NG = NQ // 128
            cwtm = kern[:][:, NDC * E:NDC * E + NG * E]
            scx = smallb[:, :]

            PS_A = es.enter_context(nc.psum_tensor("PS_A", [128, 2048], F32))
            PS_B = es.enter_context(nc.psum_tensor("PS_B", [128, 2048], F32))

            semnames = ["cst", "x1rhs", "x2rhs", "sqd", "sqf",
                        "x1mm0", "x1mm1", "x1mm2", "x1mm3",
                        "qmm0", "qmm1", "qmm2", "qmm3",
                        "fmm0", "fmm1", "fmm2", "fmm3",
                        "lnxh0", "lnxh1",
                        "st0", "bc0", "lnxw", "lnxc",
                        "qw", "qrhs", "qgd", "qev",
                        "prep_pe", "prep_dve", "prep_act",
                        "tra", "tfa", "trv", "tfv",
                        "kdn", "evd", "pvd", "pad", "pbf", "p3d", "p3b",
                        "vch", "ach", "bgo", "x2w",
                        "bdw", "bmm", "bev", "bst", "bbc", "cwt", "cwr",
                        "fw", "frhs", "fgd", "fhp", "fpd", "fyw", "l4",
                        "outw"]
            for s in semnames:
                self.sem[s] = es.enter_context(nc.semaphore(s))

            ctx = dict(csb=csb, mlrb=mlrb, QT2=QT2, nKT2=nKT2, VT=VT, vp=vp,
                       nskcol=nskcol, nskrow=nskrow, epsp=epsp, recrow=recrow, ring=ring, NG=NG,
                       kern=kern, kern2=kern2, wv=wv, AR=AR, wbuf=wbuf, rr=rr, RR=RR,
                       bq2=bq2, bk2=bk2, bv2=bv2, bpj=bpj, dwc_s=dwc_s,
                       dwcb_s=dwcb_s, ov1=ov1, ov2=ov2, OV2W=OV2W,
                       rows=rows, RBt=RBt, MBt=MBt, sqb=sqb,
                       cwtm=cwtm, gs_sb=gs_sb, scx=scx, gw_sb=kern[:][:, 0:NDC * E],
                       PS_A=PS_A, PS_B=PS_B,
                       xbT=xbT, wqkvT=wqkvT, bqkv=bqkv, wprojT=wprojT,
                       bprojD=bprojD, dwcwD=dwcwD, dwcbD=dwcbD, gatewT=gatewT,
                       wgTD=wgTD, wuTD=wuTD, wdTD=wdTD, sgTD=sgTD, suTD=suTD,
                       sdTD=sdTD, masksD=masksD, lnx1=lnx1, ln3d=ln3d, x2d=x2d,
                       rowscr=rowscr, outD=outD, cwd=cwd)
            self.x = type("X", (), ctx)
            self.dram = dram

            phases = self.c.get("PHASES", "0ABC")
            self._consts()
            if "0" in phases:
                self._phase0()
            if "A" in phases:
                self._phase_a()
            if "B" in phases:
                self._phase_b()
            if "C" in phases:
                self._phase_c()
            if "C" not in phases:
                for s_ in ("bgo", "vch", "ach"):
                    self.wait("gpsimd", s_, self.cnt(s_))
                self.dma("gpsimd", outD[0:128, 0:min(512, NQ)],
                         self.x.gs_sb[:, 0:min(512, NQ)], "outw")
                self.wait("gpsimd", "outw", self.cnt("outw"))

            block = es.enter_context(nc.Block())

            @block.sync
            def _(q):
                for fn in self.ops["sync"]:
                    fn(q)

            @block.gpsimd
            def _(q):
                for fn in self.ops["gpsimd"]:
                    fn(q)

            @block.tensor
            def _(q):
                for fn in self.ops["tensor"]:
                    fn(q)

            @block.vector
            def _(q):
                for fn in self.ops["vector"]:
                    fn(q)

            @block.scalar
            def _(q):
                for fn in self.ops["scalar"]:
                    fn(q)

        return nc

    def _consts(self):
        x = self.x
        for name in CONST_SHAPES:
            self.dma("gpsimd", x.csb[name][:], self.dram[name], "cst")
        self.dma("gpsimd", x.rowscr[7:8, 0:2], x.masksD, "cst")
        self.wait("gpsimd", "cst", self.cnt("cst"))
        self.dma("gpsimd", x.mlrb, x.rowscr[7:8, 0:2].broadcast_to((128, 2)), "cst")
        CST = self.cnt("cst")
        for e in ("tensor", "vector", "scalar", "sync"):
            self.wait(e, "cst", CST)

    def _phase0(self):
        x, c = self.x, self.c
        DM, NDC, TCW, NTC, RR = c["DM"], c["NDC"], c["TCW"], c["NTC"], 3
        rel = []
        for tc in range(NTC):
            for ci in range(NDC):
                k = tc * 2 * NDC + ci
                if k >= RR:
                    self.wait("sync", *rel[k - RR])
                slot = "x1mm%d" % (k % RR)
                self.dma("sync", x.rr[k % RR][:].bitcast(F32R),
                         x.xbT[128 * ci:128 * ci + 128,
                               TCW * tc:TCW * tc + TCW].bitcast(F32R), slot)
                self.wait("scalar", slot, self.cnt(slot))
                kk = tc * NDC + ci
                if kk >= 2:
                    self.wait("scalar", "x1rhs", kk - 1)
                rel.append(None)
                self.act(x.sqb[:, (ci % 2) * TCW:(ci % 2) * TCW + TCW], x.rr[k % RR][:],
                         AF.Square, inc=self.inc("sqd"))
                self.wait("tensor", slot, self.cnt(slot))
                if ci == 0:
                    self.wait("tensor", "st0", self.cnt("st0"))
                self.mm(x.PS_A[0:1, 0:TCW], x.csb["ones_c128"][:], x.rr[k % RR][:],
                        start=(ci == 0), stop=(ci == NDC - 1), inc=None)
                self.wait("tensor", "sqd", self.cnt("sqd"))
                self.mm(x.PS_A[0:1, 512:512 + TCW], x.csb["ones_c128"][:],
                        x.sqb[:, (ci % 2) * TCW:(ci % 2) * TCW + TCW],
                        start=(ci == 0), stop=(ci == NDC - 1), inc=self.inc("x1rhs"))
                rel[k] = ("x1rhs", self.cnt("x1rhs"))
            self._ln_rows(x.PS_A, x.rows, TCW, srhs=("x1rhs", self.cnt("x1rhs")))
            self.wait("gpsimd", "st0", self.cnt("st0"))
            self.wait("gpsimd", "bc0", self.cnt("bc0"))
            self.dma("gpsimd", x.rowscr[0:1, 0:TCW], x.rows[0:1, 1024:1024 + TCW], "bc0")
            self.dma("gpsimd", x.rowscr[1:2, 0:TCW], x.rows[0:1, 1536:1536 + TCW], "bc0")
            self.wait("gpsimd", "bc0", self.cnt("bc0"))
            self.dma("gpsimd", x.RBt, x.rowscr[0:1, 0:TCW].broadcast_to((128, TCW)), "bc0")
            self.dma("gpsimd", x.MBt, x.rowscr[1:2, 0:TCW].broadcast_to((128, TCW)), "bc0")
            BC = self.cnt("bc0")
            for ci in range(NDC):
                k2 = tc * 2 * NDC + NDC + ci
                if k2 >= RR:
                    self.wait("sync", *rel[k2 - RR])
                slot2 = "x1mm%d" % (k2 % RR)
                self.dma("sync", x.rr[k2 % RR][:].bitcast(F32R),
                         x.xbT[128 * ci:128 * ci + 128,
                               TCW * tc:TCW * tc + TCW].bitcast(F32R), slot2)
                self.wait("vector", slot2, self.cnt(slot2))
                self.wait("vector", "bc0", BC)
                kk2 = tc * NDC + ci
                half = ci % 2
                if kk2 >= 2:
                    self.wait("vector", "lnxh%d" % half, self.cnt("lnxh%d" % half))
                self.vo("tt", x.gs_sb[:, half * TCW:half * TCW + TCW],
                        x.rr[k2 % RR][:], x.RBt, ALU.mult)
                self.release("vector", "x2rhs")
                rel.append(("x2rhs", self.cnt("x2rhs")))
                self.vo("tt", x.gs_sb[:, half * TCW:half * TCW + TCW],
                        x.gs_sb[:, half * TCW:half * TCW + TCW], x.MBt, ALU.add)
                self.release("vector", "lnxw")
                self.wait("gpsimd", "lnxw", self.cnt("lnxw"))
                self.dma("gpsimd", x.lnx1[128 * ci:128 * ci + 128, TCW * tc:TCW * tc + TCW],
                         x.gs_sb[:, half * TCW:half * TCW + TCW], "lnxh%d" % half)
        self.wait("sync", "lnxh0", self.cnt("lnxh0"))
        self.wait("sync", "lnxh1", self.cnt("lnxh1"))

    def _ln_rows(self, PS, rows, W, srhs=None, ssq=None):
        DM = self.c["DM"]
        if srhs:
            self.wait("vector", srhs[0], srhs[1])
        if ssq:
            self.wait("vector", ssq[0], ssq[1])
        self.vo("ts", rows[0:1, 0:W], PS[0:1, 0:W], 1.0 / DM, ALU.mult)
        self.vo("ts", rows[0:1, 512:512 + W], PS[0:1, 512:512 + W], 1.0 / DM, ALU.mult)
        self.wait("scalar", "vch", self.cnt("vch"))
        self.ao(rows[0:1, 2048:2048 + W], rows[0:1, 0:W], AF.Square)
        self.wait("vector", "ach", self.cnt("ach"))
        self.vo("tt", rows[0:1, 2560:2560 + W], rows[0:1, 512:512 + W],
                rows[0:1, 2048:2048 + W], ALU.subtract)
        self.wait("scalar", "vch", self.cnt("vch"))
        self.ao(rows[0:1, 2048:2048 + W], rows[0:1, 2560:2560 + W], AF.Ln,
                bias=self.x.csb["lneps"][:])
        self.ao(rows[0:1, 1024:1024 + W], rows[0:1, 2048:2048 + W], AF.Exp, scale=-0.5)
        self.wait("vector", "ach", self.cnt("ach"))
        self.vo("tt", rows[0:1, 2560:2560 + W], rows[0:1, 0:W], rows[0:1, 1024:1024 + W],
                ALU.mult)
        self.ts(rows[0:1, 1536:1536 + W], rows[0:1, 2560:2560 + W], -1.0, ALU.mult,
                inc=self.inc("st0"))

    def _phase_a(self):
        x, c = self.x, self.c
        S, DM, H, DH = c["S"], c["DM"], c["H"], c["DH"]
        NQ, NBLK, NDC, TCW, NTC = c["NQ"], c["NBLK"], c["NDC"], c["TCW"], c["NTC"]
        ICW, NIC = c["ICW"], c["NIC"]
        ACT_B = c["ACT_B"]
        HALF = S // 2
        RR = 3
        TA = [x.ring[i] for i in range(3)]
        TV = [x.ring[3 + i] for i in range(5)]
        NQCH = max(1, NQ // TCW)
        QW_LAST = NQ - TCW * (NQCH - 1) if NQ >= TCW else NQ

        for h in range(self.c.get("HLIM", H)):
            hp, hq = h // 2, h % 2
            even = (hq == 0)

            self.wait("gpsimd", "qev", self.cnt("qev"))
            for ci in range(NDC):
                src = x.wqkvT[128 * ci:128 * ci + 128, DH * h:DH * h + DH].bitcast(F32R)
                self.dma("gpsimd", x.wbuf[:, 128 * ci:128 * ci + 64].bitcast(F32R), src, "qw")
                self.dma("gpsimd", x.wbuf[:, 128 * ci + 64:128 * ci + 128].bitcast(F32R), src, "qw")
            for ci in range(NDC):
                src = x.wqkvT[128 * ci:128 * ci + 128, DM + DH * h:DM + DH * h + DH].bitcast(F32R)
                self.dma("gpsimd", x.wbuf[:, DM + 128 * ci:DM + 128 * ci + 64].bitcast(F32R), src, "qw")
                self.dma("gpsimd", x.wbuf[:, DM + 128 * ci + 64:DM + 128 * ci + 128].bitcast(F32R), src, "qw")
            if even:
                for ci in range(NDC):
                    src = x.wqkvT[128 * ci:128 * ci + 128,
                                  2 * DM + 128 * hp:2 * DM + 128 * hp + 128]
                    self.dma("gpsimd", x.wv[:, 128 * ci:128 * ci + 128].bitcast(F32R),
                             src.bitcast(F32R), "qw")
            self.dma("gpsimd", x.bq2[0:64, :], x.bqkv[DH * h:DH * h + DH, :], "qw")
            self.dma("gpsimd", x.bq2[64:128, :], x.bqkv[DH * h:DH * h + DH, :], "qw")
            self.dma("gpsimd", x.bk2[0:64, :], x.bqkv[DM + DH * h:DM + DH * h + DH, :], "qw")
            self.dma("gpsimd", x.bk2[64:128, :], x.bqkv[DM + DH * h:DM + DH * h + DH, :], "qw")
            if even:
                self.dma("gpsimd", x.bv2,
                         x.bqkv[2 * DM + 128 * hp:2 * DM + 128 * hp + 128, :], "qw")
            self.wait("gpsimd", "bgo", self.cnt("bgo"))
            self.dma("gpsimd", x.dwc_s[64 * hq:64 * hq + 64, :],
                     x.dwcwD[DH * h:DH * h + DH, :], "qw")
            self.dma("gpsimd", x.dwcb_s[64 * hq:64 * hq + 64, :],
                     x.dwcbD[DH * h:DH * h + DH, :], "qw")
            QW = self.cnt("qw")

            self.wait("tensor", "qw", QW)
            for tc in range(NTC):
                own = tc < NQCH
                qw_ = TCW if (tc < NQCH - 1 or NQ >= TCW * NQCH) else QW_LAST
                if NQ < TCW:
                    qw_ = NQ
                if tc > 0:
                    self.wait("tensor", "qev", self.cnt("qev"))
                for ci in range(NDC):
                    k = (h * NTC + tc) * NDC + ci
                    if k >= RR:
                        self.wait("sync", "qrhs", k - RR + 1)
                    slot = "qmm%d" % (k % RR)
                    self.dma("sync", x.rr[k % RR][:].bitcast(F32R),
                             x.lnx1[128 * ci:128 * ci + 128,
                                    TCW * tc:TCW * tc + TCW].bitcast(F32R), slot)
                    self.wait("tensor", slot, self.cnt(slot))
                    first, last = (ci == 0), (ci == NDC - 1)
                    roles = [("K", x.PS_A[:, 512:512 + TCW], x.wbuf[:, DM:2 * DM], TCW)]
                    if even:
                        roles.append(("V", x.PS_A[:, 1024:1024 + TCW],
                                      x.wv[:, 0:DM], TCW))
                    if own:
                        roles.append(("Q", x.PS_A[:, 0:qw_], x.wbuf[:, 0:DM], qw_))
                    for ri, (rn, pslice, wslab, w_) in enumerate(roles):
                        lastrole = ri == len(roles) - 1
                        inc = self.inc("qrhs") if lastrole else None
                        self.mm(pslice, wslab[:, 128 * ci:128 * ci + 128].bitcast(F32R),
                                x.rr[k % RR][:, 0:w_].bitcast(F32R),
                                start=first, stop=last, inc=inc)
                self.wait("vector", "qrhs", self.cnt("qrhs"))
                nblk_tc = TCW // 128
                g0 = tc * nblk_tc
                ktop = x.nKT2[0:64, 64 * g0:64 * (g0 + nblk_tc)].bitcast(F32R)
                ktop = ktop.rearrange("p (g s) -> p g s", g=nblk_tc)
                src_t = x.PS_A[0:64, 512:512 + TCW].rearrange(
                    "p (g u s) -> p g u s", g=nblk_tc, u=2)[:, :, 0, :]
                self.vo("ts", ktop, src_t, x.bk2[0:64, :], ALU.add, -1.0, ALU.mult)
                kbot = x.nKT2[64:128, 64 * g0:64 * (g0 + nblk_tc)].bitcast(F32R)
                kbot = kbot.rearrange("p (g s) -> p g s", g=nblk_tc)
                src_b = x.PS_A[64:128, 512:512 + TCW].rearrange(
                    "p (g u s) -> p g u s", g=nblk_tc, u=2)[:, :, 1, :]
                self.vo("ts", kbot, src_b, x.bk2[64:128, :], ALU.add, -1.0, ALU.mult)
                if even:
                    self.vo("ts", x.VT[:, TCW * tc:TCW * tc + TCW].bitcast(F32R),
                            x.PS_A[:, 1024:1024 + TCW], x.bv2, ALU.add)
                if own:
                    self.vo("ts", x.QT2[:, TCW * tc:TCW * tc + qw_], x.PS_A[:, 0:qw_],
                            x.bq2, ALU.add)
                self.emit("vector", lambda q: q.sem_inc(self.sem["qev"], 1))
                self.bump("qev")

            if self.c.get("ASTOP") == "qkv":
                return
            self.wait("tensor", "qev", self.cnt("qev"))
            for q0 in range(0, NQ, 512):
                qw2 = min(512, NQ - q0)
                self.mm(x.PS_A[0:1, q0:q0 + qw2], x.csb["halfones"][:],
                        x.QT2[:, q0:q0 + qw2], True, True,
                        inc=self.inc("prep_pe") if q0 + qw2 >= NQ else None)
            self.wait("scalar", "prep_pe", self.cnt("prep_pe"))
            self.act(x.epsp, x.PS_A[0:1, 0:NQ], AF.Exp, bias=x.csb["logeps"][:],
                     scale=-1.0 / 16.0, inc=self.inc("prep_act"))
            self.wait("tensor", "prep_act", self.cnt("prep_act"))
            for q0 in range(0, HALF, 512):
                qw2 = min(512, HALF - q0)
                self.mm(x.PS_A[0:1, q0:q0 + qw2], x.csb["halfones"][:],
                        x.nKT2[:, q0:q0 + qw2], True, True,
                        inc=self.inc("prep_pe") if q0 + qw2 >= HALF else None)
            self.wait("vector", "prep_pe", self.cnt("prep_pe"))
            ev_row = x.nskrow[0:1, :].rearrange("p (g u s) -> p g u s", g=NBLK, u=2)
            self.ts(ev_row[:, :, 0, :], x.PS_A[0:1, 0:HALF], 1.0 / 16.0, ALU.mult,
                    inc=self.inc("prep_dve"))
            self.wait("tensor", "prep_dve", self.cnt("prep_dve"))
            for q0 in range(0, HALF, 512):
                qw2 = min(512, HALF - q0)
                self.mm(x.PS_A[0:1, q0:q0 + qw2], x.csb["otherhalf"][:],
                        x.nKT2[:, q0:q0 + qw2], True, True,
                        inc=self.inc("prep_pe") if q0 + qw2 >= HALF else None)
            self.wait("vector", "prep_pe", self.cnt("prep_pe"))
            self.ts(ev_row[:, :, 1, :], x.PS_A[0:1, 0:HALF], 1.0 / 16.0, ALU.mult,
                    inc=self.inc("prep_dve"))
            self.wait("tensor", "prep_dve", self.cnt("prep_dve"))
            for b in range(NBLK):
                self.transp(x.PS_A[:, b:b + 1], x.nskrow[0:1, 128 * b:128 * b + 128],
                            x.csb["id1"][:],
                            inc=self.inc("prep_pe") if b == NBLK - 1 else None)
            self.wait("vector", "prep_pe", self.cnt("prep_pe"))
            self.cp(x.nskcol, x.PS_A[:, 0:NBLK], inc=self.inc("prep_dve"))
            self.wait("vector", "prep_dve", self.cnt("prep_dve"))
            vsrc = x.VT[64 * hq:64 * hq + 64, :]
            ngrp = (NBLK + 7) // 8
            for g in range(ngrp):
                nb = min(8, NBLK - 8 * g)
                self.wait("tensor", "prep_dve", self.cnt("prep_dve"))
                for j in range(nb):
                    b = 8 * g + j
                    self.transp(x.PS_A[:, 64 * j:64 * j + 64],
                                vsrc[:, 128 * b:128 * b + 128],
                                x.csb["id128"][64 * hq:64 * hq + 64, 64 * hq:64 * hq + 64],
                                inc=self.inc("prep_pe") if j == nb - 1 else None)
                self.wait("vector", "prep_pe", self.cnt("prep_pe"))
                dst = x.vp[:, 65 * 8 * g:65 * (8 * g + nb)]
                dst = dst.rearrange("p (b w) -> p b w", b=nb)[:, :, 0:64].bitcast(F32R)
                src = x.PS_A[:, 0:64 * nb].rearrange("p (b w) -> p b w", b=nb)
                self.cp(dst, src, inc=self.inc("prep_dve"))
                self.wait("vector", "prep_dve", self.cnt("prep_dve"))
            if h == 0:
                ones_ap = x.vp[:, :].rearrange("p (b w) -> p b w", b=NBLK)[:, :, 64:65]
                ones_in = x.csb["id128"][:, 0:NBLK].rearrange("p (b w) -> p b w", w=1)
                self.ts(ones_ap.bitcast(F32R), ones_in, 0.0, ALU.mult,
                        1.0, ALU.add, inc=self.inc("prep_dve"))
            PREP = self.cnt("prep_dve")
            if self.c.get("ASTOP") == "prep":
                return

            headblk = h * NBLK
            na2 = self.cnt("tra")
            nv2 = self.cnt("trv")
            fa0 = self.cnt("tfa")
            fv0 = self.cnt("tfv")
            ACT_S = c["ACT_S"]

            NDVE = 64 - len(c["ACT_S"])

            def emit_produce(eng, b, s, kp, Tring, s_rdy, s_free, f0):
                R = len(Tring)
                if kp == 0:
                    self.wait(eng, "prep_dve", PREP)
                elif kp >= R:
                    pk = kp - R
                    if eng == "vector" and pk % NDVE == NDVE - 1:
                        self.wait(eng, "evd", headblk + pk // NDVE + 1)
                    else:
                        if eng == "vector":
                            nrel = pk + 1 - pk // NDVE
                        else:
                            nrel = pk + 1
                        self.wait(eng, s_free, f0 + nrel)
                tile = Tring[kp % R]
                inc = self.inc(s_rdy)
                if eng == "scalar":
                    self.act(tile[:, 0:NQ], x.QT2[:], AF.Relu,
                             bias=x.nKT2[:, 64 * b + s:64 * b + s + 1], inc=inc)
                else:
                    self.ts(tile[:, 0:NQ], x.QT2[:],
                            x.nKT2[:, 64 * b + s:64 * b + s + 1],
                            ALU.add, 0.0, ALU.max, inc=inc)

            def emit_evac(b):
                gbi = headblk + b
                if b <= 1:
                    if b == 0:
                        self.wait("scalar", "prep_dve", PREP)
                    self.wait("scalar", "pad", h)
                else:
                    self.wait("scalar", "pvd", gbi - (h + 1))
                self.wait("scalar", "kdn", gbi + 1)
                dst = x.kern if b % 2 == 0 else x.kern2
                self.act(dst[:, 0:NQ].bitcast(F32R), x.PS_A[:, 0:NQ], AF.Exp,
                         bias=x.nskcol[:, b:b + 1],
                         scale=-1.0 / 8.0, inc=self.inc("evd"))

            actS = sorted(ACT_S)
            dveS = [s for s in range(64) if s not in ACT_S]

            ai = 0
            for g in range(NBLK):
                if g >= 1:
                    emit_evac(g - 1)
                for s in actS:
                    emit_produce("scalar", g, s, ai, TA, "tra", "tfa", fa0)
                    ai += 1
            emit_evac(NBLK - 1)

            vi = 0
            for g in range(NBLK):
                for s in dveS:
                    emit_produce("vector", g, s, vi, TV, "trv", "tfv", fv0)
                    vi += 1

            pa = 0
            pv = 0
            for b in range(NBLK):
                gbi = headblk + b
                for s in range(64):
                    isact = s in ACT_S
                    ringp = TA if isact else TV
                    s_rdy = "tra" if isact else "trv"
                    kp = pa if isact else pv
                    base = na2 if isact else nv2
                    self.wait("tensor", s_rdy, base + kp + 1)
                    if s == 0:
                        if gbi > 0:
                            self.wait("tensor", "evd", gbi)
                        if b == 0:
                            self.wait("tensor", "prep_dve", PREP)
                    tile = ringp[kp % len(ringp)]
                    for ic in range(NIC):
                        inc = None
                        if ic == NIC - 1:
                            if s == 63:
                                inc = self.inc("kdn")
                            else:
                                inc = self.inc("tfa" if isact else "tfv")
                        self.mm(x.PS_A[:, 512 * ic:512 * ic + ICW],
                                x.csb["wbase"][:, 64 - s:192 - s],
                                tile[:, ICW * ic:ICW * ic + ICW],
                                start=(s == 0), stop=(s == 63), inc=inc)
                    if isact:
                        pa += 1
                    else:
                        pv += 1
                if b > 0:
                    self._emit_av(b - 1, h)
            self._emit_av(NBLK - 1, h)

            if self.c.get("ASTOP") == "dist":
                return
            b0 = 64 * hq
            rb_v = x.ov1[b0:b0 + 64, 0:NQ]
            lo_v = x.ov2[b0:b0 + 64, 0:NQ]
            e_v = x.ov2[b0:b0 + 64, x.OV2W:x.OV2W + 2]
            dwc_v = x.dwc_s[b0:b0 + 64, :]
            dwcb_v = x.dwcb_s[b0:b0 + 64, :]
            mlr_v = x.mlrb[b0:b0 + 64, :]
            dst = x.AR[b0:b0 + 64, NQ * hp:NQ * hp + NQ]
            self.wait("vector", "pad", h + 1)
            self.vo("tt", x.epsp, x.PS_B[64:65, 0:NQ], x.epsp, ALU.add)
            self.recip(x.recrow, x.epsp, inc=self.inc("p3d"))
            self.wait("gpsimd", "p3d", self.cnt("p3d"))
            self.wait("gpsimd", "p3b", self.cnt("p3b"))
            if h == 0:
                self.wait("gpsimd", "bc0", self.cnt("bc0"))
            self.dma("gpsimd", x.rowscr[2:3, 0:NQ], x.recrow, "p3b")
            self.wait("gpsimd", "p3b", self.cnt("p3b"))
            self.dma("gpsimd", rb_v, x.rowscr[2:3, 0:NQ].broadcast_to((64, NQ)), "p3b")
            if self.c.get("ASTOP") == "p3rec":
                return
            self.wait("vector", "p3b", self.cnt("p3b"))
            self.tt(dst, x.PS_B[0:64, 0:NQ], rb_v, ALU.mult, inc=self.inc("pbf"))
            self.wait("vector", "pbf", self.cnt("pbf"))
            if self.c.get("ASTOP") == "p3div":
                return
            v64 = x.VT[:][b0:b0 + 64, :]
            self.vo("ts", lo_v, v64[:, 0:NQ], dwc_v[:, 1:2], ALU.mult)
            self.vo("stt", lo_v[:, 1:NQ], v64[:, 0:NQ - 1], dwc_v[:, 0:1],
                    lo_v[:, 1:NQ], ALU.mult, ALU.add)
            self.vo("stt", lo_v[:, 0:NQ - 1], v64[:, 1:NQ], dwc_v[:, 2:3],
                    lo_v[:, 0:NQ - 1], ALU.mult, ALU.add)
            if self.c.get("ASTOP") == "p3conv":
                return
            self.vo("ts", e_v[:, 0:1], v64[:, S - 1:S], mlr_v[:, 0:1], ALU.mult)
            self.vo("stt", lo_v[:, 0:1], e_v[:, 0:1], dwc_v[:, 0:1],
                    lo_v[:, 0:1], ALU.mult, ALU.add)
            self.vo("ts", e_v[:, 1:2], v64[:, NQ:NQ + 1], mlr_v[:, 1:2], ALU.mult)
            self.vo("stt", lo_v[:, NQ - 1:NQ], e_v[:, 1:2], dwc_v[:, 2:3],
                    lo_v[:, NQ - 1:NQ], ALU.mult, ALU.add)
            if self.c.get("ASTOP") == "p3edge":
                return
            self.vo("ts", lo_v, lo_v, dwcb_v, ALU.add)
            if self.c.get("ASTOP") == "p3bias":
                return
            v_ = self.c.get("P3VAR", 0)
            if v_ == 1:
                self.tt(dst, dst, lo_v, ALU.add, inc=None)
                self.bump("bgo")
                self.release("vector", "bgo")
            elif v_ == 2:
                self.tt(x.gs_sb[:, 0:NQ] if NQ <= 2 * 512 else lo_v, dst, lo_v,
                        ALU.add, inc=self.inc("bgo"))
                self.wait("vector", "bgo", self.cnt("bgo"))
            else:
                self.tt(dst, dst, lo_v, ALU.add, inc=self.inc("bgo"))
                self.wait("vector", "bgo", self.cnt("bgo"))

    def _emit_av(self, b, h):
        x, c = self.x, self.c
        NQ, ICW, NIC, NBLK = c["NQ"], c["ICW"], c["NIC"], c["NBLK"]
        gbi = h * NBLK + b
        self.wait("tensor", "evd", gbi + 1)
        if b == 0:
            self.wait("tensor", "pbf", h)
        src = x.kern if b % 2 == 0 else x.kern2
        for ic in range(NIC):
            inc = None
            if ic == NIC - 1:
                inc = self.inc("pad") if b == NBLK - 1 else self.inc("pvd")
            self.mm(x.PS_B[0:65, 512 * ic:512 * ic + ICW],
                    x.vp[:, 65 * b:65 * b + 65].bitcast(F32R),
                    src[:, ICW * ic:ICW * ic + ICW].bitcast(F32R),
                    start=(b == 0), stop=(b == NBLK - 1), inc=inc)

    def _phase_b(self):
        x, c = self.x, self.c
        DM, NQ, NDC, QCW, NQC, E = c["DM"], c["NQ"], c["NDC"], c["QCW"], c["NQC"], c["E"]
        TCW = c["TCW"]
        RR = 3
        BGO = self.cnt("bgo")
        for e in ("tensor", "vector", "scalar", "gpsimd", "sync"):
            self.wait(e, "bgo", BGO)
        for ci in range(NDC):
            self.dma("gpsimd", x.gw_sb[:, E * ci:E * ci + E].bitcast(F32R),
                     x.gatewT[128 * ci:128 * ci + 128, :].bitcast(F32R), "bdw")
        GWD = self.cnt("bdw")
        slotV = lambda oc: x.VT[:][:, QCW * oc:QCW * oc + QCW]

        for tc in range(NQC):
            self.wait("tensor", "cwt", self.cnt("cwt"))
            self.wait("vector", "x2w", self.cnt("x2w"))
            for oc in range(NDC):
                self.wait("gpsimd", "bev", self.cnt("bev"))
                self.wait("gpsimd", "bdw", self.cnt("bdw"))
                for ci in range(NDC):
                    self.dma("gpsimd", x.wbuf[:, 128 * ci:128 * ci + 128].bitcast(F32R),
                             x.wprojT[128 * ci:128 * ci + 128,
                                      128 * oc:128 * oc + 128].bitcast(F32R),
                             "bdw")
                self.dma("gpsimd", x.bpj, x.bprojD[128 * oc:128 * oc + 128, :], "bdw")
                self.wait("tensor", "bdw", self.cnt("bdw"))
                slot = oc % 2
                key = ("b1", slot)
                if key in self.slotuse:
                    self.wait("tensor", "bev", self.slotuse[key])
                for ci in range(NDC):
                    self.mm(x.PS_A[:, 512 * slot:512 * slot + QCW],
                            x.wbuf[:, 128 * ci:128 * ci + 128],
                            x.AR[:, ci * NQ + QCW * tc:ci * NQ + QCW * tc + QCW],
                            start=(ci == 0), stop=(ci == NDC - 1),
                            inc=self.inc("bmm") if ci == NDC - 1 else None)
                self.wait("vector", "bmm", self.cnt("bmm"))
                self.ts(slotV(oc).bitcast(F32R),
                        x.PS_A[:, 512 * slot:512 * slot + QCW], x.bpj[:],
                        ALU.add, inc=self.inc("bev"))
                self.slotuse[("b1", slot)] = self.cnt("bev")
                self.wait("vector", "bev", self.cnt("bev"))
            if self.c.get("BSTOP") == "proj":
                return
            PREV = self.cnt("bev")
            self.wait("tensor", "bev", PREV)
            self.wait("scalar", "bev", PREV)
            for oc in range(NDC):
                kk = tc * NDC + oc
                if oc == 0:
                    self.wait("tensor", "st0", self.cnt("st0"))
                self.mm(x.PS_B[0:1, 0:QCW], x.csb["ones_c128"][:], slotV(oc),
                        start=(oc == 0), stop=(oc == NDC - 1),
                        inc=self.inc("bst") if oc == NDC - 1 else None)
                if kk >= 2:
                    self.wait("scalar", "sqf", self.cnt("sqf") - 1)
                self.act(x.sqb[:, (oc % 2) * TCW:(oc % 2) * TCW + QCW], slotV(oc),
                         AF.Square, inc=self.inc("sqd"))
                self.wait("tensor", "sqd", self.cnt("sqd"))
                self.mm(x.PS_B[0:1, 512:512 + QCW], x.csb["ones_c128"][:],
                        x.sqb[:, (oc % 2) * TCW:(oc % 2) * TCW + QCW],
                        start=(oc == 0), stop=(oc == NDC - 1), inc=self.inc("sqf"))
            if self.c.get("BSTOP") == "ln2stats":
                return
            self._ln_rows(x.PS_B, x.rows, QCW, srhs=("bst", self.cnt("bst")),
                          ssq=("sqf", self.cnt("sqf")))
            if self.c.get("BSTOP") == "ln2rows":
                return
            self.wait("gpsimd", "st0", self.cnt("st0"))
            self.wait("gpsimd", "bbc", self.cnt("bbc"))
            self.dma("gpsimd", x.rowscr[3:4, 0:QCW], x.rows[0:1, 1024:1024 + QCW], "bbc")
            self.dma("gpsimd", x.rowscr[4:5, 0:QCW], x.rows[0:1, 1536:1536 + QCW], "bbc")
            self.wait("gpsimd", "bbc", self.cnt("bbc"))
            self.dma("gpsimd", x.RBt[:, 0:QCW],
                     x.rowscr[3:4, 0:QCW].broadcast_to((128, QCW)), "bbc")
            self.dma("gpsimd", x.MBt[:, 0:QCW],
                     x.rowscr[4:5, 0:QCW].broadcast_to((128, QCW)), "bbc")
            BBC = self.cnt("bbc")
            self.wait("vector", "lnxh0", self.cnt("lnxh0"))
            for oc in range(NDC):
                kx = tc * NDC + oc
                self.wait("sync", "bev", self.cnt("bev"))
                slotx = "qmm%d" % (kx % RR)
                self.dma("sync", x.rr[kx % RR][:, 0:QCW].bitcast(F32R),
                         x.xbT[128 * oc:128 * oc + 128,
                               QCW * tc:QCW * tc + QCW].bitcast(F32R), slotx)
                self.wait("vector", slotx, self.cnt(slotx))
                self.wait("vector", "bbc", BBC)
                self.vo("tt", x.gs_sb[:, 0:QCW], slotV(oc), x.RBt[:, 0:QCW], ALU.mult)
                self.vo("tt", x.gs_sb[:, 0:QCW], x.gs_sb[:, 0:QCW], x.MBt[:, 0:QCW],
                        ALU.add)
                self.tt(slotV(oc).bitcast(F32R),
                        x.gs_sb[:, 0:QCW], x.rr[kx % RR][:, 0:QCW], ALU.add,
                        inc=self.inc("bev"))
                self.wait("vector", "bev", self.cnt("bev"))
                self.wait("gpsimd", "bev", self.cnt("bev"))
                self.dma("gpsimd", x.x2d[128 * oc:128 * oc + 128, QCW * tc:QCW * tc + QCW],
                         slotV(oc), "x2w")
                self.wait("tensor", "bev", self.cnt("bev"))
                if oc == 0:
                    self.wait("tensor", "st0", self.cnt("st0"))
                self.mm(x.PS_B[0:1, 1024:1024 + QCW], x.csb["ones_c128"][:], slotV(oc),
                        start=(oc == 0), stop=(oc == NDC - 1),
                        inc=self.inc("bst") if oc == NDC - 1 else None)
                self.wait("scalar", "bev", self.cnt("bev"))
                kk = tc * NDC + oc
                if kk >= 2:
                    self.wait("scalar", "sqf", self.cnt("sqf") - 1)
                self.act(x.sqb[:, (oc % 2) * TCW:(oc % 2) * TCW + QCW], slotV(oc),
                         AF.Square, inc=self.inc("sqd"))
                self.wait("tensor", "sqd", self.cnt("sqd"))
                self.mm(x.PS_B[0:1, 1536:1536 + QCW], x.csb["ones_c128"][:],
                        x.sqb[:, (oc % 2) * TCW:(oc % 2) * TCW + QCW],
                        start=(oc == 0), stop=(oc == NDC - 1), inc=self.inc("sqf"))
            if self.c.get("BSTOP") == "apply":
                return
            self._ln_rows(x.PS_B[:, 1024:2048], x.rows, QCW,
                          srhs=("bst", self.cnt("bst")), ssq=("sqf", self.cnt("sqf")))
            self.wait("gpsimd", "st0", self.cnt("st0"))
            self.wait("gpsimd", "bbc", self.cnt("bbc"))
            self.dma("gpsimd", x.rowscr[5:6, 0:QCW], x.rows[0:1, 1024:1024 + QCW], "bbc")
            self.dma("gpsimd", x.rowscr[6:7, 0:QCW], x.rows[0:1, 1536:1536 + QCW], "bbc")
            self.wait("gpsimd", "bbc", self.cnt("bbc"))
            self.dma("gpsimd", x.RBt[:, 0:QCW],
                     x.rowscr[5:6, 0:QCW].broadcast_to((128, QCW)), "bbc")
            self.dma("gpsimd", x.MBt[:, 0:QCW],
                     x.rowscr[6:7, 0:QCW].broadcast_to((128, QCW)), "bbc")
            BBC3 = self.cnt("bbc")
            if tc == 0:
                self.wait("tensor", "bdw", GWD)
            for oc in range(NDC):
                slot = oc % 2
                self.wait("vector", "bbc", BBC3)
                if tc * NDC + oc >= 2:
                    self.wait("vector", "lnxh%d" % slot, self.cnt("lnxh%d" % slot))
                self.vo("tt", x.gs_sb[:, slot * TCW:slot * TCW + QCW], slotV(oc),
                        x.RBt[:, 0:QCW], ALU.mult)
                self.tt(x.gs_sb[:, slot * TCW:slot * TCW + QCW],
                        x.gs_sb[:, slot * TCW:slot * TCW + QCW], x.MBt[:, 0:QCW],
                        ALU.add, inc=self.inc("bev"))
                self.wait("vector", "bev", self.cnt("bev"))
                self.wait("gpsimd", "bev", self.cnt("bev"))
                self.dma("gpsimd", x.ln3d[128 * oc:128 * oc + 128, QCW * tc:QCW * tc + QCW],
                         x.gs_sb[:, slot * TCW:slot * TCW + QCW], "lnxh%d" % slot)
                self.wait("tensor", "bev", self.cnt("bev"))
                for gt in range(QCW // 128):
                    self.mm(x.PS_A[:, 512 * gt:512 * gt + E],
                            x.gs_sb[:, slot * TCW + 128 * gt:slot * TCW + 128 * gt + 128],
                            x.gw_sb[:, E * oc:E * oc + E],
                            start=(oc == 0), stop=(oc == NDC - 1),
                            inc=self.inc("bmm")
                            if (oc == NDC - 1 and gt == QCW // 128 - 1) else None)
            if self.c.get("BSTOP") == "ln3apply":
                return
            self.wait("vector", "bmm", self.cnt("bmm"))
            for gt in range(QCW // 128):
                g_all = tc * (QCW // 128) + gt
                self._softmax_top2(x.PS_A[:, 512 * gt:512 * gt + E],
                                   x.cwtm[:, E * g_all:E * g_all + E])
            if self.c.get("BSTOP") == "softmax":
                return
        NG = NQ // 128
        self.wait("tensor", "cwt", self.cnt("cwt"))
        for e in range(E):
            if e >= 1:
                self.wait("tensor", "cwt", self.cnt("cwt"))
            for g in range(NG):
                self.mm(x.PS_B[0:1, 128 * g:128 * g + 128],
                        x.cwtm[:, E * g + e:E * g + e + 1],
                        x.csb["id128"][:], True, True,
                        inc=self.inc("bmm") if g == NG - 1 else None)
            if self.c.get("BSTOP") == "cw0t" and e == 0:
                return
            self.wait("vector", "bmm", self.cnt("bmm"))
            if e >= 1:
                self.wait("vector", "cwr", self.cnt("cwr"))
            self.cp(x.ov2[0:1, 0:NQ], x.PS_B[0:1, 0:NQ], inc=self.inc("cwt"))
            self.wait("vector", "cwt", self.cnt("cwt"))
            if self.c.get("BSTOP") == "cw0c" and e == 0:
                return
            self.wait("gpsimd", "cwt", self.cnt("cwt"))
            self.dma("gpsimd", x.cwd[e:e + 1, :], x.ov2[0:1, 0:NQ], "cwr")
            if self.c.get("BSTOP") == "cw0d" and e == 0:
                return
    
    def _softmax_top2(self, ps, cw_out):
        E = self.c["E"]
        x = self.x
        s = x.scx
        self.vo("red", s[:, 0:1], ps, ALU.max)
        self.vo("ts", s[:, 1:1 + E], ps, s[:, 0:1], ALU.subtract)
        self.wait("scalar", "vch", self.cnt("vch"))
        self.ao(s[:, 1 + E:1 + 2 * E], s[:, 1:1 + E], AF.Exp)
        self.wait("vector", "ach", self.cnt("ach"))
        self.vo("red", s[:, 1 + 2 * E:2 + 2 * E], s[:, 1 + E:1 + 2 * E], ALU.add)
        self.vo("recip", s[:, 2 + 2 * E:3 + 2 * E], s[:, 1 + 2 * E:2 + 2 * E])
        sm = s[:, 3 + 2 * E:3 + 3 * E]
        self.vo("ts", sm, s[:, 1 + E:1 + 2 * E], s[:, 2 + 2 * E:3 + 2 * E], ALU.mult)
        self.vo("red", s[:, 0:1], sm, ALU.max)
        eq = s[:, 3 + 3 * E:3 + 4 * E]
        self.vo("ts", eq, sm, s[:, 0:1], ALU.is_ge)
        msk = s[:, 3 + 4 * E:3 + 5 * E]
        self.vo("ts", msk, eq, -1e30, ALU.mult)
        self.vo("tt", msk, msk, sm, ALU.add)
        self.vo("red", s[:, 1:2], msk, ALU.max)
        ge = s[:, 3 + 5 * E:3 + 6 * E]
        self.vo("ts", ge, sm, s[:, 1:2], ALU.is_ge)
        self.tt(cw_out.bitcast(mybir.dt.float32r), sm, ge, ALU.mult,
                inc=self.inc("cwt"))
        self.wait("vector", "cwt", self.cnt("cwt"))

    def _phase_c(self):
        x, c = self.x, self.c
        DM, NQ, NDC, QCW, NQC = c["DM"], c["NQ"], c["NDC"], c["QCW"], c["NQC"]
        INTER, E, NICH, TCW = c["INTER"], c["E"], c["NICH"], c["TCW"]
        RR = 3
        CWB = x.QT2
        for eng in ("tensor", "vector", "scalar", "gpsimd", "sync"):
            self.wait(eng, "cwr", 16 * E)
        for eng in ("vector", "tensor", "sync", "gpsimd"):
            self.wait(eng, "x2w", self.cnt("x2w"))
            self.wait(eng, "lnxh0", self.cnt("lnxh0"))
            self.wait(eng, "lnxh1", self.cnt("lnxh1"))

        def wg_tile(ci, ic):
            f = (ci * NICH + ic) * 128
            return x.VT[:][:, f:f + 128].bitcast(F32R)

        VP_CAP = (max(c["NBLK"] * 65, NQ) // 128) * 128
        assert NICH * NDC * 128 <= VP_CAP + 2 * DM, "wu arena too small"

        def wu_tile(ci, ic):
            f = (ci * NICH + ic) * 128
            if f < VP_CAP:
                return x.vp[:][:, f:f + 128].bitcast(F32R)
            return x.wbuf[:, f - VP_CAP:f - VP_CAP + 128].bitcast(F32R)

        def wd_tile(half, ic):
            col = (half * NICH + ic) * 128
            return x.nKT2[:][:, col:col + 128].bitcast(F32R)

        def hp_tile(ic, w):
            return x.kern[:, 512 * ic:512 * ic + w].bitcast(F32R)

        for ei in range(-1, E):
            shared = ei < 0
            wgS = x.sgTD if shared else x.wgTD[ei]
            wuS = x.suTD if shared else x.wuTD[ei]
            wdS = x.sdTD if shared else x.wdTD[ei]
            self.wait("gpsimd", "frhs", self.cnt("frhs"))
            self.wait("gpsimd", "fw", self.cnt("fw"))
            for ci in range(NDC):
                for ic in range(NICH):
                    self.dma("gpsimd", wg_tile(ci, ic),
                             wgS[128 * ci:128 * ci + 128,
                                 128 * ic:128 * ic + 128].bitcast(F32R), "fw")
                    self.dma("gpsimd", wu_tile(ci, ic),
                             wuS[128 * ci:128 * ci + 128,
                                 128 * ic:128 * ic + 128].bitcast(F32R), "fw")
            FW = self.cnt("fw")
            if not shared:
                self.wait("gpsimd", "fhp", self.cnt("fhp"))
                self.wait("gpsimd", "l4", self.cnt("l4"))
                self.dma("gpsimd", CWB[:][:, 0:NQ],
                         x.cwd[ei:ei + 1, :].broadcast_to((128, NQ)), "l4")
            L4C = self.cnt("l4")

            self.wait("tensor", "fw", FW)
            FPD0 = self.cnt("fpd")
            for tc in range(NQC):
                for ci in range(NDC):
                    k = ((ei + 1) * NQC + tc) * NDC + ci
                    if k >= RR:
                        self.wait("sync", "frhs", k - RR + 1)
                    slot = "fmm%d" % (k % RR)
                    self.dma("sync", x.rr[k % RR][:, 0:QCW].bitcast(F32R),
                             x.ln3d[128 * ci:128 * ci + 128,
                                    QCW * tc:QCW * tc + QCW].bitcast(F32R),
                             slot)
                    self.wait("tensor", slot, self.cnt(slot))
                    if ci == 0:
                        self.wait("tensor", "fyw", self.cnt("fyw"))
                        self.wait("tensor", "fhp", self.cnt("fhp"))
                    for ic in range(NICH):
                        self.mm(x.PS_A[:, 512 * ic:512 * ic + QCW],
                                wg_tile(ci, ic), x.rr[k % RR][:, 0:QCW].bitcast(F32R),
                                start=(ci == 0), stop=(ci == NDC - 1), inc=None)
                        inc = self.inc("frhs") if ic == NICH - 1 else None
                        self.mm(x.PS_B[:, 512 * ic:512 * ic + QCW],
                                wu_tile(ci, ic), x.rr[k % RR][:, 0:QCW].bitcast(F32R),
                                start=(ci == 0), stop=(ci == NDC - 1), inc=inc)
                self.wait("scalar", "frhs", self.cnt("frhs"))
                self.wait("scalar", "l4", L4C)
                self.wait("vector", "l4", L4C)
                self.wait("vector", "fpd", self.cnt("fpd"))
                for ic in range(NICH):
                    self.wait("scalar", "vch", self.cnt("vch"))
                    self.ao(x.gs_sb[:, 0:QCW], x.PS_A[:, 512 * ic:512 * ic + QCW],
                            AF.Silu)
                    self.wait("vector", "ach", self.cnt("ach"))
                    self.vo("tt", x.gs_sb[:, TCW:TCW + QCW], x.gs_sb[:, 0:QCW],
                            x.PS_B[:, 512 * ic:512 * ic + QCW], ALU.mult)
                    if shared:
                        self.cp(hp_tile(ic, QCW), x.gs_sb[:, TCW:TCW + QCW],
                                inc=self.inc("fhp"))
                    else:
                        self.tt(hp_tile(ic, QCW), x.gs_sb[:, TCW:TCW + QCW],
                                CWB[:][:, QCW * tc:QCW * tc + QCW], ALU.mult,
                                inc=self.inc("fhp"))
                    self.wait("vector", "fhp", self.cnt("fhp"))
                FHPTC = self.cnt("fhp")
                self.wait("tensor", "fhp", FHPTC)
                for dmc in range(NDC):
                    half = dmc % 2
                    if ("wd", half) in self.slotuse:
                        self.wait("gpsimd", "fpd", self.slotuse[("wd", half)])
                    self.wait("gpsimd", "fw", self.cnt("fw"))
                    for ic in range(NICH):
                        self.dma("gpsimd", wd_tile(half, ic),
                                 wdS[128 * ic:128 * ic + 128,
                                     128 * dmc:128 * dmc + 128].bitcast(F32R),
                                 "fw")
                    FWD = self.cnt("fw")
                    self.wait("tensor", "fw", FWD)
                    yslot = dmc % 2
                    key = ("dn", yslot)
                    if key in self.slotuse:
                        self.wait("tensor", "fyw", self.slotuse[key])
                    for ic in range(NICH):
                        inc = self.inc("fpd") if ic == NICH - 1 else None
                        self.mm(x.PS_A[:, 512 * yslot:512 * yslot + QCW],
                                wd_tile(half, ic), hp_tile(ic, QCW),
                                start=(ic == 0), stop=(ic == NICH - 1), inc=inc)
                    self.slotuse[("wd", half)] = self.cnt("fpd")
                    self.wait("vector", "fpd", self.cnt("fpd"))
                    dstc = x.AR[:, dmc * NQ + QCW * tc:dmc * NQ + QCW * tc + QCW]
                    if shared:
                        self.cp(dstc, x.PS_A[:, 512 * yslot:512 * yslot + QCW],
                                inc=self.inc("fyw"))
                    else:
                        self.tt(dstc, dstc, x.PS_A[:, 512 * yslot:512 * yslot + QCW],
                                ALU.add, inc=self.inc("fyw"))
                    self.slotuse[("dn", yslot)] = self.cnt("fyw")
                    self.wait("vector", "fyw", self.cnt("fyw"))

        FYW = self.cnt("fyw")
        self.wait("tensor", "fyw", FYW)
        self.wait("scalar", "fyw", FYW)
        out_dma = []
        for tc in range(NQC):
            for oc in range(NDC):
                kk = tc * NDC + oc
                if oc == 0:
                    self.wait("tensor", "st0", self.cnt("st0"))
                self.mm(x.PS_B[0:1, 0:QCW], x.csb["ones_c128"][:],
                        x.AR[:, oc * NQ + QCW * tc:oc * NQ + QCW * tc + QCW],
                        start=(oc == 0), stop=(oc == NDC - 1),
                        inc=self.inc("bst") if oc == NDC - 1 else None)
                if kk >= 2:
                    self.wait("scalar", "sqf", self.cnt("sqf") - 1)
                self.act(x.sqb[:, (oc % 2) * TCW:(oc % 2) * TCW + QCW],
                         x.AR[:, oc * NQ + QCW * tc:oc * NQ + QCW * tc + QCW], AF.Square,
                         inc=self.inc("sqd"))
                self.wait("tensor", "sqd", self.cnt("sqd"))
                self.mm(x.PS_B[0:1, 512:512 + QCW], x.csb["ones_c128"][:],
                        x.sqb[:, (oc % 2) * TCW:(oc % 2) * TCW + QCW],
                        start=(oc == 0), stop=(oc == NDC - 1), inc=self.inc("sqf"))
            self._ln_rows(x.PS_B, x.rows, QCW, srhs=("bst", self.cnt("bst")),
                          ssq=("sqf", self.cnt("sqf")))
            self.wait("gpsimd", "st0", self.cnt("st0"))
            self.wait("gpsimd", "bbc", self.cnt("bbc"))
            self.dma("gpsimd", x.rowscr[0:1, 0:QCW], x.rows[0:1, 1024:1024 + QCW], "bbc")
            self.dma("gpsimd", x.rowscr[1:2, 0:QCW], x.rows[0:1, 1536:1536 + QCW], "bbc")
            self.wait("gpsimd", "bbc", self.cnt("bbc"))
            self.dma("gpsimd", x.RBt[:, 0:QCW],
                     x.rowscr[0:1, 0:QCW].broadcast_to((128, QCW)), "bbc")
            self.dma("gpsimd", x.MBt[:, 0:QCW],
                     x.rowscr[1:2, 0:QCW].broadcast_to((128, QCW)), "bbc")
            BBC = self.cnt("bbc")
            for oc in range(NDC):
                kx = tc * NDC + oc
                self.wait("sync", "frhs", self.cnt("frhs"))
                slotx = "fmm%d" % (kx % RR)
                self.dma("sync", x.rr[kx % RR][:, 0:QCW].bitcast(F32R),
                         x.x2d[128 * oc:128 * oc + 128,
                               QCW * tc:QCW * tc + QCW].bitcast(F32R), slotx)
                self.wait("vector", slotx, self.cnt(slotx))
                self.wait("vector", "bbc", BBC)
                if out_dma:
                    self.wait("vector", "outw", out_dma[-1])
                arc = x.AR[:, oc * NQ + QCW * tc:oc * NQ + QCW * tc + QCW]
                self.vo("tt", x.gs_sb[:, 0:QCW], arc, x.RBt[:, 0:QCW], ALU.mult)
                self.vo("tt", x.gs_sb[:, 0:QCW], x.gs_sb[:, 0:QCW], x.MBt[:, 0:QCW],
                        ALU.add)
                self.tt(x.gs_sb[:, TCW:TCW + QCW],
                        x.gs_sb[:, 0:QCW], x.rr[kx % RR][:, 0:QCW], ALU.add,
                        inc=self.inc("frhs"))
                self.wait("vector", "frhs", self.cnt("frhs"))
                self.wait("gpsimd", "frhs", self.cnt("frhs"))
                self.dma("gpsimd", x.outD[128 * oc:128 * oc + 128, QCW * tc:QCW * tc + QCW],
                         x.gs_sb[:, TCW:TCW + QCW], "outw")
                out_dma.append(self.cnt("outw"))
        self.wait("gpsimd", "outw", self.cnt("outw"))



def stage_core_inputs(cfg, core, full):
    S, DM = cfg["S"], cfg["DM"]
    b, half = core // 2, core % 2
    x = np.asarray(full["L0_attn_x"][b], dtype=np.float32)
    NQ = S // 2
    xr = x if half == 0 else np.concatenate([x[NQ:], x[:NQ]], axis=0)
    d = {
        "xbT": np.ascontiguousarray(xr.T),
        "wqkvT": np.ascontiguousarray(np.asarray(full["L0_attn_qkv_w"], np.float32).T),
        "bqkv": np.ascontiguousarray(np.asarray(full["L0_attn_qkv_b"], np.float32).reshape(-1, 1)),
        "wprojT": np.ascontiguousarray(np.asarray(full["L0_attn_proj_w"], np.float32).T),
        "bproj": np.ascontiguousarray(np.asarray(full["L0_attn_proj_b"], np.float32).reshape(-1, 1)),
        "dwcw": np.ascontiguousarray(np.asarray(full["L0_attn_dwc_w"], np.float32).reshape(DM, 3)),
        "dwcb": np.ascontiguousarray(np.asarray(full["L0_attn_dwc_b"], np.float32).reshape(-1, 1)),
        "gatewT": np.ascontiguousarray(np.asarray(full["L0_ffn_gate_w"], np.float32).T),
        "wgT": np.ascontiguousarray(np.asarray(full["L0_ffn_wg"], np.float32).transpose(0, 2, 1)),
        "wuT": np.ascontiguousarray(np.asarray(full["L0_ffn_wu"], np.float32).transpose(0, 2, 1)),
        "wdT": np.ascontiguousarray(np.asarray(full["L0_ffn_wd"], np.float32).transpose(0, 2, 1)),
        "sgT": np.ascontiguousarray(np.asarray(full["L0_ffn_sg"], np.float32).T),
        "suT": np.ascontiguousarray(np.asarray(full["L0_ffn_su"], np.float32).T),
        "sdT": np.ascontiguousarray(np.asarray(full["L0_ffn_sd"], np.float32).T),
        "masks": np.array([[1.0 if half == 1 else 0.0,
                            1.0 if half == 0 else 0.0]], dtype=np.float32),
    }
    d.update(host_consts(cfg["LN_EPS"]))
    return d


_PROG_CACHE = {}


def kernel(**inputs):
    cfg = FULL_CFG
    if "full" not in _PROG_CACHE:
        _PROG_CACHE["full"] = Builder(cfg).build()
    nc = _PROG_CACHE["full"]
    in_maps = [stage_core_inputs(cfg, c, inputs) for c in range(8)]
    res = run_bass_kernel_spmd(nc, in_maps, core_ids=list(range(8)))
    S, DM = cfg["S"], cfg["DM"]
    NQ = S // 2
    out = np.empty((4, S, DM), dtype=np.float32)
    for c in range(8):
        b, half = c // 2, c % 2
        o = np.asarray(res.results[c]["out"])
        if half == 0:
            out[b, 0:NQ, :] = o.T
        else:
            out[b, NQ:S, :] = o.T
    return out
